# version 11
# baseline (speedup 1.0000x reference)
import numpy as np
import ml_dtypes
import concourse.bass as bass
import concourse.mybir as mybir
from concourse.bass import IndirectOffsetOnAxis
from concourse.bass_utils import run_bass_kernel_spmd

F32 = mybir.dt.float32
F32R = mybir.dt.float32r
BF16 = mybir.dt.bfloat16
I32 = mybir.dt.int32
AF = mybir.ActivationFunctionType
ALU = mybir.AluOpType

B, T, DM = 2, 8192, 1024
DI, DS, DTR, DC = 2048, 16, 64, 4
EPS = 1e-5
NCORE = 8
DQ = DI // 4
NDT = DQ // 128
TQ = T // 4
E = 8
IEX = 512
NU = 10
SCALE_MOE = 2.5
TC = 1024
NTC = T // TC
TG = 512
NTG = T // TG
GROUPS = [[0, 1, 2, 3], [4, 5, 6, 7]]

_DBG = []
_PHASES = ("p1", "dt", "scan", "p3")
_P3_STOP = None

ENGINES = ("sync", "gpsimd", "scalar", "vector", "tensor")


class Em:

    NCH = 4

    def __init__(self, nc):
        self.nc = nc
        self.streams = {e: [] for e in ENGINES}
        self.sems = {}
        self.counts = {}
        for e in ENGINES:
            cm = nc.semaphore(f"S_{e}_c")
            self.sems[(e, "c")] = cm.__enter__()
            self.counts[(e, "c")] = 0
        for e in ("sync", "gpsimd", "scalar"):
            for i in range(self.NCH):
                cm = nc.semaphore(f"S_{e}_q{i}")
                self.sems[(e, f"q{i}")] = cm.__enter__()
                self.counts[(e, f"q{i}")] = 0
        self.rr = {e: 0 for e in ENGINES}
        self.waited = {e: {} for e in ENGINES}

    def ev(self, engine, kind="c"):
        return (engine, kind, self.counts[(engine, kind)])

    def qsnap(self, engine):
        return [(engine, f"q{i}", self.counts[(engine, f"q{i}")])
                for i in range(self.NCH)]

    def _wl(self, engine, kind, waits):
        wl = []
        flat = []
        for w in waits:
            if w is None:
                continue
            if isinstance(w, list):
                flat.extend(x for x in w if x is not None)
            else:
                flat.append(w)
        for w in flat:
            (pe_, pk, val) = w
            if pe_ == engine and pk == "c" and kind == "c" and engine == "tensor":
                continue
            key = (pe_, pk)
            if val <= self.waited[engine].get(key, 0):
                continue
            self.waited[engine][key] = val
            wl.append((self.sems[key], val))
        return wl

    def emit(self, engine, build, waits=(), dma=False, inc=True, inc_amt=None):
        if dma:
            ci = self.rr[engine] % self.NCH
            self.rr[engine] += 1
            kind = f"q{ci}"
            amt = 16
            prev = self.counts[(engine, kind)]
            wl = self._wl(engine, kind, waits)
            key = (engine, kind)
            if prev > self.waited[engine].get(key, 0):
                self.waited[engine][key] = prev
                wl.insert(0, (self.sems[key], prev))
            self.counts[key] = prev + amt
            tgt = prev + amt
            sem = self.sems[key]
        else:
            kind = "c"
            amt = inc_amt if inc_amt is not None else 1
            wl = self._wl(engine, kind, waits)
            if engine in ("vector", "scalar", "gpsimd"):
                key = (engine, "c")
                prev = self.counts[key]
                if prev > self.waited[engine].get(key, 0):
                    self.waited[engine][key] = prev
                    wl.insert(0, (self.sems[key], prev))
            if inc:
                self.counts[(engine, kind)] += amt
            tgt = self.counts[(engine, kind)]
            sem = self.sems[(engine, kind)]

        def run(eng, wl=wl, build=build, sem=sem, amt=amt, inc=inc):
            for (h, v) in wl:
                eng.wait_ge(h, v)
            ins = build(eng)
            if inc and ins is not None:
                ins.then_inc(sem, amt)
            return ins

        self.streams[engine].append(run)
        return (engine, kind, tgt)

    def wait_only(self, engine, waits):
        wl = self._wl(engine, "c", [w for w in waits if w is not None])
        if not wl:
            return

        def run(eng, wl=wl):
            for (h, v) in wl:
                eng.wait_ge(h, v)

        self.streams[engine].append(run)

    def barrier(self):
        evs = [(e, k, self.counts[(e, k)]) for (e, k) in self.counts]
        for e in ENGINES:
            self.wait_only(e, evs)


def _bld(fn, *a, **k):
    return lambda eng: fn(eng, *a, **k)


def _kpm(ap, p=128):
    return ap.rearrange("(kc p) m -> kc p m", p=p).transpose([1, 0, 2])


def build_program():
    nc = bass.Bass("TRN2", target_bir_lowering=False, debug=False)
    D, S = {}, {}

    def din(name, shape, dt=F32):
        D[name] = nc.dram_tensor(name, shape, dt, kind="ExternalInput").ap()

    def dout(name, shape, dt=F32):
        D[name] = nc.dram_tensor(name, shape, dt, kind="ExternalOutput").ap()

    def dint(name, shape, dt=F32):
        D[name] = nc.dram_tensor(name, shape, dt).ap()

    din("x", [T, DM])
    din("xq", [TQ, DM])
    din("w_inx", [DM, DQ])
    din("w_inz", [DM, DQ])
    din("convw", [DC, DQ])
    din("convb", [128, NDT])
    din("w_xp", [DQ, 96], BF16)
    din("w_dt", [DTR, DQ], F32R)
    din("dtb", [128, NDT])
    din("acol", [128, NDT * DS])
    din("dcol", [128, NDT])
    din("nrmw", [128, DM // 128])
    din("w_out", [DI, DM], BF16)
    din("w_gate", [DM, E], F32R)
    din("wg_all", [NU, DM, IEX], BF16)
    din("wu_all", [NU, DM, IEX], BF16)
    din("wd_all", [NU, IEX, DM], BF16)
    din("idbf", [128, 128], BF16)
    din("idf", [128, 128])
    din("ones_r", [128, 8], F32R)
    din("yidx", [128, 64], I32)
    din("consts", [128, 8])
    dout("out", [TQ, DM])

    dint("rstd_row", [1, T])
    dint("ppart", [2, 96, T // 2])
    dint("pred", [2, 96, T // 2])
    dint("dt_d", [DQ, T])
    dint("xc_d", [DQ, T], BF16)
    dint("sz_d", [DQ, T], BF16)
    dint("predb", [2, 32, T // 2], BF16)
    dint("ygat_in", [DQ, T], BF16)
    dint("ygat_all", [DI, T], BF16)
    dint("srow_d", [2, TQ])
    dint("mrow_d", [2, TQ])
    dint("combt_d", [E, TQ])

    dbg_shapes = {
        "xc": ([DQ, T], BF16), "sz": ([DQ, T], BF16), "ppart": ([96, T], F32),
        "pred": ([96, T], F32), "dt": ([DQ, T], F32), "y": ([DQ, T], BF16),
        "ygall": ([DI, T], BF16), "r1": ([DM, TQ], F32), "x1": ([DM, TQ], F32),
        "comb": ([E, TQ], F32), "sc": ([128, 16 * E], F32),
    }
    for nm in _DBG:
        sh, dt_ = dbg_shapes[nm]
        dout("dbg_" + nm, sh, dt_)

    SB_BASE = 16512
    SB_END = 229376
    cur = [SB_BASE]

    def salloc(name, shape, dt=F32, at=None):
        szb = {F32: 4, F32R: 4, BF16: 2, I32: 4}[dt]
        nbytes = int(np.prod(shape[1:])) * szb
        off = ((cur[0] if at is None else at) + 63) & ~63
        if at is None:
            cur[0] = off + nbytes
        assert off + nbytes <= SB_END, (name, off, nbytes)
        S[name] = nc.alloc_sbuf_tensor_at(name, shape, dt, offset=off)
        return off + nbytes

    for nm, sh, dt_ in [
        ("convb_t", [128, NDT], F32), ("dtb_t", [128, NDT], F32),
        ("acol_t", [128, NDT * DS], F32), ("dcol_t", [128, NDT], F32),
        ("nrmw_t", [128, DM // 128], F32), ("idbf_t", [128, 128], BF16),
        ("idf_t", [128, 128], F32), ("ones_rt", [128, 8], F32R),
        ("onesb_t", [128, 8], BF16), ("st_t", [128, NDT * DS], F32),
        ("yidx_t", [128, 64], I32),
        ("spa_t", [128, TG], F32), ("spb_t", [128, 2 * TG], F32),
        ("spc_t", [128, 2 * TG], F32), ("sq_t", [128, TG], BF16),
        ("srow_t", [1, TG], F32), ("srow2_t", [1, TG], F32),
        ("m16_t", [128, 64], F32),
        ("tiny_t", [128, 64], F32),
        ("gwt_t", [128, 8 * E], F32R), ("xstage_t", [128, 2 * DM], F32),
        ("w_dt_t", [64, DQ], F32R), ("dtraw_t", [64, 2 * TG], F32R),
        ("m1_t", [128, 16], F32), ("m2_t", [128, 16], F32),
        ("msk_t", [128, 8], F32), ("msk2_t", [128, 8], F32),
        ("lg_t", [128, 16 * E], F32), ("sc_t", [128, 16 * E], F32),
        ("cb_t", [128, 16 * E], F32), ("cbp_t", [E, TQ], F32),
    ]:
        salloc(nm, sh, dt_)
    salloc("consts_t", [128, 8], F32)
    nc.const_aps.register(F32, EPS, S["consts_t"][:, 0:1])
    nc.const_aps.register(F32, 1.0, S["consts_t"][:, 1:2])
    nc.const_aps.register(F32, -1.0, S["consts_t"][:, 2:3])
    nc.const_aps.register(F32, 1.0 / DM, S["consts_t"][:, 3:4])
    arena = (cur[0] + 63) & ~63

    a = arena
    a = salloc("wtap_t", [128, DC * 8 * DQ], BF16, at=a)
    a = salloc("w_inz_t", [128, 8 * DQ], BF16, at=a)
    a = salloc("cwbc_t", [128, DC * DQ], F32, at=a)
    a = salloc("wtmp_t", [128, DQ], F32, at=a)
    a = salloc("w_xp_t", [128, NDT * 96], BF16, at=a)
    a = salloc("xt_t", [128, 2 * 4 * DM], F32, at=a)
    a = salloc("hn_t", [128, 2 * 8 * (TG + 3)], BF16, at=a)
    a = salloc("rbc_t", [128, 2 * TG], F32, at=a)
    a = salloc("xcg_t", [128, NDT * TG], BF16, at=a)
    a = salloc("szg_t", [128, TG], BF16, at=a)
    a = salloc("rstdc_t", [128, 64], F32, at=a)
    a = salloc("rsev_t", [4, 2 * 128], F32, at=a)
    a = salloc("projs_t", [96, 2 * TG], F32, at=a)
    assert a <= SB_END


    a = arena
    a = salloc("dt_c", [128, 2 * NDT * TC], F32, at=a)
    a = salloc("xcc", [128, 2 * NDT * TC], BF16, at=a)
    a = salloc("szc", [128, 2 * NDT * TC], BF16, at=a)
    a = salloc("dtx_c", [128, NDT * TC], BF16, at=a)
    a = salloc("bbc", [128, 4 * TC], BF16, at=a)
    a = salloc("cbc", [128, 4 * TC], BF16, at=a)
    a = salloc("da_t", [128, 4 * TC], F32, at=a)
    a = salloc("dbx_t", [128, 4 * TC], BF16, at=a)
    a = salloc("h_t", [128, 6 * TC], BF16, at=a)
    a = salloc("hc_t", [128, 6 * TC], BF16, at=a)
    a = salloc("y1_t", [128, TC], F32, at=a)
    a = salloc("yg_t", [128, 2 * TC], BF16, at=a)
    assert a <= SB_END

    a = arena
    a = salloc("xqb_t", [128, 8 * TQ], BF16, at=a)
    XQ_OFF = arena
    a = salloc("r1_t", [128, 8 * TQ], F32R, at=a)
    R1_OFF = a - 8 * TQ * 4
    a3 = a
    a = salloc("w_out_t", [128, 16 * DM], BF16, at=a3)
    YR_OFF = a
    a = salloc("yr_t", [128, 2 * 16 * 512], BF16, at=a)
    assert a <= SB_END
    a = a3
    WG_OFF = a
    a = salloc("wg_t", [128, 2 * 8 * IEX], BF16, at=a)
    a = salloc("wu_t", [128, 2 * 8 * IEX], BF16, at=a)
    a = salloc("wd_t", [128, 2 * 4 * DM], BF16, at=a)
    a = salloc("sg_t", [128, 4 * 512], BF16, at=a)
    a = salloc("hm_t", [128, 2 * 4 * 512], BF16, at=a)
    a = salloc("cbbc_t", [128, 2 * 512], BF16, at=a)
    a = salloc("x2_t", [128, 8 * 128], F32, at=a)
    a = salloc("xo_t", [128, DM], F32, at=a)
    assert a <= SB_END, ("ffn arena", a)
    salloc("mbc_t", [128, TQ], F32, at=WG_OFF)
    salloc("rbc2_t", [128, TQ], F32, at=WG_OFF + TQ * 4)
    salloc("cbp_t", [E, TQ], F32, at=YR_OFF)

    x1b = S["xqb_t"]

    PB = [nc.alloc_psum_tensor(f"pb{i}", [128, 512], F32) for i in range(8)]

    em = Em(nc)
    ctx = dict(nc=nc, D=D, S=S, PB=PB, em=em)

    emit_init(ctx)
    if "p1" in _PHASES:
        emit_phase1(ctx)
    if "dt" in _PHASES:
        ctx["dt_done"] = {}
        emit_dt_init(ctx)
    if "scan" in _PHASES:
        emit_scan(ctx)
    if "p3" in _PHASES:
        emit_phase3(ctx)
    em.barrier()

    with nc.Block() as block:
        @block.sync
        def _(eng):
            for f in em.streams["sync"]:
                f(eng)

        @block.gpsimd
        def _(eng):
            for f in em.streams["gpsimd"]:
                f(eng)

        @block.scalar
        def _(eng):
            for f in em.streams["scalar"]:
                f(eng)

        @block.vector
        def _(eng):
            for f in em.streams["vector"]:
                f(eng)

        @block.tensor
        def _(eng):
            for f in em.streams["tensor"]:
                f(eng)

    return nc, D


def emit_init(ctx):
    D, S, em = ctx["D"], ctx["S"], ctx["em"]
    for nm_s, nm_d in [("convb_t", "convb"), ("dtb_t", "dtb"), ("acol_t", "acol"),
                       ("dcol_t", "dcol"), ("nrmw_t", "nrmw"), ("idbf_t", "idbf"),
                       ("idf_t", "idf"), ("ones_rt", "ones_r"), ("yidx_t", "yidx"),
                       ("consts_t", "consts")]:
        em.emit("sync", _bld(lambda e, o=S[nm_s][:], i=D[nm_d]: e.dma_start(out=o, in_=i)),
                dma=True)
    em.emit("sync", _bld(lambda e: e.dma_start(out=S["gwt_t"][:], in_=_kpm(D["w_gate"]))),
            dma=True)
    ctx["init_ev"] = em.qsnap("sync")
    em.emit("gpsimd", _bld(lambda e: e.dma_start(
        out=S["onesb_t"][:], in_=D["ones_r"].bitcast(F32))), dma=True)
    ctx["init_ev_g"] = em.qsnap("gpsimd")
    e1 = em.emit("scalar", _bld(lambda e: e.activation(
        S["acol_t"][:], S["acol_t"][:], AF.Exp, bias=0.0, scale=1.0)),
        waits=[ctx["init_ev"]])
    em.emit("vector", _bld(lambda e: e.tensor_scalar(
        S["acol_t"][:], S["acol_t"][:], -1.0, None, ALU.mult)), waits=[e1])
    ctx["acol_ev"] = em.ev("vector")


def emit_phase1(ctx):
    D, S, PB, em = ctx["D"], ctx["S"], ctx["PB"], ctx["em"]
    hnW = TG + 3

    ev_z = em.emit("gpsimd", _bld(lambda e: e.dma_start(
        out=S["w_inz_t"][:], in_=_kpm(D["w_inz"]))),
        dma=True)
    for kc in range(8):
        em.emit("vector", _bld(lambda e, kc=kc: e.tensor_scalar(
            S["w_inz_t"][:, kc * DQ:(kc + 1) * DQ], S["w_inz_t"][:, kc * DQ:(kc + 1) * DQ],
            S["nrmw_t"][:, kc:kc + 1], None, ALU.mult)), waits=[ev_z, ctx["init_ev"]])
    ctx["winz_ev"] = em.ev("vector")

    ev_cw = em.emit("gpsimd", _bld(lambda e: e.dma_start(
        out=S["cwbc_t"][:],
        in_=D["convw"].rearrange("k m -> (k m)").unsqueeze(0).broadcast_to([128, DC * DQ]))),
        dma=True)
    for kc in range(8):
        lv = em.emit("sync", _bld(lambda e, kc=kc: e.dma_start(
            out=S["wtmp_t"][:], in_=D["w_inx"][kc * 128:(kc + 1) * 128, :])),
            waits=[em.ev("vector")], dma=True)
        em.emit("vector", _bld(lambda e, kc=kc: e.tensor_scalar(
            S["wtmp_t"][:], S["wtmp_t"][:], S["nrmw_t"][:, kc:kc + 1], None, ALU.mult)),
            waits=[lv, ctx["init_ev"]])
        for k in range(DC):
            em.emit("vector", _bld(lambda e, kc=kc, k=k: e.tensor_tensor(
                S["wtap_t"][:, (k * 8 + kc) * DQ:(k * 8 + kc + 1) * DQ],
                S["wtmp_t"][:], S["cwbc_t"][:, k * DQ:(k + 1) * DQ], ALU.mult)),
                waits=[ev_cw])
    ctx["wtap_ev"] = em.ev("vector")
    ev_xp = em.emit("sync", _bld(lambda e: e.dma_start(
        out=S["w_xp_t"][:], in_=_kpm(D["w_xp"]))),
        dma=True)

    hn = [S["hn_t"][:, 0:8 * hnW].rearrange("p (kc w) -> p kc w", kc=8),
          S["hn_t"][:, 8 * hnW:16 * hnW].rearrange("p (kc w) -> p kc w", kc=8)]
    for b_ in range(2):
        for kc in range(8):
            em.emit("vector", _bld(lambda e, b_=b_, kc=kc: e.tensor_copy(
                S["hn_t"][:, (b_ * 8 + kc) * hnW:(b_ * 8 + kc) * hnW + 3],
                S["consts_t"][:, 4:5].broadcast_to([128, 3]))),
                waits=[ctx["init_ev"]])
    hz_ev = em.ev("vector")

    xld = {}
    rbc_ev = {}
    hn_ev = {}
    tail_ev = {}
    pe_hn = {}
    xc_ev = {}
    sz_st = {}
    xc_st = {}
    pp_st = {}
    CONV_B = (0, 1)
    Z_B = (2, 3)
    XT_B = (4, 5)
    PP_B = 6

    rst_ev = {}
    rrow_st = {}
    ca_ev = {}

    def emit_xld(g):
        gb = g % 2
        for tt in range(4):
            w = [xld.get((g - 2, tt)), hn_ev.get(g - 2)]
            xld[(g, tt)] = em.emit("sync", _bld(
                lambda e, g=g, tt=tt, gb=gb: e.dma_start(
                    out=S["xt_t"][:, (gb * 4 + tt) * DM:(gb * 4 + tt + 1) * DM],
                    in_=D["x"][g * TG + tt * 128:g * TG + (tt + 1) * 128, :])),
                waits=w, dma=True)

    def emit_chainA(g):
        gb = g % 2
        for tt in range(4):
            buf = S["xt_t"][:, (gb * 4 + tt) * DM:(gb * 4 + tt + 1) * DM]
            em.emit("scalar", _bld(
                lambda e, buf=buf, gb=gb, tt=tt: e.activation(
                    S["sq_t"][:], buf[:, 0:TG], AF.Square, bias=0.0, scale=1.0,
                    accum_out=S["rstdc_t"][:, gb * 8 + tt:gb * 8 + tt + 1])),
                waits=[xld[(g, tt)], rst_ev.get(g - 2)])
            em.emit("scalar", _bld(
                lambda e, buf=buf: e.activation(
                    S["sq_t"][:], buf[:, TG:DM], AF.Square, bias=0.0, scale=1.0,
                    accum_out=S["tiny_t"][:, 0:1])),
                waits=[em.ev("vector")])
            em.emit("vector", _bld(
                lambda e, gb=gb, tt=tt: e.tensor_tensor(
                    S["rstdc_t"][:, gb * 8 + tt:gb * 8 + tt + 1],
                    S["rstdc_t"][:, gb * 8 + tt:gb * 8 + tt + 1],
                    S["tiny_t"][:, 0:1], ALU.add)), waits=[em.ev("scalar")])
        e1 = em.emit("scalar", _bld(lambda e, gb=gb: e.activation(
            S["rstdc_t"][:, gb * 8:gb * 8 + 4], S["rstdc_t"][:, gb * 8:gb * 8 + 4],
            AF.Sqrt, bias=EPS, scale=1.0 / DM)), waits=[em.ev("vector")])
        ca_ev[g] = em.emit("vector", _bld(lambda e, gb=gb: e.reciprocal(
            S["rstdc_t"][:, gb * 8:gb * 8 + 4], S["rstdc_t"][:, gb * 8:gb * 8 + 4])),
            waits=[e1])

    def emit_rfin(g):
        gb = g % 2
        e3 = em.emit("tensor", _bld(lambda e, gb=gb: e.transpose(
            PB[7][0:4, 0:128], S["rstdc_t"][:, gb * 8:gb * 8 + 4], S["idf_t"][:])),
            waits=[ca_ev[g], ctx["init_ev"], rst_ev.get(g - 1)])
        rst_ev[g] = em.emit("scalar", _bld(lambda e, gb=gb: e.activation(
            S["rsev_t"][0:4, gb * 128:(gb + 1) * 128], PB[7][0:4, 0:128],
            AF.Copy, bias=0.0, scale=1.0)), waits=[e3])
        rrow_st[g] = em.emit("gpsimd", _bld(lambda e, g=g, gb=gb: e.dma_start(
            out=D["rstd_row"].rearrange("o (g2 w) -> (o g2) w", w=128)[4 * g:4 * (g + 1), :],
            in_=S["rsev_t"][0:4, gb * 128:(gb + 1) * 128])),
            waits=[rst_ev[g]], dma=True)
        rbc_ev[g] = em.emit("gpsimd", _bld(
            lambda e, g=g, gb=gb: e.dma_start(
                out=S["rbc_t"][:, gb * TG:(gb + 1) * TG],
                in_=D["rstd_row"][0:1, g * TG:(g + 1) * TG].broadcast_to([128, TG]))),
            waits=[rrow_st[g], hn_ev.get(g - 2)], dma=True)

    emit_xld(0)
    emit_chainA(0)
    emit_rfin(0)
    for g in range(NTG):
        gb = g % 2
        if g + 1 < NTG:
            emit_xld(g + 1)
            emit_chainA(g + 1)

        for kc in range(8):
            xb = XT_B[kc % 2]
            tevs = []
            for tt in range(4):
                w = [xld[(g, tt)]]
                if kc == 0 and tt == 0:
                    w += [pe_hn.get(g - 2), tail_ev.get(g - 1)]
                w.append(hn_ev.get((g, kc - 2)))
                tevs.append(em.emit("tensor", _bld(
                    lambda e, kc=kc, tt=tt, gb=gb, xb=xb: e.transpose(
                        PB[xb][:, tt * 128:(tt + 1) * 128],
                        S["xt_t"][:, (gb * 4 + tt) * DM + kc * 128:(gb * 4 + tt) * DM + (kc + 1) * 128],
                        S["idf_t"][:])), waits=w))
            hn_ev[(g, kc)] = em.emit("vector", _bld(
                lambda e, kc=kc, gb=gb, xb=xb: e.tensor_tensor(
                    hn[gb][:, kc, 3:3 + TG], PB[xb][:, 0:TG],
                    S["rbc_t"][:, gb * TG:(gb + 1) * TG], ALU.mult)),
                waits=[tevs[-1], rbc_ev[g], hz_ev])
        hn_ev[g] = hn_ev[(g, 7)]
        tail_ev[g] = em.emit("vector", _bld(
            lambda e, gb=gb: e.tensor_copy(hn[1 - gb][:, :, 0:3], hn[gb][:, :, TG:TG + 3])),
            waits=[hn_ev[g], pe_hn.get(g - 1)])

        for dti in range(NDT):
            bank = CONV_B[dti % 2]
            w0 = [hn_ev[g], ctx["wtap_ev"], tail_ev.get(g - 1) if g >= 1 else None,
                  xc_ev.get((g, dti - 2)), xc_ev.get((g - 1, dti + 2))]
            first = True
            for k in range(DC):
                for kc in range(8):
                    last = (k == DC - 1 and kc == 7)
                    em.emit("tensor", _bld(
                        lambda e, k=k, kc=kc, dti=dti, gb=gb, bank=bank, first=first, last=last: e.matmul(
                            PB[bank][:, 0:TG],
                            S["wtap_t"][:, (k * 8 + kc) * DQ + dti * 128:(k * 8 + kc) * DQ + (dti + 1) * 128],
                            hn[gb][:, kc, k:k + TG], start=first, stop=last)),
                        waits=w0 if first else [], inc=last)
                    first = False
            mm = em.ev("tensor")
            xc_ev[(g, dti)] = em.emit("scalar", _bld(
                lambda e, dti=dti, bank=bank: e.activation(
                    S["xcg_t"][:, dti * TG:(dti + 1) * TG], PB[bank][:, 0:TG],
                    AF.Silu, bias=S["convb_t"][:, dti:dti + 1], scale=1.0)),
                waits=[mm, xc_st.get((g - 1, dti))])
            xc_st[(g, dti)] = em.emit("gpsimd", _bld(
                lambda e, g=g, dti=dti: e.dma_start(
                    out=D["xc_d"][dti * 128:(dti + 1) * 128, g * TG:(g + 1) * TG],
                    in_=S["xcg_t"][:, dti * TG:(dti + 1) * TG])),
                waits=[xc_ev[(g, dti)]], dma=True)
            ctx.setdefault("xc_done", {})[(g, dti)] = xc_st[(g, dti)]

        if g + 1 < NTG:
            emit_rfin(g + 1)

        for dti in range(NDT):
            bank = Z_B[dti % 2]
            w0 = [hn_ev[g], ctx["winz_ev"],
                  em.ev("scalar") if dti >= 2 else None]
            for kc in range(8):
                em.emit("tensor", _bld(
                    lambda e, kc=kc, dti=dti, gb=gb, bank=bank: e.matmul(
                        PB[bank][:, 0:TG],
                        S["w_inz_t"][:, kc * DQ + dti * 128:kc * DQ + (dti + 1) * 128],
                        hn[gb][:, kc, 3:3 + TG], start=(kc == 0), stop=(kc == 7))),
                    waits=w0 if kc == 0 else [], inc=(kc == 7))
            mm = em.ev("tensor")
            sev = em.emit("scalar", _bld(
                lambda e, bank=bank: e.activation(
                    S["szg_t"][:], PB[bank][:, 0:TG], AF.Silu, bias=0.0, scale=1.0)),
                waits=[mm, sz_st.get((g, dti - 1)), sz_st.get((g - 1, NDT - 1))])
            sz_st[(g, dti)] = em.emit("gpsimd", _bld(
                lambda e, g=g, dti=dti: e.dma_start(
                    out=D["sz_d"][dti * 128:(dti + 1) * 128, g * TG:(g + 1) * TG],
                    in_=S["szg_t"][:])), waits=[sev], dma=True)
            ctx.setdefault("sz_done", {})[(g, dti)] = sz_st[(g, dti)]
        pe_hn[g] = em.ev("tensor")

        w0 = [xc_ev[(g, d2)] for d2 in range(NDT)] + [ev_xp, pp_st.get(g - 2)]
        for dti in range(NDT):
            em.emit("tensor", _bld(
                lambda e, dti=dti: e.matmul(
                    PB[PP_B][0:96, 0:TG], S["w_xp_t"][:, dti * 96:(dti + 1) * 96],
                    S["xcg_t"][:, dti * TG:(dti + 1) * TG],
                    start=(dti == 0), stop=(dti == NDT - 1))),
                waits=w0 if dti == 0 else [], inc=(dti == NDT - 1))
        mm = em.ev("tensor")
        pev = em.emit("scalar", _bld(
            lambda e, gb=gb: e.activation(
                S["projs_t"][0:96, gb * TG:(gb + 1) * TG], PB[PP_B][0:96, 0:TG],
                AF.Copy, bias=0.0, scale=1.0)), waits=[mm, pp_st.get(g - 2)])
        pp_st[g] = em.emit("gpsimd", _bld(
            lambda e, g=g, gb=gb: e.dma_start(
                out=D["ppart"][g // 8, 0:96, (g % 8) * TG:(g % 8 + 1) * TG],
                in_=S["projs_t"][0:96, gb * TG:(gb + 1) * TG])),
            waits=[pev], dma=True)

    if _DBG:
        em.barrier()
    if "xc" in _DBG:
        em.emit("gpsimd", _bld(lambda e: e.dma_start(out=D["dbg_xc"], in_=D["xc_d"])), dma=True)
    if "sz" in _DBG:
        em.emit("gpsimd", _bld(lambda e: e.dma_start(out=D["dbg_sz"], in_=D["sz_d"])), dma=True)
    if "ppart" in _DBG:
        em.emit("gpsimd", _bld(lambda e: e.dma_start(out=D["dbg_ppart"], in_=D["ppart"])), dma=True)
    arh = []
    pbh = []
    for h in range(2):
        ev = em.emit("gpsimd", _bld(lambda e, h=h: e.collective_compute(
            "AllReduce", ALU.add, replica_groups=GROUPS,
            ins=[D["ppart"][h].opt()], outs=[D["pred"][h].opt()])),
            waits=[pp_st[g] for g in range(h * 8, h * 8 + 8)], inc_amt=1)
        arh.append(ev)
        pbh.append(em.emit("gpsimd", _bld(lambda e, h=h: e.dma_start(
            out=D["predb"][h], in_=D["pred"][h, 64:96, :])), waits=[ev], dma=True))
    ctx["arh"] = arh
    ctx["pbh"] = pbh
    ctx["ar_ev"] = arh[1]
    if "pred" in _DBG:
        em.barrier()
        em.emit("gpsimd", _bld(lambda e: e.dma_start(out=D["dbg_pred"].rearrange("r (h w) -> h r w", h=2), in_=D["pred"])), dma=True)
        em.barrier()


def emit_dt_init(ctx):
    D, S, em = ctx["D"], ctx["S"], ctx["em"]
    ctx["dt_w_ev"] = em.emit("sync", _bld(lambda e: e.dma_start(
        out=S["w_dt_t"][0:64, :], in_=D["w_dt"])), waits=[ctx["init_ev"]], dma=True)
    ctx["dt_st"] = {}
    ctx["dt_e5"] = {}
    ctx["dt_ld"] = {}
    ctx["dt_ti"] = [0]
    ctx["dtb_last"] = {}


def emit_dt_group(ctx, g, extra=()):
    D, S, PB, em = ctx["D"], ctx["S"], ctx["PB"], ctx["em"]
    st_ev, e5s, ld = ctx["dt_st"], ctx["dt_e5"], ctx["dt_ld"]
    gb = g % 2
    ld[g] = em.emit("gpsimd", _bld(lambda e, g=g, gb=gb: e.dma_start(
        out=S["dtraw_t"][0:64, gb * TG:(gb + 1) * TG],
        in_=D["pred"][g // 8, 0:64, (g % 8) * TG:(g % 8 + 1) * TG].bitcast(F32R))),
        waits=[st_ev.get((g - 2, NDT - 1)), ctx["arh"][g // 8]], dma=True)
    for dti in range(NDT):
        ti = ctx["dt_ti"][0]
        bank = 2 + ti % 2
        par = ti % 2
        sp = slice(par * TG, (par + 1) * TG)
        em.emit("tensor", _bld(lambda e, dti=dti, gb=gb, bank=bank: e.matmul(
            PB[bank][:, 0:TG], S["w_dt_t"][0:64, dti * 128:(dti + 1) * 128],
            S["dtraw_t"][0:64, gb * TG:(gb + 1) * TG], start=True, stop=True)),
            waits=[ld[g], ctx["dt_w_ev"], st_ev.get((g, dti - 2)), st_ev.get((g - 1, dti + 2)),
                   e5s.get(ti - 2)] + list(extra))
        mm = em.ev("tensor")
        e1 = em.emit("scalar", _bld(lambda e, bank=bank, dti=dti, sp=sp: e.activation(
            S["spb_t"][:, sp], PB[bank][:, 0:TG], AF.Abs,
            bias=S["dtb_t"][:, dti:dti + 1], scale=1.0)),
            waits=[mm, st_ev.get((g, dti - 2)), st_ev.get((g - 1, dti + 2)),
                   e5s.get(ti - 2)])
        e2 = em.emit("scalar", _bld(lambda e, sp=sp: e.activation(
            S["spc_t"][:, sp], S["spb_t"][:, sp], AF.Exp, bias=0.0, scale=-1.0)))
        e3 = em.emit("scalar", _bld(lambda e, sp=sp: e.activation(
            S["spc_t"][:, sp], S["spc_t"][:, sp], AF.Ln, bias=1.0, scale=1.0)))
        e4 = em.emit("scalar", _bld(lambda e, bank=bank, dti=dti: e.activation(
            S["spa_t"][:], PB[bank][:, 0:TG], AF.Relu,
            bias=S["dtb_t"][:, dti:dti + 1], scale=1.0)),
            waits=[e5s.get(ti - 1)])
        e5 = em.emit("vector", _bld(lambda e, sp=sp: e.tensor_tensor(
            S["spb_t"][:, sp], S["spa_t"][:], S["spc_t"][:, sp], ALU.add)),
            waits=[e4])
        e5s[ti] = e5
        st_ev[(g, dti)] = em.emit("gpsimd", _bld(lambda e, g=g, dti=dti, sp=sp: e.dma_start(
            out=D["dt_d"][dti * 128:(dti + 1) * 128, g * TG:(g + 1) * TG],
            in_=S["spb_t"][:, sp])), waits=[e5], dma=True)
        ctx["dt_done"][(g, dti)] = st_ev[(g, dti)]
        ctx["dt_ti"][0] = ti + 1


def emit_scan(ctx):
    D, S, PB, em = ctx["D"], ctx["S"], ctx["PB"], ctx["em"]
    NBC = 4
    NBUF = 4
    NBH = 6
    ld_dt, ld_xc, ld_sz = {}, {}, {}
    bb_ev, cb_ev = {}, {}
    dtx_ev, da_ev, dbx_ev, scan_ev, hc_ev, pe_ev, st_cp = {}, {}, {}, {}, {}, {}, {}
    y_done, yg_dma = {}, {}

    if "dt" in _PHASES:
        for g0 in range(4):
            emit_dt_group(ctx, g0,
                          extra=([ctx["sz_done"][(NTG - 1, NDT - 1)]] if g0 == 0 else []))
        ctx["dtb_last"][0] = ctx["dt_e5"][ctx["dt_ti"][0] - 1]
    for tc in range(NTC):
        cbuf = tc % 2
        if "dt" in _PHASES and tc < NTC - 2:
            emit_dt_group(ctx, 2 * tc + 4, extra=[y_done.get((tc - 1, 1))])
            emit_dt_group(ctx, 2 * tc + 5)
            ctx["dtb_last"][tc] = ctx["dt_e5"][ctx["dt_ti"][0] - 1]
        for dti in range(NDT):
            w = [y_done.get((tc - 2, dti))]
            dtw = [ctx["dt_done"][(g2, dti)] for g2 in range(tc * TC // TG, (tc + 1) * TC // TG)]
            ld_dt[(tc, dti)] = em.emit("sync", _bld(
                lambda e, tc=tc, dti=dti, cbuf=cbuf: e.dma_start(
                    out=S["dt_c"][:, (cbuf * NDT + dti) * TC:(cbuf * NDT + dti + 1) * TC],
                    in_=D["dt_d"][dti * 128:(dti + 1) * 128, tc * TC:(tc + 1) * TC])),
                waits=w + dtw + [scan_ev.get((tc - 2, DS - 1, dti))], dma=True)
            xcw = [ctx["xc_done"][(g2, dti)] for g2 in range(tc * TC // TG, (tc + 1) * TC // TG)]
            szw = [ctx["sz_done"][(g2, dti)] for g2 in range(tc * TC // TG, (tc + 1) * TC // TG)]
            ld_xc[(tc, dti)] = em.emit("sync", _bld(
                lambda e, tc=tc, dti=dti, cbuf=cbuf: e.dma_start(
                    out=S["xcc"][:, (cbuf * NDT + dti) * TC:(cbuf * NDT + dti + 1) * TC],
                    in_=D["xc_d"][dti * 128:(dti + 1) * 128, tc * TC:(tc + 1) * TC])),
                waits=w + xcw + [ctx["ar_ev"]], dma=True)
            ld_sz[(tc, dti)] = em.emit("sync", _bld(
                lambda e, tc=tc, dti=dti, cbuf=cbuf: e.dma_start(
                    out=S["szc"][:, (cbuf * NDT + dti) * TC:(cbuf * NDT + dti + 1) * TC],
                    in_=D["sz_d"][dti * 128:(dti + 1) * 128, tc * TC:(tc + 1) * TC])),
                waits=w + szw + [ctx["ar_ev"]], dma=True)
        for dti in range(NDT):
            dtx_ev[(tc, dti)] = em.emit("vector", _bld(
                lambda e, dti=dti, cbuf=cbuf: e.tensor_tensor(
                    S["dtx_c"][:, dti * TC:(dti + 1) * TC],
                    S["dt_c"][:, (cbuf * NDT + dti) * TC:(cbuf * NDT + dti + 1) * TC],
                    S["xcc"][:, (cbuf * NDT + dti) * TC:(cbuf * NDT + dti + 1) * TC],
                    ALU.mult)),
                waits=[ld_dt[(tc, dti)], ld_xc[(tc, dti)],
                       scan_ev.get((tc - 1, DS - 1, dti))])

        for s in range(DS):
            slot = s % NBC
            bb_ev[(tc, s)] = em.emit("scalar", _bld(
                lambda e, tc=tc, s=s, slot=slot: e.dma_start(
                    out=S["bbc"][:, slot * TC:(slot + 1) * TC],
                    in_=D["predb"][tc // 4, s:s + 1, (tc % 4) * TC:(tc % 4 + 1) * TC]
                    .broadcast_to([128, TC]))),
                waits=[dbx_ev.get((tc, s - NBC, NDT - 1)),
                       dbx_ev.get((tc - 1, s - NBC + DS, NDT - 1)), ctx["pbh"][tc // 4]], dma=True)
            cb_ev[(tc, s)] = em.emit("scalar", _bld(
                lambda e, tc=tc, s=s, slot=slot: e.dma_start(
                    out=S["cbc"][:, slot * TC:(slot + 1) * TC],
                    in_=D["predb"][tc // 4, 16 + s:17 + s, (tc % 4) * TC:(tc % 4 + 1) * TC]
                    .broadcast_to([128, TC]))),
                waits=[hc_ev.get((tc, s - NBC, NDT - 1)),
                       hc_ev.get((tc - 1, s - NBC + DS, NDT - 1)), ctx["pbh"][tc // 4]], dma=True)

            for dti in range(NDT):
                i = s * NDT + dti
                ib = i % NBUF
                pi = ((i - NBUF) // NDT, (i - NBUF) % NDT) if i >= NBUF else None
                da_ev[(tc, s, dti)] = em.emit("scalar", _bld(
                    lambda e, s=s, dti=dti, ib=ib, cbuf=cbuf: e.activation(
                        S["da_t"][:, ib * TC:(ib + 1) * TC],
                        S["dt_c"][:, (cbuf * NDT + dti) * TC:(cbuf * NDT + dti + 1) * TC],
                        AF.Exp, bias=0.0,
                        scale=S["acol_t"][:, dti * DS + s:dti * DS + s + 1])),
                    waits=[ld_dt[(tc, dti)], ctx["acol_ev"],
                           scan_ev.get((tc,) + pi) if pi else scan_ev.get((tc - 1, DS - 1, (i - NBUF) % NDT)),
                           ] + ([ctx["ar_ev"]] if tc == 0 and s == 0 else []))
                dbx_ev[(tc, s, dti)] = em.emit("vector", _bld(
                    lambda e, s=s, dti=dti, ib=ib, slot=slot: e.tensor_tensor(
                        S["dbx_t"][:, ib * TC:(ib + 1) * TC],
                        S["dtx_c"][:, dti * TC:(dti + 1) * TC],
                        S["bbc"][:, slot * TC:(slot + 1) * TC], ALU.mult)),
                    waits=[dtx_ev[(tc, dti)], bb_ev[(tc, s)]])

                gi = tc * 64 + i
                ibh = gi % NBH
                p6 = ((gi - NBH) // 64, ((gi - NBH) % 64) // NDT,
                      (gi - NBH) % NDT) if gi >= NBH else None

                def mkscan(e, tc=tc, dti=dti, s=s, ib=ib, ibh=ibh):
                    init = 0.0 if tc == 0 else S["st_t"][:, dti * DS + s:dti * DS + s + 1]
                    return e.tensor_tensor_scan(
                        S["h_t"][:, ibh * TC:(ibh + 1) * TC],
                        S["da_t"][:, ib * TC:(ib + 1) * TC],
                        S["dbx_t"][:, ib * TC:(ib + 1) * TC], init, ALU.mult, ALU.add)
                scan_ev[(tc, s, dti)] = em.emit("vector", mkscan,
                    waits=[da_ev[(tc, s, dti)], st_cp.get((tc - 1, s, dti)),
                           st_cp.get(p6), hc_ev.get(p6)])
                if tc < NTC - 1:
                    st_cp[(tc, s, dti)] = em.emit("gpsimd", _bld(
                        lambda e, s=s, dti=dti, ibh=ibh: e.tensor_copy(
                            S["st_t"][:, dti * DS + s:dti * DS + s + 1],
                            S["h_t"][:, ibh * TC + TC - 1:ibh * TC + TC])),
                        waits=[scan_ev[(tc, s, dti)]])
                hc_ev[(tc, s, dti)] = em.emit("gpsimd", _bld(
                    lambda e, s=s, dti=dti, ibh=ibh, slot=slot: e.tensor_tensor(
                        S["hc_t"][:, ibh * TC:(ibh + 1) * TC],
                        S["h_t"][:, ibh * TC:(ibh + 1) * TC],
                        S["cbc"][:, slot * TC:(slot + 1) * TC], ALU.mult)),
                    waits=[scan_ev[(tc, s, dti)], cb_ev[(tc, s)], pe_ev.get(p6)])
                wpe = [hc_ev[(tc, s, dti)], ctx["init_ev_g"], ctx["init_ev"]]
                if s == 0:
                    wpe.append(y_done.get((tc - 1, dti)))
                    if dti == 1:
                        wpe.append(ctx["dtb_last"].get(tc))
                for j in range(2):
                    em.emit("tensor", _bld(
                        lambda e, dti=dti, ibh=ibh, j=j, s=s: e.matmul(
                            PB[2 * dti + j][:, 0:512], S["idbf_t"][:],
                            S["hc_t"][:, ibh * TC + j * 512:ibh * TC + (j + 1) * 512],
                            start=(s == 0), stop=(s == DS - 1))),
                        waits=wpe if j == 0 else [], inc=(j == 1))
                pe_ev[(tc, s, dti)] = em.ev("tensor")

        for dti in range(NDT):
            w = [pe_ev[(tc, DS - 1, dti)], ld_xc[(tc, dti)], ctx["init_ev"]]
            for j in range(2):
                em.emit("vector", _bld(
                    lambda e, dti=dti, cbuf=cbuf, j=j: e.scalar_tensor_tensor(
                        S["y1_t"][:, j * 512:(j + 1) * 512],
                        S["xcc"][:, (cbuf * NDT + dti) * TC + j * 512:(cbuf * NDT + dti) * TC + (j + 1) * 512],
                        S["dcol_t"][:, dti:dti + 1], PB[2 * dti + j][:, 0:512],
                        ALU.mult, ALU.add)), waits=w if j == 0 else [])
            gi = tc * NDT + dti
            ygs = gi % 2
            pgi = ((gi - 2) // NDT, (gi - 2) % NDT) if gi >= 2 else None
            em.emit("vector", _bld(
                lambda e, dti=dti, cbuf=cbuf, ygs=ygs: e.tensor_tensor(
                    S["yg_t"][:, ygs * TC:(ygs + 1) * TC], S["y1_t"][:],
                    S["szc"][:, (cbuf * NDT + dti) * TC:(cbuf * NDT + dti + 1) * TC],
                    ALU.mult)),
                waits=[ld_sz[(tc, dti)], yg_dma.get(pgi)])
            y_done[(tc, dti)] = em.ev("vector")
            yg_dma[(tc, dti)] = em.emit("sync", _bld(
                lambda e, tc=tc, dti=dti, ygs=ygs: e.dma_start(
                    out=D["ygat_in"][dti * 128:(dti + 1) * 128, tc * TC:(tc + 1) * TC],
                    in_=S["yg_t"][:, ygs * TC:(ygs + 1) * TC])),
                waits=[y_done[(tc, dti)]], dma=True)

    ctx["scan_done"] = [em.ev("vector"), em.ev("scalar"), em.ev("tensor")] + \
        [yg_dma[(NTC - 1, dti)] for dti in range(NDT)]
    if "y" in _DBG:
        em.barrier()
        em.emit("gpsimd", _bld(lambda e: e.dma_start(out=D["dbg_y"], in_=D["ygat_in"])), dma=True)
        em.barrier()
    ctx["ag_ev"] = em.emit("gpsimd", _bld(lambda e: e.collective_compute(
        "AllGather", ALU.bypass, replica_groups=GROUPS,
        ins=[D["ygat_in"].opt()], outs=[D["ygat_all"].opt()])),
        waits=[yg_dma[(tc, dti)] for tc in range(NTC) for dti in range(NDT)], inc_amt=1)
    if "ygall" in _DBG:
        em.barrier()
        em.emit("gpsimd", _bld(lambda e: e.dma_start(out=D["dbg_ygall"], in_=D["ygat_all"])), dma=True)
        em.barrier()


def emit_phase3(ctx):
    D, S, PB, em = ctx["D"], ctx["S"], ctx["PB"], ctx["em"]
    NTT = TQ // 128

    xq_ld = {}
    xq_ev = {}
    for tt in range(NTT):
        bi = tt % 2
        xq_ld[tt] = em.emit("sync", _bld(
            lambda e, tt=tt, bi=bi: e.dma_start(
                out=S["xstage_t"][:, bi * DM:(bi + 1) * DM],
                in_=D["xq"][tt * 128:(tt + 1) * 128, :])),
            waits=[xq_ev.get(tt - 2)], dma=True)
        for kc in range(8):
            xb = 6 + (kc % 2)
            evt = em.emit("tensor", _bld(
                lambda e, tt=tt, kc=kc, bi=bi, xb=xb: e.transpose(
                    PB[xb][:, 0:128],
                    S["xstage_t"][:, bi * DM + kc * 128:bi * DM + (kc + 1) * 128],
                    S["idf_t"][:])),
                waits=[xq_ld[tt], xq_ev.get((tt, kc - 2)), xq_ev.get((tt - 1, kc + 6)), ctx["init_ev"]]
                + (ctx["scan_done"] if tt == 0 and kc < 2 else []))
            xq_ev[(tt, kc)] = em.emit("scalar", _bld(
                lambda e, tt=tt, kc=kc, xb=xb: e.activation(
                    S["xqb_t"][:, kc * TQ + tt * 128:kc * TQ + (tt + 1) * 128],
                    PB[xb][:, 0:128],
                    AF.Copy, bias=0.0, scale=1.0)),
                waits=[evt] + (ctx["scan_done"] if tt == 0 and kc == 0 else []))
        xq_ev[tt] = xq_ev[(tt, 7)]
    xq_all = em.ev("scalar")
    if _P3_STOP == "xq":
        return
    wo_ld = em.emit("sync", _bld(lambda e: e.dma_start(
        out=S["w_out_t"][:], in_=_kpm(D["w_out"]))),
        waits=ctx["scan_done"], dma=True)

    yld = {}
    r1_ev = {}
    sq_ev = {}
    st_mm = {}
    srow_ev = {}
    OP_B = (0, 1)
    SUM_B, SQ_B = 2, 3
    for nch in range(4):
        nb = nch % 2
        for kc in range(16):
            yld[(nch, kc)] = em.emit("gpsimd", _bld(
                lambda e, nch=nch, kc=kc, nb=nb: e.indirect_dma_start(
                    out=S["yr_t"][:, (nb * 16 + kc) * 512:(nb * 16 + kc + 1) * 512],
                    out_offset=None,
                    in_=D["ygat_all"].rearrange("d (blk w) -> (d blk) w", w=512),
                    in_offset=IndirectOffsetOnAxis(
                        S["yidx_t"][:, kc * 4 + nch:kc * 4 + nch + 1], axis=0))),
                waits=[ctx["init_ev"], ctx["ag_ev"], r1_ev.get((nch - 2, 7))], dma=True)
        for mt in range(8):
            bank = OP_B[mt % 2]
            w0 = [yld[(nch, kc)] for kc in range(16)]
            w0 += [wo_ld, r1_ev.get((nch, mt - 2)), r1_ev.get((nch - 1, mt + 6))]
            for kc in range(16):
                em.emit("tensor", _bld(
                    lambda e, mt=mt, kc=kc, nb=nb, bank=bank: e.matmul(
                        PB[bank][:, 0:512],
                        S["w_out_t"][:, kc * DM + mt * 128:kc * DM + (mt + 1) * 128],
                        S["yr_t"][:, (nb * 16 + kc) * 512:(nb * 16 + kc + 1) * 512],
                        start=(kc == 0), stop=(kc == 15))),
                    waits=w0 if kc == 0 else [], inc=(kc == 15))
            mm = em.ev("tensor")
            r1_ev[(nch, mt)] = em.emit("vector", _bld(
                lambda e, mt=mt, nch=nch, bank=bank: e.scalar_tensor_tensor(
                    S["r1_t"][:, mt * TQ + nch * 512:mt * TQ + (nch + 1) * 512],
                    S["xqb_t"][:, mt * TQ + nch * 512:mt * TQ + (nch + 1) * 512],
                    2.0, PB[bank][:, 0:512], ALU.mult, ALU.add)),
                waits=[mm, xq_all])
            sq_ev[(nch, mt)] = em.emit("scalar", _bld(
                lambda e, mt=mt, nch=nch: e.activation(
                    S["sq_t"][:], S["r1_t"][:, mt * TQ + nch * 512:mt * TQ + (nch + 1) * 512],
                    AF.Square, bias=0.0, scale=1.0)),
                waits=[r1_ev[(nch, mt)], st_mm.get((nch, mt - 1)), st_mm.get((nch - 1, 7))])
            em.emit("tensor", _bld(
                lambda e, mt=mt, nch=nch: e.matmul(
                    PB[SUM_B][0:1, 0:512], S["ones_rt"][:, 0:1],
                    S["r1_t"][:, mt * TQ + nch * 512:mt * TQ + (nch + 1) * 512],
                    start=(mt == 0), stop=(mt == 7))),
                waits=[r1_ev[(nch, mt)], srow_ev.get(nch - 1)])
            em.emit("tensor", _bld(
                lambda e, mt=mt: e.matmul(
                    PB[SQ_B][0:1, 0:512], S["onesb_t"][:, 0:1], S["sq_t"][:, 0:512].bitcast(BF16) if False else S["sq_t"][:],
                    start=(mt == 0), stop=(mt == 7))),
                waits=[sq_ev[(nch, mt)]])
            st_mm[(nch, mt)] = em.ev("tensor")
        smm = em.ev("tensor")
        ev1 = em.emit("scalar", _bld(lambda e: e.activation(
            S["srow_t"][0:1, :], PB[SUM_B][0:1, 0:512], AF.Copy, bias=0.0, scale=1.0)),
            waits=[smm])
        ev2 = em.emit("scalar", _bld(lambda e: e.activation(
            S["srow2_t"][0:1, :], PB[SQ_B][0:1, 0:512], AF.Copy, bias=0.0, scale=1.0)))
        sa = em.emit("gpsimd", _bld(lambda e, nch=nch: e.dma_start(
            out=D["srow_d"][0:1, nch * 512:(nch + 1) * 512], in_=S["srow_t"][0:1, :])),
            waits=[ev1], dma=True)
        sb = em.emit("gpsimd", _bld(lambda e, nch=nch: e.dma_start(
            out=D["srow_d"][1:2, nch * 512:(nch + 1) * 512], in_=S["srow2_t"][0:1, :])),
            waits=[ev2], dma=True)
        srow_ev[nch] = [sa, sb]

    if _P3_STOP == "op":
        if "x1" in _DBG:
            em.barrier()
            for mt in range(8):
                em.emit("gpsimd", _bld(lambda e, mt=mt: e.dma_start(
                    out=D["dbg_x1"][mt * 128:(mt + 1) * 128, :],
                    in_=S["r1_t"][:, mt * TQ:(mt + 1) * TQ].bitcast(F32))), dma=True)
        return
    ln_rows(ctx, "srow_d", "mrow_d", [srow_ev[n] for n in range(4)])
    mb_ev = em.emit("gpsimd", _bld(lambda e: e.dma_start(
        out=S["mbc_t"][:], in_=D["mrow_d"][0:1, :].broadcast_to([128, TQ]))),
        waits=[ctx["lnrow_ev"]], dma=True)
    rb_ev = em.emit("gpsimd", _bld(lambda e: e.dma_start(
        out=S["rbc2_t"][:], in_=D["mrow_d"][1:2, :].broadcast_to([128, TQ]))),
        waits=[ctx["lnrow_ev"]], dma=True)
    x1_ev = {}
    for mt in range(8):
        em.emit("vector", _bld(lambda e, mt=mt: e.tensor_tensor(
            S["r1_t"][:, mt * TQ:(mt + 1) * TQ], S["r1_t"][:, mt * TQ:(mt + 1) * TQ],
            S["mbc_t"][:], ALU.subtract)),
            waits=[mb_ev, rb_ev, r1_ev[(3, mt)], sq_ev[(3, mt)], st_mm[(3, 7)]])
        em.emit("vector", _bld(lambda e, mt=mt: e.tensor_tensor(
            S["r1_t"][:, mt * TQ:(mt + 1) * TQ], S["r1_t"][:, mt * TQ:(mt + 1) * TQ],
            S["rbc2_t"][:], ALU.mult)))
        x1_ev[mt] = em.emit("vector", _bld(lambda e, mt=mt: e.tensor_copy(
            S["xqb_t"][:, mt * TQ:(mt + 1) * TQ], S["r1_t"][:, mt * TQ:(mt + 1) * TQ])))
    x1_all = em.ev("vector")
    if "x1" in _DBG:
        for mt in range(8):
            em.emit("gpsimd", _bld(lambda e, mt=mt: e.dma_start(
                out=D["dbg_x1"][mt * 128:(mt + 1) * 128, :],
                in_=S["r1_t"][:, mt * TQ:(mt + 1) * TQ].bitcast(F32))),
                waits=[x1_all], dma=True)
        em.barrier()

    if _P3_STOP == "x1":
        return
    GL_B = 4
    lg_ev = {}
    cmb_tr = {}
    for tt in range(NTT):
        for kc in range(8):
            em.emit("tensor", _bld(
                lambda e, tt=tt, kc=kc: e.matmul(
                    PB[GL_B][0:128, 0:8],
                    S["r1_t"][:, kc * TQ + tt * 128:kc * TQ + (tt + 1) * 128],
                    S["gwt_t"][:, kc * E:(kc + 1) * E],
                    start=(kc == 0), stop=(kc == 7))),
                waits=([x1_all, ctx["init_ev"], lg_ev.get(tt - 1)] if kc == 0 else []),
                inc=(kc == 7))
        mmev = em.ev("tensor")
        lg_ev[tt] = em.emit("scalar", _bld(
            lambda e, tt=tt: e.activation(
                S["sc_t"][:, tt * E:(tt + 1) * E], PB[GL_B][0:128, 0:8],
                AF.Sigmoid, bias=0.0, scale=1.0)), waits=[mmev])
    scv = S["sc_t"][:].rearrange("p (g e) -> p g e", e=E)
    cbv = S["cb_t"][:].rearrange("p (g e) -> p g e", e=E)
    m1b = S["m1_t"][:].rearrange("p g -> p g").unsqueeze(2).broadcast_to([128, NTT, E])
    m2b = S["m2_t"][:].rearrange("p g -> p g").unsqueeze(2).broadcast_to([128, NTT, E])
    em.emit("vector", _bld(lambda e: e.tensor_reduce(
        S["m1_t"][:], scv, mybir.AxisListType.X, ALU.max)),
        waits=[lg_ev[t_] for t_ in range(NTT)])
    em.emit("vector", _bld(lambda e: e.tensor_tensor(
        S["lg_t"][:].rearrange("p (g e) -> p g e", e=E), scv, m1b, ALU.is_ge)))
    em.emit("vector", _bld(lambda e: e.scalar_tensor_tensor(
        cbv, S["lg_t"][:].rearrange("p (g e) -> p g e", e=E), -1e30, scv,
        ALU.mult, ALU.add)))
    em.emit("vector", _bld(lambda e: e.tensor_reduce(
        S["m2_t"][:], cbv, mybir.AxisListType.X, ALU.max)))
    em.emit("vector", _bld(lambda e: e.tensor_tensor(
        cbv, cbv, m2b, ALU.is_ge)))
    em.emit("vector", _bld(lambda e: e.tensor_tensor(
        S["lg_t"][:].rearrange("p (g e) -> p g e", e=E),
        S["lg_t"][:].rearrange("p (g e) -> p g e", e=E), m1b, ALU.mult)))
    em.emit("vector", _bld(lambda e: e.tensor_tensor(
        cbv, cbv, m2b, ALU.mult)))
    em.emit("vector", _bld(lambda e: e.tensor_tensor(
        cbv, cbv, S["lg_t"][:].rearrange("p (g e) -> p g e", e=E), ALU.add)))
    em.emit("vector", _bld(lambda e: e.tensor_tensor(
        S["tiny_t"][:, 0:16], S["m1_t"][:], S["m2_t"][:], ALU.add)))
    em.emit("vector", _bld(lambda e: e.tensor_scalar(
        S["tiny_t"][:, 0:16], S["tiny_t"][:, 0:16], 1e-20, None, ALU.add)))
    em.emit("vector", _bld(lambda e: e.reciprocal(
        S["m1_t"][:], S["tiny_t"][:, 0:16])))
    em.emit("vector", _bld(lambda e: e.tensor_scalar(
        S["m1_t"][:], S["m1_t"][:], SCALE_MOE, None, ALU.mult)))
    cmb_all = em.emit("vector", _bld(lambda e: e.tensor_tensor(
        cbv, cbv, m1b, ALU.mult)))
    for tt in range(NTT):
        evt = em.emit("tensor", _bld(lambda e, tt=tt: e.transpose(
            PB[5][0:8, 0:128], S["cb_t"][:, tt * E:(tt + 1) * E], S["idf_t"][:])),
            waits=[cmb_all, cmb_tr.get(tt - 1)])
        cmb_tr[tt] = em.emit("scalar", _bld(lambda e, tt=tt: e.activation(
            S["cbp_t"][0:8, tt * 128:(tt + 1) * 128], PB[5][0:8, 0:128],
            AF.Copy, bias=0.0, scale=1.0)), waits=[evt])
    cb_dma = em.emit("gpsimd", _bld(lambda e: e.dma_start(
        out=D["combt_d"], in_=S["cbp_t"][0:8, :])), waits=[cmb_tr[NTT - 1]], dma=True)
    if "comb" in _DBG:
        em.emit("gpsimd", _bld(lambda e: e.dma_start(
            out=D["dbg_comb"], in_=D["combt_d"])), waits=[cb_dma], dma=True)
    if "sc" in _DBG:
        em.emit("gpsimd", _bld(lambda e: e.dma_start(
            out=D["dbg_sc"], in_=S["sc_t"][:])), waits=[em.ev("vector")], dma=True)

    if _P3_STOP == "gate":
        return
    GB = (4, 6)
    UB = (5, 7)
    DN_B = (0, 1)
    SUM2_B, SQ2_B = 6, 7
    NCH2 = 4
    W2 = 512

    wg_ld, wu_ld, wd_ld, cbb_ev = {}, {}, {}, {}
    sg_ev, hm_ev, dn_ev = {}, {}, {}
    r2_ev, sq2_ev, s2_mm, srow2_ev = {}, {}, {}, {}
    gu_done = {}
    ffn_guard = [em.ev("scalar"), em.ev("vector")]
    nch_list = [0] if _P3_STOP == "ffn0" else list(range(NCH2))
    for nch in nch_list:
        for up in range(5):
            for ui in range(2):
                u = 2 * up + ui
                ub = u % 2
                wldw = [gu_done.get((nch, u - 2)), gu_done.get((nch - 1, u + NU - 2))]
                if nch == 0 and u == 0:
                    wldw += [x1_all, mb_ev, rb_ev, cb_dma]
                wg_ld[(nch, u)] = em.emit("sync", _bld(
                    lambda e, u=u, ub=ub: e.dma_start(
                        out=S["wg_t"][:, ub * 8 * IEX:(ub + 1) * 8 * IEX],
                        in_=_kpm(D["wg_all"][u]))), waits=wldw, dma=True)
                wu_ld[(nch, u)] = em.emit("sync", _bld(
                    lambda e, u=u, ub=ub: e.dma_start(
                        out=S["wu_t"][:, ub * 8 * IEX:(ub + 1) * 8 * IEX],
                        in_=_kpm(D["wu_all"][u]))), waits=wldw, dma=True)
                wd_ld[(nch, u)] = em.emit("sync", _bld(
                    lambda e, u=u, ub=ub: e.dma_start(
                        out=S["wd_t"][:, ub * 4 * DM:(ub + 1) * 4 * DM],
                        in_=_kpm(D["wd_all"][u]))),
                    waits=[dn_ev.get((nch, up - 1)), dn_ev.get((nch - 1, up + 4))]
                    + ([cb_dma, x1_all] if nch == 0 and up == 0 else []),
                    dma=True)
                if u >= 2:
                    cbb_ev[(nch, u)] = em.emit("gpsimd", _bld(
                        lambda e, u=u, nch=nch, ub=ub: e.dma_start(
                            out=S["cbbc_t"][:, ub * W2:(ub + 1) * W2],
                            in_=D["combt_d"][u - 2:u - 1, nch * W2:(nch + 1) * W2]
                            .broadcast_to([128, W2]))),
                        waits=[cb_dma, hm_ev.get((nch, u - 2, 3))], dma=True)
                for it in range(4):
                    gb_ = GB[it % 2]
                    ub_ = UB[it % 2]
                    w0 = [wg_ld[(nch, u)], x1_all, sg_ev.get((nch, u, it - 2)),
                          sg_ev.get((nch, u - 1, 2 + it % 2)),
                          sg_ev.get((nch - 1, NU - 1, 2 + it % 2)),
                          srow2_ev.get(nch - 1) if u == 0 else None]
                    if nch == 0 and u == 0 and it == 0:
                        w0 += ffn_guard
                    for kc in range(8):
                        em.emit("tensor", _bld(
                            lambda e, it=it, kc=kc, ub=ub, nch=nch, gb_=gb_: e.matmul(
                                PB[gb_][:, 0:W2],
                                S["wg_t"][:, ub * 8 * IEX + kc * IEX + it * 128:ub * 8 * IEX + kc * IEX + (it + 1) * 128],
                                S["xqb_t"][:, kc * TQ + nch * W2:kc * TQ + (nch + 1) * W2],
                                start=(kc == 0), stop=(kc == 7))),
                            waits=w0 if kc == 0 else [], inc=(kc == 7))
                    gmm = em.ev("tensor")
                    sg_ev[(nch, u, it)] = em.emit("scalar", _bld(
                        lambda e, it=it, gb_=gb_: e.activation(
                            S["sg_t"][:, it * W2:(it + 1) * W2], PB[gb_][:, 0:W2],
                            AF.Silu, bias=0.0, scale=1.0)), waits=[gmm])
                    w0u = [wu_ld[(nch, u)], hm_ev.get((nch, u - 1, 2 + it % 2)),
                           hm_ev.get((nch, u, it - 2)),
                           srow2_ev.get(nch - 1) if u == 0 else None]
                    if nch == 0 and u == 0 and it == 0:
                        w0u = w0u + ffn_guard
                    if nch > 0 and u <= 1 and it == 0:
                        w0u.append(hm_ev.get((nch - 1, u + 8, 3)))
                    for kc in range(8):
                        em.emit("tensor", _bld(
                            lambda e, it=it, kc=kc, ub=ub, nch=nch, ub_=ub_: e.matmul(
                                PB[ub_][:, 0:W2],
                                S["wu_t"][:, ub * 8 * IEX + kc * IEX + it * 128:ub * 8 * IEX + kc * IEX + (it + 1) * 128],
                                S["xqb_t"][:, kc * TQ + nch * W2:kc * TQ + (nch + 1) * W2],
                                start=(kc == 0), stop=(kc == 7))),
                            waits=w0u if kc == 0 else [], inc=(kc == 7))
                    umm = em.ev("tensor")
                    ev = em.emit("vector", _bld(
                        lambda e, it=it, ub=ub, ub_=ub_: e.tensor_tensor(
                            S["hm_t"][:, (ub * 4 + it) * W2:(ub * 4 + it + 1) * W2],
                            S["sg_t"][:, it * W2:(it + 1) * W2], PB[ub_][:, 0:W2],
                            ALU.mult)),
                        waits=[umm, sg_ev[(nch, u, it)], dn_ev.get((nch, up - 1))])
                    if u >= 2:
                        ev = em.emit("vector", _bld(
                            lambda e, it=it, ub=ub: e.tensor_tensor(
                                S["hm_t"][:, (ub * 4 + it) * W2:(ub * 4 + it + 1) * W2],
                                S["hm_t"][:, (ub * 4 + it) * W2:(ub * 4 + it + 1) * W2],
                                S["cbbc_t"][:, ub * W2:(ub + 1) * W2], ALU.mult)),
                            waits=[cbb_ev[(nch, u)]])
                    hm_ev[(nch, u, it)] = ev
                gu_done[(nch, u)] = em.ev("tensor")
            hm0 = hm_ev[(nch, 2 * up, 3)]
            hm1 = hm_ev[(nch, 2 * up + 1, 3)]
            for mt in range(8):
                bank = DN_B[mt % 2]
                w0 = [wd_ld[(nch, 2 * up)], wd_ld[(nch, 2 * up + 1)], hm0, hm1,
                      r2_ev.get((nch, up, mt - 2)), r2_ev.get((nch, up - 1, mt + 6)),
                      r2_ev.get((nch - 1, 4, mt + 6))]
                if nch == 0 and up == 0 and mt <= 1:
                    w0 += ffn_guard
                first = True
                for ui in range(2):
                    u = 2 * up + ui
                    ub = u % 2
                    for kc in range(4):
                        em.emit("tensor", _bld(
                            lambda e, mt=mt, kc=kc, ub=ub, ui=ui, bank=bank: e.matmul(
                                PB[bank][:, 0:W2],
                                S["wd_t"][:, ub * 4 * DM + kc * DM + mt * 128:ub * 4 * DM + kc * DM + (mt + 1) * 128],
                                S["hm_t"][:, (ub * 4 + kc) * W2:(ub * 4 + kc + 1) * W2],
                                start=(ui == 0 and kc == 0), stop=(ui == 1 and kc == 3))),
                            waits=w0 if first else [], inc=(ui == 1 and kc == 3))
                        first = False
                dmm = em.ev("tensor")
                r2_ev[(nch, up, mt)] = em.emit("vector", _bld(
                    lambda e, mt=mt, nch=nch, bank=bank: e.tensor_tensor(
                        S["r1_t"][:, mt * TQ + nch * W2:mt * TQ + (nch + 1) * W2],
                        S["r1_t"][:, mt * TQ + nch * W2:mt * TQ + (nch + 1) * W2],
                        PB[bank][:, 0:W2], ALU.add)), waits=[dmm])
            dn_ev[(nch, up)] = em.ev("vector")
        for mt in range(8):
            r2_ev[(nch, mt)] = r2_ev[(nch, 4, mt)]
            sq2_ev[(nch, mt)] = em.emit("scalar", _bld(
                lambda e, mt=mt, nch=nch: e.activation(
                    S["sq_t"][:, 0:W2],
                    S["r1_t"][:, mt * TQ + nch * W2:mt * TQ + (nch + 1) * W2],
                    AF.Square, bias=0.0, scale=1.0)),
                waits=[r2_ev[(nch, mt)], s2_mm.get((nch, mt - 1)), s2_mm.get((nch - 1, 7))])
            em.emit("tensor", _bld(
                lambda e, mt=mt, nch=nch: e.matmul(
                    PB[SUM2_B][0:1, 0:W2], S["ones_rt"][:, 0:1],
                    S["r1_t"][:, mt * TQ + nch * W2:mt * TQ + (nch + 1) * W2],
                    start=(mt == 0), stop=(mt == 7))),
                waits=[r2_ev[(nch, mt)], srow2_ev.get(nch - 1)] + (ffn_guard if nch == 0 and mt == 0 else []))
            em.emit("tensor", _bld(
                lambda e, mt=mt: e.matmul(
                    PB[SQ2_B][0:1, 0:W2], S["onesb_t"][:, 0:1], S["sq_t"][:, 0:W2],
                    start=(mt == 0), stop=(mt == 7))),
                waits=[sq2_ev[(nch, mt)]] + (ffn_guard if nch == 0 and mt == 0 else []))
            s2_mm[(nch, mt)] = em.ev("tensor")
        smm = em.ev("tensor")
        ev1 = em.emit("scalar", _bld(lambda e: e.activation(
            S["srow_t"][0:1, 0:W2], PB[SUM2_B][0:1, 0:W2], AF.Copy, bias=0.0, scale=1.0)),
            waits=[smm])
        ev2 = em.emit("scalar", _bld(lambda e: e.activation(
            S["srow2_t"][0:1, 0:W2], PB[SQ2_B][0:1, 0:W2], AF.Copy, bias=0.0, scale=1.0)))
        sa = em.emit("gpsimd", _bld(lambda e, nch=nch: e.dma_start(
            out=D["srow_d"][0:1, nch * W2:(nch + 1) * W2], in_=S["srow_t"][0:1, 0:W2])),
            waits=[ev1], dma=True)
        sb = em.emit("gpsimd", _bld(lambda e, nch=nch: e.dma_start(
            out=D["srow_d"][1:2, nch * W2:(nch + 1) * W2], in_=S["srow2_t"][0:1, 0:W2])),
            waits=[ev2], dma=True)
        srow2_ev[nch] = [sa, sb]

    if _P3_STOP in ("ffn0", "ffnall"):
        return
    ln_rows(ctx, "srow_d", "mrow_d", [srow2_ev[n] for n in range(NCH2)])
    mb2 = em.emit("gpsimd", _bld(lambda e: e.dma_start(
        out=S["mbc_t"][:], in_=D["mrow_d"][0:1, :].broadcast_to([128, TQ]))),
        waits=[ctx["lnrow_ev"]], dma=True)
    rb2 = em.emit("gpsimd", _bld(lambda e: e.dma_start(
        out=S["rbc2_t"][:], in_=D["mrow_d"][1:2, :].broadcast_to([128, TQ]))),
        waits=[ctx["lnrow_ev"]], dma=True)
    if _P3_STOP == "ln2":
        return
    out_st = {}
    xo_ev = {}
    tr_ev = {}
    for tt in range(NTT):
        ob = 0
        for mt in range(8):
            w = [mb2, rb2, r2_ev[(NCH2 - 1, mt)], s2_mm[(NCH2 - 1, 7)], tr_ev.get((tt - 1, mt))]
            if mt == 0:
                w.append(out_st.get(tt - 2))
            e1 = em.emit("vector", _bld(lambda e, tt=tt, mt=mt: e.tensor_tensor(
                S["x2_t"][:, mt * 128:(mt + 1) * 128],
                S["r1_t"][:, mt * TQ + tt * 128:mt * TQ + (tt + 1) * 128],
                S["mbc_t"][:, tt * 128:(tt + 1) * 128], ALU.subtract)), waits=w)
            e2 = em.emit("vector", _bld(lambda e, tt=tt, mt=mt: e.tensor_tensor(
                S["x2_t"][:, mt * 128:(mt + 1) * 128],
                S["x2_t"][:, mt * 128:(mt + 1) * 128],
                S["rbc2_t"][:, tt * 128:(tt + 1) * 128], ALU.mult)))
            evt = em.emit("tensor", _bld(lambda e, mt=mt: e.transpose(
                PB[4 + (mt % 2)][:, (mt // 2) * 128:(mt // 2 + 1) * 128],
                S["x2_t"][:, mt * 128:(mt + 1) * 128], S["idf_t"][:])),
                waits=[e2, xo_ev.get((tt - 1, 6 + (mt % 2))), xo_ev.get((tt, mt - 2))])
            tr_ev[(tt, mt)] = evt
            xo_ev[(tt, mt)] = em.emit("scalar", _bld(lambda e, mt=mt, ob=ob: e.activation(
                S["xo_t"][:, ob * DM + mt * 128:ob * DM + (mt + 1) * 128],
                PB[4 + (mt % 2)][:, (mt // 2) * 128:(mt // 2 + 1) * 128],
                AF.Copy, bias=0.0, scale=1.0)), waits=[evt, out_st.get(tt - 1)])
        oev = em.ev("scalar")
        out_st[tt] = em.emit("gpsimd", _bld(lambda e, tt=tt, ob=ob: e.dma_start(
            out=D["out"][tt * 128:(tt + 1) * 128, :], in_=S["xo_t"][:, ob * DM:(ob + 1) * DM])),
            waits=[oev], dma=True)


def ln_rows(ctx, src, dst, waits):
    D, S, em = ctx["D"], ctx["S"], ctx["em"]
    l1 = em.emit("sync", _bld(lambda e: e.dma_start(
        out=S["m16_t"][:, 0:16], in_=D[src][0:1, :].rearrange("o (p w) -> (o p) w", p=128))),
        waits=waits, dma=True)
    l2 = em.emit("sync", _bld(lambda e: e.dma_start(
        out=S["m16_t"][:, 16:32], in_=D[src][1:2, :].rearrange("o (p w) -> (o p) w", p=128))),
        waits=waits, dma=True)
    e1 = em.emit("vector", _bld(lambda e: e.tensor_scalar(
        S["m16_t"][:, 0:16], S["m16_t"][:, 0:16], 1.0 / DM, None, ALU.mult)),
        waits=[l1])
    e2 = em.emit("vector", _bld(lambda e: e.tensor_tensor(
        S["m16_t"][:, 32:48], S["m16_t"][:, 0:16], S["m16_t"][:, 0:16], ALU.mult)),
        waits=[l2])
    e3 = em.emit("vector", _bld(lambda e: e.scalar_tensor_tensor(
        S["m16_t"][:, 48:64], S["m16_t"][:, 16:32], 1.0 / DM, S["m16_t"][:, 32:48],
        ALU.mult, ALU.subtract)))
    e4 = em.emit("scalar", _bld(lambda e: e.activation(
        S["m16_t"][:, 48:64], S["m16_t"][:, 48:64], AF.Sqrt, bias=EPS, scale=1.0)),
        waits=[e3])
    e5 = em.emit("vector", _bld(lambda e: e.reciprocal(
        S["m16_t"][:, 48:64], S["m16_t"][:, 48:64])), waits=[e4])
    s1 = em.emit("gpsimd", _bld(lambda e: e.dma_start(
        out=D[dst][0:1, :].rearrange("o (p w) -> (o p) w", p=128), in_=S["m16_t"][:, 0:16])),
        waits=[e1], dma=True)
    s2 = em.emit("gpsimd", _bld(lambda e: e.dma_start(
        out=D[dst][1:2, :].rearrange("o (p w) -> (o p) w", p=128), in_=S["m16_t"][:, 48:64])),
        waits=[e5], dma=True)
    ctx["lnrow_ev"] = s2
    em.wait_only("gpsimd", [s1])


def _prep_core_inputs(c, I):
    b, q = c // 4, c % 4
    sl = slice(q * DQ, (q + 1) * DQ)
    f = np.float32
    bf = ml_dtypes.bfloat16
    cont = np.ascontiguousarray

    d = {}
    xb = np.asarray(I["L0_attn_hidden_states"][b], dtype=f)
    d["x"] = cont(xb)
    d["xq"] = cont(xb[q * TQ:(q + 1) * TQ])
    ipw = np.asarray(I["L0_attn_in_proj_w"], dtype=f)
    d["w_inx"] = cont(ipw[sl].T)
    d["w_inz"] = cont(ipw[2048 + q * DQ:2048 + (q + 1) * DQ].T)
    d["convw"] = cont(np.asarray(I["L0_attn_conv1d_w"], dtype=f)[sl].T)
    d["convb"] = cont(np.asarray(I["L0_attn_conv1d_b"], dtype=f)[sl].reshape(NDT, 128).T)
    d["w_xp"] = cont(np.asarray(I["L0_attn_x_proj_w"], dtype=f)[:, sl].T.astype(bf))
    d["w_dt"] = cont(np.asarray(I["L0_attn_dt_proj_w"], dtype=f)[sl].T)
    d["dtb"] = cont(np.asarray(I["L0_attn_dt_proj_b"], dtype=f)[sl].reshape(NDT, 128).T)
    d["acol"] = cont(np.asarray(I["L0_attn_A_log"], dtype=f)[sl]
                     .reshape(NDT, 128, DS).transpose(1, 0, 2).reshape(128, NDT * DS))
    d["dcol"] = cont(np.asarray(I["L0_attn_D"], dtype=f)[sl].reshape(NDT, 128).T)
    d["nrmw"] = cont(np.asarray(I["L0_attn_norm_weight"], dtype=f).reshape(-1, 128).T)
    d["w_out"] = cont(np.asarray(I["L0_attn_out_proj_w"], dtype=f).T.astype(bf))
    d["w_gate"] = cont(np.asarray(I["L0_ffn_gate_weight"], dtype=f).T)
    wg = np.asarray(I["L0_ffn_w_gate"], dtype=f)
    wu = np.asarray(I["L0_ffn_w_up"], dtype=f)
    wd = np.asarray(I["L0_ffn_w_down"], dtype=f)
    wsg = np.asarray(I["L0_ffn_ws_gate"], dtype=f)
    wsu = np.asarray(I["L0_ffn_ws_up"], dtype=f)
    wsd = np.asarray(I["L0_ffn_ws_down"], dtype=f)
    g_all = np.concatenate([wsg.reshape(2, IEX, DM), wg], axis=0)
    u_all = np.concatenate([wsu.reshape(2, IEX, DM), wu], axis=0)
    dn_all = np.concatenate([wsd.T.reshape(2, IEX, DM), wd.transpose(0, 2, 1)], axis=0)
    d["wg_all"] = cont(g_all.transpose(0, 2, 1).astype(bf))
    d["wu_all"] = cont(u_all.transpose(0, 2, 1).astype(bf))
    d["wd_all"] = cont(dn_all.astype(bf))
    d["idbf"] = np.eye(128, dtype=bf)
    d["idf"] = np.eye(128, dtype=f)
    d["ones_r"] = np.ones((128, 8), dtype=f)
    cvals = np.zeros(8, dtype=f)
    cvals[:4] = [EPS, 1.0, -1.0, 1.0 / DM]
    d["consts"] = np.broadcast_to(cvals, (128, 8)).copy()
    yidx = np.empty((128, 64), dtype=np.int32)
    for kc in range(16):
        for nch in range(4):
            yidx[:, kc * 4 + nch] = (kc * 128 + np.arange(128)) * (T // 512) + q * 4 + nch
    d["yidx"] = yidx
    return d


def kernel(**inputs):
    nc, D = build_program()
    in_maps = [_prep_core_inputs(c, inputs) for c in range(NCORE)]
    res = run_bass_kernel_spmd(nc, in_maps, core_ids=list(range(NCORE)))
    out = np.empty((B, T, DM), dtype=np.float32)
    for c in range(NCORE):
        b, q = c // 4, c % 4
        out[b, q * TQ:(q + 1) * TQ] = res.results[c]["out"]
    return out


# version 12
# speedup vs baseline: 1.0669x; 1.0669x over previous
import numpy as np
import ml_dtypes
import concourse.bass as bass
import concourse.mybir as mybir
from concourse.bass import IndirectOffsetOnAxis
from concourse.bass_utils import run_bass_kernel_spmd

F32 = mybir.dt.float32
F32R = mybir.dt.float32r
BF16 = mybir.dt.bfloat16
I32 = mybir.dt.int32
AF = mybir.ActivationFunctionType
ALU = mybir.AluOpType

B, T, DM = 2, 8192, 1024
DI, DS, DTR, DC = 2048, 16, 64, 4
EPS = 1e-5
NCORE = 8
DQ = DI // 4
NDT = DQ // 128
TQ = T // 4
E = 8
IEX = 512
NU = 10
SCALE_MOE = 2.5
TC = 1024
NTC = T // TC
TG = 512
NTG = T // TG
GROUPS = [[0, 1, 2, 3], [4, 5, 6, 7]]

_DBG = []
_PHASES = ("p1", "dt", "scan", "p3")
_P3_STOP = None

ENGINES = ("sync", "gpsimd", "scalar", "vector", "tensor")


class Em:

    NCH = 4

    def __init__(self, nc):
        self.nc = nc
        self.streams = {e: [] for e in ENGINES}
        self.sems = {}
        self.counts = {}
        for e in ENGINES:
            cm = nc.semaphore(f"S_{e}_c")
            self.sems[(e, "c")] = cm.__enter__()
            self.counts[(e, "c")] = 0
        for e in ("sync", "gpsimd", "scalar"):
            for i in range(self.NCH):
                cm = nc.semaphore(f"S_{e}_q{i}")
                self.sems[(e, f"q{i}")] = cm.__enter__()
                self.counts[(e, f"q{i}")] = 0
        self.rr = {e: 0 for e in ENGINES}
        self.waited = {e: {} for e in ENGINES}

    def ev(self, engine, kind="c"):
        return (engine, kind, self.counts[(engine, kind)])

    def qsnap(self, engine):
        return [(engine, f"q{i}", self.counts[(engine, f"q{i}")])
                for i in range(self.NCH)]

    def _wl(self, engine, kind, waits):
        wl = []
        flat = []
        for w in waits:
            if w is None:
                continue
            if isinstance(w, list):
                flat.extend(x for x in w if x is not None)
            else:
                flat.append(w)
        for w in flat:
            (pe_, pk, val) = w
            if pe_ == engine and pk == "c" and kind == "c" and engine == "tensor":
                continue
            key = (pe_, pk)
            if val <= self.waited[engine].get(key, 0):
                continue
            self.waited[engine][key] = val
            wl.append((self.sems[key], val))
        return wl

    def emit(self, engine, build, waits=(), dma=False, inc=True, inc_amt=None):
        if dma:
            ci = self.rr[engine] % self.NCH
            self.rr[engine] += 1
            kind = f"q{ci}"
            amt = 16
            prev = self.counts[(engine, kind)]
            wl = self._wl(engine, kind, waits)
            key = (engine, kind)
            if prev > self.waited[engine].get(key, 0):
                self.waited[engine][key] = prev
                wl.insert(0, (self.sems[key], prev))
            self.counts[key] = prev + amt
            tgt = prev + amt
            sem = self.sems[key]
        else:
            kind = "c"
            amt = inc_amt if inc_amt is not None else 1
            wl = self._wl(engine, kind, waits)
            if engine in ("vector", "scalar", "gpsimd"):
                key = (engine, "c")
                prev = self.counts[key]
                if prev > self.waited[engine].get(key, 0):
                    self.waited[engine][key] = prev
                    wl.insert(0, (self.sems[key], prev))
            if inc:
                self.counts[(engine, kind)] += amt
            tgt = self.counts[(engine, kind)]
            sem = self.sems[(engine, kind)]

        def run(eng, wl=wl, build=build, sem=sem, amt=amt, inc=inc):
            for (h, v) in wl:
                eng.wait_ge(h, v)
            ins = build(eng)
            if inc and ins is not None:
                ins.then_inc(sem, amt)
            return ins

        self.streams[engine].append(run)
        return (engine, kind, tgt)

    def wait_only(self, engine, waits):
        wl = self._wl(engine, "c", [w for w in waits if w is not None])
        if not wl:
            return

        def run(eng, wl=wl):
            for (h, v) in wl:
                eng.wait_ge(h, v)

        self.streams[engine].append(run)

    def barrier(self):
        evs = [(e, k, self.counts[(e, k)]) for (e, k) in self.counts]
        for e in ENGINES:
            self.wait_only(e, evs)


def _bld(fn, *a, **k):
    return lambda eng: fn(eng, *a, **k)


def _kpm(ap, p=128):
    return ap.rearrange("(kc p) m -> kc p m", p=p).transpose([1, 0, 2])


def build_program():
    nc = bass.Bass("TRN2", target_bir_lowering=False, debug=False)
    D, S = {}, {}

    def din(name, shape, dt=F32):
        D[name] = nc.dram_tensor(name, shape, dt, kind="ExternalInput").ap()

    def dout(name, shape, dt=F32):
        D[name] = nc.dram_tensor(name, shape, dt, kind="ExternalOutput").ap()

    def dint(name, shape, dt=F32):
        D[name] = nc.dram_tensor(name, shape, dt).ap()

    din("x", [T, DM])
    din("xq", [TQ, DM])
    din("w_inx", [DM, DQ])
    din("w_inz", [DM, DQ])
    din("convw", [DC, DQ])
    din("convb", [128, NDT])
    din("w_xp", [DQ, 96], BF16)
    din("w_dt", [DTR, DQ], F32R)
    din("dtb", [128, NDT])
    din("acol", [128, NDT * DS])
    din("dcol", [128, NDT])
    din("nrmw", [128, DM // 128])
    din("w_out", [DI, DM], BF16)
    din("w_gate", [DM, E], F32R)
    din("wg_all", [NU, DM, IEX], BF16)
    din("wu_all", [NU, DM, IEX], BF16)
    din("wd_all", [NU, IEX, DM], BF16)
    din("idbf", [128, 128], BF16)
    din("idf", [128, 128])
    din("ones_r", [128, 8], F32R)
    din("yidx", [128, 64], I32)
    din("consts", [128, 8])
    dout("out", [TQ, DM])

    dint("rstd_row", [1, T])
    dint("ppart", [2, 96, T // 2])
    dint("pred", [2, 96, T // 2])
    dint("dt_d", [DQ, T])
    dint("xc_d", [DQ, T], BF16)
    dint("sz_d", [DQ, T], BF16)
    dint("predb", [2, 32, T // 2], BF16)
    dint("ygat_in", [DQ, T], BF16)
    dint("ygat_all", [DI, T], BF16)
    dint("srow_d", [2, TQ])
    dint("mrow_d", [2, TQ])
    dint("combt_d", [E, TQ])

    dbg_shapes = {
        "xc": ([DQ, T], BF16), "sz": ([DQ, T], BF16), "ppart": ([96, T], F32),
        "pred": ([96, T], F32), "dt": ([DQ, T], F32), "y": ([DQ, T], BF16),
        "ygall": ([DI, T], BF16), "r1": ([DM, TQ], F32), "x1": ([DM, TQ], F32),
        "comb": ([E, TQ], F32), "sc": ([128, 16 * E], F32),
    }
    for nm in _DBG:
        sh, dt_ = dbg_shapes[nm]
        dout("dbg_" + nm, sh, dt_)

    SB_BASE = 16512
    SB_END = 229376
    cur = [SB_BASE]

    def salloc(name, shape, dt=F32, at=None):
        szb = {F32: 4, F32R: 4, BF16: 2, I32: 4}[dt]
        nbytes = int(np.prod(shape[1:])) * szb
        off = ((cur[0] if at is None else at) + 63) & ~63
        if at is None:
            cur[0] = off + nbytes
        assert off + nbytes <= SB_END, (name, off, nbytes)
        S[name] = nc.alloc_sbuf_tensor_at(name, shape, dt, offset=off)
        return off + nbytes

    for nm, sh, dt_ in [
        ("convb_t", [128, NDT], F32), ("dtb_t", [128, NDT], F32),
        ("acol_t", [128, NDT * DS], F32), ("dcol_t", [128, NDT], F32),
        ("nrmw_t", [128, DM // 128], F32), ("idbf_t", [128, 128], BF16),
        ("idf_t", [128, 128], F32), ("ones_rt", [128, 8], F32R),
        ("onesb_t", [128, 8], BF16), ("st_t", [128, NDT * DS], F32),
        ("yidx_t", [128, 64], I32),
        ("spa_t", [128, TG], F32), ("spb_t", [128, 2 * TG], F32),
        ("spc_t", [128, 2 * TG], F32), ("sq_t", [128, TG], BF16),
        ("srow_t", [1, TG], F32), ("srow2_t", [1, TG], F32),
        ("m16_t", [128, 64], F32),
        ("tiny_t", [128, 64], F32),
        ("gwt_t", [128, 8 * E], F32R), ("xstage_t", [128, 2 * DM], F32),
        ("w_dt_t", [64, DQ], F32R), ("dtraw_t", [64, 2 * TG], F32R),
        ("m1_t", [128, 16], F32), ("m2_t", [128, 16], F32),
        ("msk_t", [128, 8], F32), ("msk2_t", [128, 8], F32),
        ("lg_t", [128, 16 * E], F32), ("sc_t", [128, 16 * E], F32),
        ("cb_t", [128, 16 * E], F32), ("cbp_t", [E, TQ], F32),
    ]:
        salloc(nm, sh, dt_)
    salloc("consts_t", [128, 8], F32)
    nc.const_aps.register(F32, EPS, S["consts_t"][:, 0:1])
    nc.const_aps.register(F32, 1.0, S["consts_t"][:, 1:2])
    nc.const_aps.register(F32, -1.0, S["consts_t"][:, 2:3])
    nc.const_aps.register(F32, 1.0 / DM, S["consts_t"][:, 3:4])
    arena = (cur[0] + 63) & ~63

    a = arena
    a = salloc("wtap_t", [128, DC * 8 * DQ], BF16, at=a)
    a = salloc("w_inz_t", [128, 8 * DQ], BF16, at=a)
    a = salloc("cwbc_t", [128, DC * DQ], F32, at=a)
    a = salloc("wtmp_t", [128, DQ], F32, at=a)
    a = salloc("w_xp_t", [128, NDT * 96], BF16, at=a)
    a = salloc("xt_t", [128, 2 * 4 * DM], F32, at=a)
    a = salloc("hn_t", [128, 2 * 8 * (TG + 3)], BF16, at=a)
    a = salloc("rbc_t", [128, 2 * TG], F32, at=a)
    a = salloc("xcg_t", [128, NDT * TG], BF16, at=a)
    a = salloc("szg_t", [128, TG], BF16, at=a)
    a = salloc("rstdc_t", [128, 64], F32, at=a)
    a = salloc("rsev_t", [4, 2 * 128], F32, at=a)
    a = salloc("projs_t", [96, 2 * TG], F32, at=a)
    assert a <= SB_END


    a = arena
    a = salloc("dt_c", [128, 2 * NDT * TC], F32, at=a)
    a = salloc("xcc", [128, 2 * NDT * TC], BF16, at=a)
    a = salloc("szc", [128, 2 * NDT * TC], BF16, at=a)
    a = salloc("dtx_c", [128, 2 * NDT * TC], BF16, at=a)
    a = salloc("bbc", [128, 4 * TC], BF16, at=a)
    a = salloc("cbc", [128, 4 * TC], BF16, at=a)
    a = salloc("da_t", [128, 4 * TC], F32, at=a)
    a = salloc("dbx_t", [128, 4 * TC], BF16, at=a)
    a = salloc("h_t", [128, 6 * TC], BF16, at=a)
    a = salloc("hc_t", [128, 6 * TC], BF16, at=a)
    a = salloc("y1_t", [128, TC], F32, at=a)
    a = salloc("yg_t", [128, 2 * TC], BF16, at=a)
    assert a <= SB_END

    a = arena
    a = salloc("xqb_t", [128, 8 * TQ], BF16, at=a)
    XQ_OFF = arena
    a = salloc("r1_t", [128, 8 * TQ], F32R, at=a)
    R1_OFF = a - 8 * TQ * 4
    a3 = a
    a = salloc("w_out_t", [128, 16 * DM], BF16, at=a3)
    YR_OFF = a
    a = salloc("yr_t", [128, 2 * 16 * 512], BF16, at=a)
    assert a <= SB_END
    a = a3
    WG_OFF = a
    a = salloc("wg_t", [128, 2 * 8 * IEX], BF16, at=a)
    a = salloc("wu_t", [128, 2 * 8 * IEX], BF16, at=a)
    a = salloc("wd_t", [128, 2 * 4 * DM], BF16, at=a)
    a = salloc("sg_t", [128, 4 * 512], BF16, at=a)
    a = salloc("hm_t", [128, 2 * 4 * 512], BF16, at=a)
    a = salloc("cbbc_t", [128, 2 * 512], BF16, at=a)
    a = salloc("x2_t", [128, 8 * 128], F32, at=a)
    a = salloc("xo_t", [128, DM], F32, at=a)
    assert a <= SB_END, ("ffn arena", a)
    salloc("mbc_t", [128, TQ], F32, at=WG_OFF)
    salloc("rbc2_t", [128, TQ], F32, at=WG_OFF + TQ * 4)
    salloc("cbp_t", [E, TQ], F32, at=YR_OFF)

    x1b = S["xqb_t"]

    PB = [nc.alloc_psum_tensor(f"pb{i}", [128, 512], F32) for i in range(8)]

    em = Em(nc)
    ctx = dict(nc=nc, D=D, S=S, PB=PB, em=em)

    emit_init(ctx)
    if "p1" in _PHASES:
        emit_phase1(ctx)
    if "dt" in _PHASES:
        ctx["dt_done"] = {}
        emit_dt_init(ctx)
    if "scan" in _PHASES:
        emit_scan(ctx)
    if "p3" in _PHASES:
        emit_phase3(ctx)
    em.barrier()

    with nc.Block() as block:
        @block.sync
        def _(eng):
            for f in em.streams["sync"]:
                f(eng)

        @block.gpsimd
        def _(eng):
            for f in em.streams["gpsimd"]:
                f(eng)

        @block.scalar
        def _(eng):
            for f in em.streams["scalar"]:
                f(eng)

        @block.vector
        def _(eng):
            for f in em.streams["vector"]:
                f(eng)

        @block.tensor
        def _(eng):
            for f in em.streams["tensor"]:
                f(eng)

    return nc, D


def emit_init(ctx):
    D, S, em = ctx["D"], ctx["S"], ctx["em"]
    for nm_s, nm_d in [("convb_t", "convb"), ("dtb_t", "dtb"), ("acol_t", "acol"),
                       ("dcol_t", "dcol"), ("nrmw_t", "nrmw"), ("idbf_t", "idbf"),
                       ("idf_t", "idf"), ("ones_rt", "ones_r"), ("yidx_t", "yidx"),
                       ("consts_t", "consts")]:
        em.emit("sync", _bld(lambda e, o=S[nm_s][:], i=D[nm_d]: e.dma_start(out=o, in_=i)),
                dma=True)
    em.emit("sync", _bld(lambda e: e.dma_start(out=S["gwt_t"][:], in_=_kpm(D["w_gate"]))),
            dma=True)
    ctx["init_ev"] = em.qsnap("sync")
    em.emit("gpsimd", _bld(lambda e: e.dma_start(
        out=S["onesb_t"][:], in_=D["ones_r"].bitcast(F32))), dma=True)
    ctx["init_ev_g"] = em.qsnap("gpsimd")
    e1 = em.emit("scalar", _bld(lambda e: e.activation(
        S["acol_t"][:], S["acol_t"][:], AF.Exp, bias=0.0, scale=1.0)),
        waits=[ctx["init_ev"]])
    em.emit("vector", _bld(lambda e: e.tensor_scalar(
        S["acol_t"][:], S["acol_t"][:], -1.0, None, ALU.mult)), waits=[e1])
    ctx["acol_ev"] = em.ev("vector")


def emit_phase1(ctx):
    D, S, PB, em = ctx["D"], ctx["S"], ctx["PB"], ctx["em"]
    hnW = TG + 3

    ev_z = em.emit("gpsimd", _bld(lambda e: e.dma_start(
        out=S["w_inz_t"][:], in_=_kpm(D["w_inz"]))),
        dma=True)
    for kc in range(8):
        em.emit("vector", _bld(lambda e, kc=kc: e.tensor_scalar(
            S["w_inz_t"][:, kc * DQ:(kc + 1) * DQ], S["w_inz_t"][:, kc * DQ:(kc + 1) * DQ],
            S["nrmw_t"][:, kc:kc + 1], None, ALU.mult)), waits=[ev_z, ctx["init_ev"]])
    ctx["winz_ev"] = em.ev("vector")

    ev_cw = em.emit("gpsimd", _bld(lambda e: e.dma_start(
        out=S["cwbc_t"][:],
        in_=D["convw"].rearrange("k m -> (k m)").unsqueeze(0).broadcast_to([128, DC * DQ]))),
        dma=True)
    for kc in range(8):
        lv = em.emit("sync", _bld(lambda e, kc=kc: e.dma_start(
            out=S["wtmp_t"][:], in_=D["w_inx"][kc * 128:(kc + 1) * 128, :])),
            waits=[em.ev("vector")], dma=True)
        em.emit("vector", _bld(lambda e, kc=kc: e.tensor_scalar(
            S["wtmp_t"][:], S["wtmp_t"][:], S["nrmw_t"][:, kc:kc + 1], None, ALU.mult)),
            waits=[lv, ctx["init_ev"]])
        for k in range(DC):
            em.emit("vector", _bld(lambda e, kc=kc, k=k: e.tensor_tensor(
                S["wtap_t"][:, (k * 8 + kc) * DQ:(k * 8 + kc + 1) * DQ],
                S["wtmp_t"][:], S["cwbc_t"][:, k * DQ:(k + 1) * DQ], ALU.mult)),
                waits=[ev_cw])
    ctx["wtap_ev"] = em.ev("vector")
    ev_xp = em.emit("sync", _bld(lambda e: e.dma_start(
        out=S["w_xp_t"][:], in_=_kpm(D["w_xp"]))),
        dma=True)

    hn = [S["hn_t"][:, 0:8 * hnW].rearrange("p (kc w) -> p kc w", kc=8),
          S["hn_t"][:, 8 * hnW:16 * hnW].rearrange("p (kc w) -> p kc w", kc=8)]
    for b_ in range(2):
        for kc in range(8):
            em.emit("vector", _bld(lambda e, b_=b_, kc=kc: e.tensor_copy(
                S["hn_t"][:, (b_ * 8 + kc) * hnW:(b_ * 8 + kc) * hnW + 3],
                S["consts_t"][:, 4:5].broadcast_to([128, 3]))),
                waits=[ctx["init_ev"]])
    hz_ev = em.ev("vector")

    xld = {}
    rbc_ev = {}
    hn_ev = {}
    tail_ev = {}
    pe_hn = {}
    xc_ev = {}
    sz_st = {}
    xc_st = {}
    pp_st = {}
    CONV_B = (0, 1)
    Z_B = (2, 3)
    XT_B = (4, 5)
    PP_B = 6

    rst_ev = {}
    rrow_st = {}
    ca_ev = {}

    def emit_xld(g):
        gb = g % 2
        for tt in range(4):
            w = [xld.get((g - 2, tt)), hn_ev.get(g - 2)]
            xld[(g, tt)] = em.emit("sync", _bld(
                lambda e, g=g, tt=tt, gb=gb: e.dma_start(
                    out=S["xt_t"][:, (gb * 4 + tt) * DM:(gb * 4 + tt + 1) * DM],
                    in_=D["x"][g * TG + tt * 128:g * TG + (tt + 1) * 128, :])),
                waits=w, dma=True)

    def emit_chainA(g):
        gb = g % 2
        for tt in range(4):
            buf = S["xt_t"][:, (gb * 4 + tt) * DM:(gb * 4 + tt + 1) * DM]
            em.emit("scalar", _bld(
                lambda e, buf=buf, gb=gb, tt=tt: e.activation(
                    S["sq_t"][:], buf[:, 0:TG], AF.Square, bias=0.0, scale=1.0,
                    accum_out=S["rstdc_t"][:, gb * 8 + tt:gb * 8 + tt + 1])),
                waits=[xld[(g, tt)], rst_ev.get(g - 2)])
            em.emit("scalar", _bld(
                lambda e, buf=buf: e.activation(
                    S["sq_t"][:], buf[:, TG:DM], AF.Square, bias=0.0, scale=1.0,
                    accum_out=S["tiny_t"][:, 0:1])),
                waits=[em.ev("vector")])
            em.emit("vector", _bld(
                lambda e, gb=gb, tt=tt: e.tensor_tensor(
                    S["rstdc_t"][:, gb * 8 + tt:gb * 8 + tt + 1],
                    S["rstdc_t"][:, gb * 8 + tt:gb * 8 + tt + 1],
                    S["tiny_t"][:, 0:1], ALU.add)), waits=[em.ev("scalar")])
        e1 = em.emit("scalar", _bld(lambda e, gb=gb: e.activation(
            S["rstdc_t"][:, gb * 8:gb * 8 + 4], S["rstdc_t"][:, gb * 8:gb * 8 + 4],
            AF.Sqrt, bias=EPS, scale=1.0 / DM)), waits=[em.ev("vector")])
        ca_ev[g] = em.emit("vector", _bld(lambda e, gb=gb: e.reciprocal(
            S["rstdc_t"][:, gb * 8:gb * 8 + 4], S["rstdc_t"][:, gb * 8:gb * 8 + 4])),
            waits=[e1])

    def emit_rfin(g):
        gb = g % 2
        e3 = em.emit("tensor", _bld(lambda e, gb=gb: e.transpose(
            PB[7][0:4, 0:128], S["rstdc_t"][:, gb * 8:gb * 8 + 4], S["idf_t"][:])),
            waits=[ca_ev[g], ctx["init_ev"], rst_ev.get(g - 1)])
        rst_ev[g] = em.emit("scalar", _bld(lambda e, gb=gb: e.activation(
            S["rsev_t"][0:4, gb * 128:(gb + 1) * 128], PB[7][0:4, 0:128],
            AF.Copy, bias=0.0, scale=1.0)), waits=[e3])
        rrow_st[g] = em.emit("gpsimd", _bld(lambda e, g=g, gb=gb: e.dma_start(
            out=D["rstd_row"].rearrange("o (g2 w) -> (o g2) w", w=128)[4 * g:4 * (g + 1), :],
            in_=S["rsev_t"][0:4, gb * 128:(gb + 1) * 128])),
            waits=[rst_ev[g]], dma=True)
        rbc_ev[g] = em.emit("gpsimd", _bld(
            lambda e, g=g, gb=gb: e.dma_start(
                out=S["rbc_t"][:, gb * TG:(gb + 1) * TG],
                in_=D["rstd_row"][0:1, g * TG:(g + 1) * TG].broadcast_to([128, TG]))),
            waits=[rrow_st[g], hn_ev.get(g - 2)], dma=True)

    emit_xld(0)
    emit_chainA(0)
    emit_rfin(0)
    for g in range(NTG):
        gb = g % 2
        if g + 1 < NTG:
            emit_xld(g + 1)
            emit_chainA(g + 1)

        for kc in range(8):
            xb = XT_B[kc % 2]
            tevs = []
            for tt in range(4):
                w = [xld[(g, tt)]]
                if kc == 0 and tt == 0:
                    w += [pe_hn.get(g - 2), tail_ev.get(g - 1)]
                w.append(hn_ev.get((g, kc - 2)))
                tevs.append(em.emit("tensor", _bld(
                    lambda e, kc=kc, tt=tt, gb=gb, xb=xb: e.transpose(
                        PB[xb][:, tt * 128:(tt + 1) * 128],
                        S["xt_t"][:, (gb * 4 + tt) * DM + kc * 128:(gb * 4 + tt) * DM + (kc + 1) * 128],
                        S["idf_t"][:])), waits=w))
            hn_ev[(g, kc)] = em.emit("vector", _bld(
                lambda e, kc=kc, gb=gb, xb=xb: e.tensor_tensor(
                    hn[gb][:, kc, 3:3 + TG], PB[xb][:, 0:TG],
                    S["rbc_t"][:, gb * TG:(gb + 1) * TG], ALU.mult)),
                waits=[tevs[-1], rbc_ev[g], hz_ev])
        hn_ev[g] = hn_ev[(g, 7)]
        tail_ev[g] = em.emit("vector", _bld(
            lambda e, gb=gb: e.tensor_copy(hn[1 - gb][:, :, 0:3], hn[gb][:, :, TG:TG + 3])),
            waits=[hn_ev[g], pe_hn.get(g - 1)])

        for dti in range(NDT):
            bank = CONV_B[dti % 2]
            w0 = [hn_ev[g], ctx["wtap_ev"], tail_ev.get(g - 1) if g >= 1 else None,
                  xc_ev.get((g, dti - 2)), xc_ev.get((g - 1, dti + 2))]
            first = True
            for k in range(DC):
                for kc in range(8):
                    last = (k == DC - 1 and kc == 7)
                    em.emit("tensor", _bld(
                        lambda e, k=k, kc=kc, dti=dti, gb=gb, bank=bank, first=first, last=last: e.matmul(
                            PB[bank][:, 0:TG],
                            S["wtap_t"][:, (k * 8 + kc) * DQ + dti * 128:(k * 8 + kc) * DQ + (dti + 1) * 128],
                            hn[gb][:, kc, k:k + TG], start=first, stop=last)),
                        waits=w0 if first else [], inc=last)
                    first = False
            mm = em.ev("tensor")
            xc_ev[(g, dti)] = em.emit("scalar", _bld(
                lambda e, dti=dti, bank=bank: e.activation(
                    S["xcg_t"][:, dti * TG:(dti + 1) * TG], PB[bank][:, 0:TG],
                    AF.Silu, bias=S["convb_t"][:, dti:dti + 1], scale=1.0)),
                waits=[mm, xc_st.get((g - 1, dti))])
            xc_st[(g, dti)] = em.emit("gpsimd", _bld(
                lambda e, g=g, dti=dti: e.dma_start(
                    out=D["xc_d"][dti * 128:(dti + 1) * 128, g * TG:(g + 1) * TG],
                    in_=S["xcg_t"][:, dti * TG:(dti + 1) * TG])),
                waits=[xc_ev[(g, dti)]], dma=True)
            ctx.setdefault("xc_done", {})[(g, dti)] = xc_st[(g, dti)]

        if g + 1 < NTG:
            emit_rfin(g + 1)

        for dti in range(NDT):
            bank = Z_B[dti % 2]
            w0 = [hn_ev[g], ctx["winz_ev"],
                  em.ev("scalar") if dti >= 2 else None]
            for kc in range(8):
                em.emit("tensor", _bld(
                    lambda e, kc=kc, dti=dti, gb=gb, bank=bank: e.matmul(
                        PB[bank][:, 0:TG],
                        S["w_inz_t"][:, kc * DQ + dti * 128:kc * DQ + (dti + 1) * 128],
                        hn[gb][:, kc, 3:3 + TG], start=(kc == 0), stop=(kc == 7))),
                    waits=w0 if kc == 0 else [], inc=(kc == 7))
            mm = em.ev("tensor")
            sev = em.emit("scalar", _bld(
                lambda e, bank=bank: e.activation(
                    S["szg_t"][:], PB[bank][:, 0:TG], AF.Silu, bias=0.0, scale=1.0)),
                waits=[mm, sz_st.get((g, dti - 1)), sz_st.get((g - 1, NDT - 1))])
            sz_st[(g, dti)] = em.emit("gpsimd", _bld(
                lambda e, g=g, dti=dti: e.dma_start(
                    out=D["sz_d"][dti * 128:(dti + 1) * 128, g * TG:(g + 1) * TG],
                    in_=S["szg_t"][:])), waits=[sev], dma=True)
            ctx.setdefault("sz_done", {})[(g, dti)] = sz_st[(g, dti)]
        pe_hn[g] = em.ev("tensor")

        w0 = [xc_ev[(g, d2)] for d2 in range(NDT)] + [ev_xp, pp_st.get(g - 2)]
        for dti in range(NDT):
            em.emit("tensor", _bld(
                lambda e, dti=dti: e.matmul(
                    PB[PP_B][0:96, 0:TG], S["w_xp_t"][:, dti * 96:(dti + 1) * 96],
                    S["xcg_t"][:, dti * TG:(dti + 1) * TG],
                    start=(dti == 0), stop=(dti == NDT - 1))),
                waits=w0 if dti == 0 else [], inc=(dti == NDT - 1))
        mm = em.ev("tensor")
        pev = em.emit("scalar", _bld(
            lambda e, gb=gb: e.activation(
                S["projs_t"][0:96, gb * TG:(gb + 1) * TG], PB[PP_B][0:96, 0:TG],
                AF.Copy, bias=0.0, scale=1.0)), waits=[mm, pp_st.get(g - 2)])
        pp_st[g] = em.emit("gpsimd", _bld(
            lambda e, g=g, gb=gb: e.dma_start(
                out=D["ppart"][g // 8, 0:96, (g % 8) * TG:(g % 8 + 1) * TG],
                in_=S["projs_t"][0:96, gb * TG:(gb + 1) * TG])),
            waits=[pev], dma=True)

    if _DBG:
        em.barrier()
    if "xc" in _DBG:
        em.emit("gpsimd", _bld(lambda e: e.dma_start(out=D["dbg_xc"], in_=D["xc_d"])), dma=True)
    if "sz" in _DBG:
        em.emit("gpsimd", _bld(lambda e: e.dma_start(out=D["dbg_sz"], in_=D["sz_d"])), dma=True)
    if "ppart" in _DBG:
        em.emit("gpsimd", _bld(lambda e: e.dma_start(out=D["dbg_ppart"], in_=D["ppart"])), dma=True)
    arh = []
    pbh = []
    for h in range(2):
        ev = em.emit("gpsimd", _bld(lambda e, h=h: e.collective_compute(
            "AllReduce", ALU.add, replica_groups=GROUPS,
            ins=[D["ppart"][h].opt()], outs=[D["pred"][h].opt()])),
            waits=[pp_st[g] for g in range(h * 8, h * 8 + 8)], inc_amt=1)
        arh.append(ev)
        pbh.append(em.emit("gpsimd", _bld(lambda e, h=h: e.dma_start(
            out=D["predb"][h], in_=D["pred"][h, 64:96, :])), waits=[ev], dma=True))
    ctx["arh"] = arh
    ctx["pbh"] = pbh
    ctx["ar_ev"] = arh[1]
    if "pred" in _DBG:
        em.barrier()
        em.emit("gpsimd", _bld(lambda e: e.dma_start(out=D["dbg_pred"].rearrange("r (h w) -> h r w", h=2), in_=D["pred"])), dma=True)
        em.barrier()


def emit_dt_init(ctx):
    D, S, em = ctx["D"], ctx["S"], ctx["em"]
    ctx["dt_w_ev"] = em.emit("sync", _bld(lambda e: e.dma_start(
        out=S["w_dt_t"][0:64, :], in_=D["w_dt"])), waits=[ctx["init_ev"]], dma=True)
    ctx["dt_st"] = {}
    ctx["dt_e5"] = {}
    ctx["dt_ld"] = {}
    ctx["dt_ti"] = [0]
    ctx["dtb_last"] = {}


def emit_dt_group(ctx, g, extra=()):
    D, S, PB, em = ctx["D"], ctx["S"], ctx["PB"], ctx["em"]
    st_ev, e5s, ld = ctx["dt_st"], ctx["dt_e5"], ctx["dt_ld"]
    gb = g % 2
    ld[g] = em.emit("gpsimd", _bld(lambda e, g=g, gb=gb: e.dma_start(
        out=S["dtraw_t"][0:64, gb * TG:(gb + 1) * TG],
        in_=D["pred"][g // 8, 0:64, (g % 8) * TG:(g % 8 + 1) * TG].bitcast(F32R))),
        waits=[st_ev.get((g - 2, NDT - 1)), ctx["arh"][g // 8]], dma=True)
    for dti in range(NDT):
        ti = ctx["dt_ti"][0]
        bank = 2 + ti % 2
        par = ti % 2
        sp = slice(par * TG, (par + 1) * TG)
        em.emit("tensor", _bld(lambda e, dti=dti, gb=gb, bank=bank: e.matmul(
            PB[bank][:, 0:TG], S["w_dt_t"][0:64, dti * 128:(dti + 1) * 128],
            S["dtraw_t"][0:64, gb * TG:(gb + 1) * TG], start=True, stop=True)),
            waits=[ld[g], ctx["dt_w_ev"], st_ev.get((g, dti - 2)), st_ev.get((g - 1, dti + 2)),
                   e5s.get(ti - 2)] + list(extra))
        mm = em.ev("tensor")
        e1 = em.emit("scalar", _bld(lambda e, bank=bank, dti=dti, sp=sp: e.activation(
            S["spb_t"][:, sp], PB[bank][:, 0:TG], AF.Abs,
            bias=S["dtb_t"][:, dti:dti + 1], scale=1.0)),
            waits=[mm, st_ev.get((g, dti - 2)), st_ev.get((g - 1, dti + 2)),
                   e5s.get(ti - 2)])
        e2 = em.emit("scalar", _bld(lambda e, sp=sp: e.activation(
            S["spc_t"][:, sp], S["spb_t"][:, sp], AF.Exp, bias=0.0, scale=-1.0)))
        e3 = em.emit("scalar", _bld(lambda e, sp=sp: e.activation(
            S["spc_t"][:, sp], S["spc_t"][:, sp], AF.Ln, bias=1.0, scale=1.0)))
        e4 = em.emit("scalar", _bld(lambda e, bank=bank, dti=dti: e.activation(
            S["spa_t"][:], PB[bank][:, 0:TG], AF.Relu,
            bias=S["dtb_t"][:, dti:dti + 1], scale=1.0)),
            waits=[e5s.get(ti - 1)])
        e5 = em.emit("vector", _bld(lambda e, sp=sp: e.tensor_tensor(
            S["spb_t"][:, sp], S["spa_t"][:], S["spc_t"][:, sp], ALU.add)),
            waits=[e4])
        e5s[ti] = e5
        st_ev[(g, dti)] = em.emit("gpsimd", _bld(lambda e, g=g, dti=dti, sp=sp: e.dma_start(
            out=D["dt_d"][dti * 128:(dti + 1) * 128, g * TG:(g + 1) * TG],
            in_=S["spb_t"][:, sp])), waits=[e5], dma=True)
        ctx["dt_done"][(g, dti)] = st_ev[(g, dti)]
        ctx["dt_ti"][0] = ti + 1


def emit_scan(ctx):
    D, S, PB, em = ctx["D"], ctx["S"], ctx["PB"], ctx["em"]
    NBC = 4
    NBUF = 4
    NBH = 6
    ld_dt, ld_xc, ld_sz = {}, {}, {}
    bb_ev, cb_ev = {}, {}
    dtx_ev, da_ev, dbx_ev, scan_ev, hc_ev, pe_ev, st_cp = {}, {}, {}, {}, {}, {}, {}
    y_done, yg_dma = {}, {}

    if "dt" in _PHASES:
        for g0 in range(4):
            emit_dt_group(ctx, g0,
                          extra=([ctx["sz_done"][(NTG - 1, NDT - 1)]] if g0 == 0 else []))
        ctx["dtb_last"][0] = ctx["dt_e5"][ctx["dt_ti"][0] - 1]
    for tc in range(NTC):
        cbuf = tc % 2
        if "dt" in _PHASES and tc < NTC - 2:
            emit_dt_group(ctx, 2 * tc + 4, extra=[y_done.get((tc - 1, 1))])
            emit_dt_group(ctx, 2 * tc + 5)
            ctx["dtb_last"][tc] = ctx["dt_e5"][ctx["dt_ti"][0] - 1]
        for dti in range(NDT):
            w = [y_done.get((tc - 2, dti))]
            dtw = [ctx["dt_done"][(g2, dti)] for g2 in range(tc * TC // TG, (tc + 1) * TC // TG)]
            ld_dt[(tc, dti)] = em.emit("sync", _bld(
                lambda e, tc=tc, dti=dti, cbuf=cbuf: e.dma_start(
                    out=S["dt_c"][:, (cbuf * NDT + dti) * TC:(cbuf * NDT + dti + 1) * TC],
                    in_=D["dt_d"][dti * 128:(dti + 1) * 128, tc * TC:(tc + 1) * TC])),
                waits=w + dtw + [scan_ev.get((tc - 2, DS - 1, dti))], dma=True)
            xcw = [ctx["xc_done"][(g2, dti)] for g2 in range(tc * TC // TG, (tc + 1) * TC // TG)]
            szw = [ctx["sz_done"][(g2, dti)] for g2 in range(tc * TC // TG, (tc + 1) * TC // TG)]
            ld_xc[(tc, dti)] = em.emit("sync", _bld(
                lambda e, tc=tc, dti=dti, cbuf=cbuf: e.dma_start(
                    out=S["xcc"][:, (cbuf * NDT + dti) * TC:(cbuf * NDT + dti + 1) * TC],
                    in_=D["xc_d"][dti * 128:(dti + 1) * 128, tc * TC:(tc + 1) * TC])),
                waits=w + xcw + [ctx["ar_ev"]], dma=True)
            ld_sz[(tc, dti)] = em.emit("sync", _bld(
                lambda e, tc=tc, dti=dti, cbuf=cbuf: e.dma_start(
                    out=S["szc"][:, (cbuf * NDT + dti) * TC:(cbuf * NDT + dti + 1) * TC],
                    in_=D["sz_d"][dti * 128:(dti + 1) * 128, tc * TC:(tc + 1) * TC])),
                waits=w + szw + [ctx["ar_ev"]], dma=True)
        for dti in range(NDT):
            dtx_ev[(tc, dti)] = em.emit("vector", _bld(
                lambda e, dti=dti, cbuf=cbuf: e.tensor_tensor(
                    S["dtx_c"][:, (cbuf * NDT + dti) * TC:(cbuf * NDT + dti + 1) * TC],
                    S["dt_c"][:, (cbuf * NDT + dti) * TC:(cbuf * NDT + dti + 1) * TC],
                    S["xcc"][:, (cbuf * NDT + dti) * TC:(cbuf * NDT + dti + 1) * TC],
                    ALU.mult)),
                waits=[ld_dt[(tc, dti)], ld_xc[(tc, dti)],
                       scan_ev.get((tc - 2, DS - 1, dti))])

        for s in range(DS):
            slot = s % NBC
            bb_ev[(tc, s)] = em.emit("scalar", _bld(
                lambda e, tc=tc, s=s, slot=slot: e.dma_start(
                    out=S["bbc"][:, slot * TC:(slot + 1) * TC],
                    in_=D["predb"][tc // 4, s:s + 1, (tc % 4) * TC:(tc % 4 + 1) * TC]
                    .broadcast_to([128, TC]))),
                waits=[dbx_ev.get((tc, s - NBC, NDT - 1)),
                       dbx_ev.get((tc - 1, s - NBC + DS, NDT - 1)), ctx["pbh"][tc // 4]], dma=True)
            cb_ev[(tc, s)] = em.emit("scalar", _bld(
                lambda e, tc=tc, s=s, slot=slot: e.dma_start(
                    out=S["cbc"][:, slot * TC:(slot + 1) * TC],
                    in_=D["predb"][tc // 4, 16 + s:17 + s, (tc % 4) * TC:(tc % 4 + 1) * TC]
                    .broadcast_to([128, TC]))),
                waits=[hc_ev.get((tc, s - NBC, NDT - 1)),
                       hc_ev.get((tc - 1, s - NBC + DS, NDT - 1)), ctx["pbh"][tc // 4]], dma=True)

            for dti in range(NDT):
                i = s * NDT + dti
                ib = i % NBUF
                pi = ((i - NBUF) // NDT, (i - NBUF) % NDT) if i >= NBUF else None
                da_ev[(tc, s, dti)] = em.emit("scalar", _bld(
                    lambda e, s=s, dti=dti, ib=ib, cbuf=cbuf: e.activation(
                        S["da_t"][:, ib * TC:(ib + 1) * TC],
                        S["dt_c"][:, (cbuf * NDT + dti) * TC:(cbuf * NDT + dti + 1) * TC],
                        AF.Exp, bias=0.0,
                        scale=S["acol_t"][:, dti * DS + s:dti * DS + s + 1])),
                    waits=[ld_dt[(tc, dti)], ctx["acol_ev"],
                           scan_ev.get((tc,) + pi) if pi else scan_ev.get((tc - 1, DS - 1, (i - NBUF) % NDT)),
                           ] + ([ctx["ar_ev"]] if tc == 0 and s == 0 else []))
                dbx_ev[(tc, s, dti)] = em.emit("vector", _bld(
                    lambda e, s=s, dti=dti, ib=ib, slot=slot, cbuf=cbuf: e.tensor_tensor(
                        S["dbx_t"][:, ib * TC:(ib + 1) * TC],
                        S["dtx_c"][:, (cbuf * NDT + dti) * TC:(cbuf * NDT + dti + 1) * TC],
                        S["bbc"][:, slot * TC:(slot + 1) * TC], ALU.mult)),
                    waits=[dtx_ev[(tc, dti)], bb_ev[(tc, s)]])

                gi = tc * 64 + i
                ibh = gi % NBH
                p6 = ((gi - NBH) // 64, ((gi - NBH) % 64) // NDT,
                      (gi - NBH) % NDT) if gi >= NBH else None

                def mkscan(e, tc=tc, dti=dti, s=s, ib=ib, ibh=ibh):
                    init = 0.0 if tc == 0 else S["st_t"][:, dti * DS + s:dti * DS + s + 1]
                    return e.tensor_tensor_scan(
                        S["h_t"][:, ibh * TC:(ibh + 1) * TC],
                        S["da_t"][:, ib * TC:(ib + 1) * TC],
                        S["dbx_t"][:, ib * TC:(ib + 1) * TC], init, ALU.mult, ALU.add)
                scan_ev[(tc, s, dti)] = em.emit("vector", mkscan,
                    waits=[da_ev[(tc, s, dti)], st_cp.get((tc - 1, s, dti)),
                           st_cp.get(p6)])
                if tc < NTC - 1:
                    st_cp[(tc, s, dti)] = em.emit("gpsimd", _bld(
                        lambda e, s=s, dti=dti, ibh=ibh: e.tensor_copy(
                            S["st_t"][:, dti * DS + s:dti * DS + s + 1],
                            S["h_t"][:, ibh * TC + TC - 1:ibh * TC + TC])),
                        waits=[scan_ev[(tc, s, dti)]])
                hc_ev[(tc, s, dti)] = em.emit("vector", _bld(
                    lambda e, s=s, dti=dti, ibh=ibh, slot=slot: e.tensor_tensor(
                        S["hc_t"][:, ibh * TC:(ibh + 1) * TC],
                        S["h_t"][:, ibh * TC:(ibh + 1) * TC],
                        S["cbc"][:, slot * TC:(slot + 1) * TC], ALU.mult)),
                    waits=[cb_ev[(tc, s)], pe_ev.get(p6)])
                wpe = [hc_ev[(tc, s, dti)], ctx["init_ev_g"], ctx["init_ev"]]
                if s == 0:
                    wpe.append(y_done.get((tc - 1, dti)))
                    if dti == 1:
                        wpe.append(ctx["dtb_last"].get(tc))
                for j in range(2):
                    em.emit("tensor", _bld(
                        lambda e, dti=dti, ibh=ibh, j=j, s=s: e.matmul(
                            PB[2 * dti + j][:, 0:512], S["idbf_t"][:],
                            S["hc_t"][:, ibh * TC + j * 512:ibh * TC + (j + 1) * 512],
                            start=(s == 0), stop=(s == DS - 1))),
                        waits=wpe if j == 0 else [], inc=(j == 1))
                pe_ev[(tc, s, dti)] = em.ev("tensor")

        for dti in range(NDT):
            w = [pe_ev[(tc, DS - 1, dti)], ld_xc[(tc, dti)], ctx["init_ev"]]
            for j in range(2):
                em.emit("vector", _bld(
                    lambda e, dti=dti, cbuf=cbuf, j=j: e.scalar_tensor_tensor(
                        S["y1_t"][:, j * 512:(j + 1) * 512],
                        S["xcc"][:, (cbuf * NDT + dti) * TC + j * 512:(cbuf * NDT + dti) * TC + (j + 1) * 512],
                        S["dcol_t"][:, dti:dti + 1], PB[2 * dti + j][:, 0:512],
                        ALU.mult, ALU.add)), waits=w if j == 0 else [])
            gi = tc * NDT + dti
            ygs = gi % 2
            pgi = ((gi - 2) // NDT, (gi - 2) % NDT) if gi >= 2 else None
            em.emit("vector", _bld(
                lambda e, dti=dti, cbuf=cbuf, ygs=ygs: e.tensor_tensor(
                    S["yg_t"][:, ygs * TC:(ygs + 1) * TC], S["y1_t"][:],
                    S["szc"][:, (cbuf * NDT + dti) * TC:(cbuf * NDT + dti + 1) * TC],
                    ALU.mult)),
                waits=[ld_sz[(tc, dti)], yg_dma.get(pgi)])
            y_done[(tc, dti)] = em.ev("vector")
            yg_dma[(tc, dti)] = em.emit("sync", _bld(
                lambda e, tc=tc, dti=dti, ygs=ygs: e.dma_start(
                    out=D["ygat_in"][dti * 128:(dti + 1) * 128, tc * TC:(tc + 1) * TC],
                    in_=S["yg_t"][:, ygs * TC:(ygs + 1) * TC])),
                waits=[y_done[(tc, dti)]], dma=True)

    ctx["scan_done"] = [em.ev("vector"), em.ev("scalar"), em.ev("tensor")] + \
        [yg_dma[(NTC - 1, dti)] for dti in range(NDT)]
    if "y" in _DBG:
        em.barrier()
        em.emit("gpsimd", _bld(lambda e: e.dma_start(out=D["dbg_y"], in_=D["ygat_in"])), dma=True)
        em.barrier()
    ctx["ag_ev"] = em.emit("gpsimd", _bld(lambda e: e.collective_compute(
        "AllGather", ALU.bypass, replica_groups=GROUPS,
        ins=[D["ygat_in"].opt()], outs=[D["ygat_all"].opt()])),
        waits=[yg_dma[(tc, dti)] for tc in range(NTC) for dti in range(NDT)], inc_amt=1)
    if "ygall" in _DBG:
        em.barrier()
        em.emit("gpsimd", _bld(lambda e: e.dma_start(out=D["dbg_ygall"], in_=D["ygat_all"])), dma=True)
        em.barrier()


def emit_phase3(ctx):
    D, S, PB, em = ctx["D"], ctx["S"], ctx["PB"], ctx["em"]
    NTT = TQ // 128

    xq_ld = {}
    xq_ev = {}
    for tt in range(NTT):
        bi = tt % 2
        xq_ld[tt] = em.emit("sync", _bld(
            lambda e, tt=tt, bi=bi: e.dma_start(
                out=S["xstage_t"][:, bi * DM:(bi + 1) * DM],
                in_=D["xq"][tt * 128:(tt + 1) * 128, :])),
            waits=[xq_ev.get(tt - 2)], dma=True)
        for kc in range(8):
            xb = 6 + (kc % 2)
            evt = em.emit("tensor", _bld(
                lambda e, tt=tt, kc=kc, bi=bi, xb=xb: e.transpose(
                    PB[xb][:, 0:128],
                    S["xstage_t"][:, bi * DM + kc * 128:bi * DM + (kc + 1) * 128],
                    S["idf_t"][:])),
                waits=[xq_ld[tt], xq_ev.get((tt, kc - 2)), xq_ev.get((tt - 1, kc + 6)), ctx["init_ev"]]
                + (ctx["scan_done"] if tt == 0 and kc < 2 else []))
            xq_ev[(tt, kc)] = em.emit("scalar", _bld(
                lambda e, tt=tt, kc=kc, xb=xb: e.activation(
                    S["xqb_t"][:, kc * TQ + tt * 128:kc * TQ + (tt + 1) * 128],
                    PB[xb][:, 0:128],
                    AF.Copy, bias=0.0, scale=1.0)),
                waits=[evt] + (ctx["scan_done"] if tt == 0 and kc == 0 else []))
        xq_ev[tt] = xq_ev[(tt, 7)]
    xq_all = em.ev("scalar")
    if _P3_STOP == "xq":
        return
    wo_ld = em.emit("sync", _bld(lambda e: e.dma_start(
        out=S["w_out_t"][:], in_=_kpm(D["w_out"]))),
        waits=ctx["scan_done"], dma=True)

    yld = {}
    r1_ev = {}
    sq_ev = {}
    st_mm = {}
    srow_ev = {}
    OP_B = (0, 1)
    SUM_B, SQ_B = 2, 3
    for nch in range(4):
        nb = nch % 2
        for kc in range(16):
            yld[(nch, kc)] = em.emit("gpsimd", _bld(
                lambda e, nch=nch, kc=kc, nb=nb: e.indirect_dma_start(
                    out=S["yr_t"][:, (nb * 16 + kc) * 512:(nb * 16 + kc + 1) * 512],
                    out_offset=None,
                    in_=D["ygat_all"].rearrange("d (blk w) -> (d blk) w", w=512),
                    in_offset=IndirectOffsetOnAxis(
                        S["yidx_t"][:, kc * 4 + nch:kc * 4 + nch + 1], axis=0))),
                waits=[ctx["init_ev"], ctx["ag_ev"], r1_ev.get((nch - 2, 7))], dma=True)
        for mt in range(8):
            bank = OP_B[mt % 2]
            w0 = [yld[(nch, kc)] for kc in range(16)]
            w0 += [wo_ld, r1_ev.get((nch, mt - 2)), r1_ev.get((nch - 1, mt + 6))]
            for kc in range(16):
                em.emit("tensor", _bld(
                    lambda e, mt=mt, kc=kc, nb=nb, bank=bank: e.matmul(
                        PB[bank][:, 0:512],
                        S["w_out_t"][:, kc * DM + mt * 128:kc * DM + (mt + 1) * 128],
                        S["yr_t"][:, (nb * 16 + kc) * 512:(nb * 16 + kc + 1) * 512],
                        start=(kc == 0), stop=(kc == 15))),
                    waits=w0 if kc == 0 else [], inc=(kc == 15))
            mm = em.ev("tensor")
            r1_ev[(nch, mt)] = em.emit("vector", _bld(
                lambda e, mt=mt, nch=nch, bank=bank: e.scalar_tensor_tensor(
                    S["r1_t"][:, mt * TQ + nch * 512:mt * TQ + (nch + 1) * 512],
                    S["xqb_t"][:, mt * TQ + nch * 512:mt * TQ + (nch + 1) * 512],
                    2.0, PB[bank][:, 0:512], ALU.mult, ALU.add)),
                waits=[mm, xq_all])
            sq_ev[(nch, mt)] = em.emit("scalar", _bld(
                lambda e, mt=mt, nch=nch: e.activation(
                    S["sq_t"][:], S["r1_t"][:, mt * TQ + nch * 512:mt * TQ + (nch + 1) * 512],
                    AF.Square, bias=0.0, scale=1.0)),
                waits=[r1_ev[(nch, mt)], st_mm.get((nch, mt - 1)), st_mm.get((nch - 1, 7))])
            em.emit("tensor", _bld(
                lambda e, mt=mt, nch=nch: e.matmul(
                    PB[SUM_B][0:1, 0:512], S["ones_rt"][:, 0:1],
                    S["r1_t"][:, mt * TQ + nch * 512:mt * TQ + (nch + 1) * 512],
                    start=(mt == 0), stop=(mt == 7))),
                waits=[r1_ev[(nch, mt)], srow_ev.get(nch - 1)])
            em.emit("tensor", _bld(
                lambda e, mt=mt: e.matmul(
                    PB[SQ_B][0:1, 0:512], S["onesb_t"][:, 0:1], S["sq_t"][:, 0:512].bitcast(BF16) if False else S["sq_t"][:],
                    start=(mt == 0), stop=(mt == 7))),
                waits=[sq_ev[(nch, mt)]])
            st_mm[(nch, mt)] = em.ev("tensor")
        smm = em.ev("tensor")
        ev1 = em.emit("scalar", _bld(lambda e: e.activation(
            S["srow_t"][0:1, :], PB[SUM_B][0:1, 0:512], AF.Copy, bias=0.0, scale=1.0)),
            waits=[smm])
        ev2 = em.emit("scalar", _bld(lambda e: e.activation(
            S["srow2_t"][0:1, :], PB[SQ_B][0:1, 0:512], AF.Copy, bias=0.0, scale=1.0)))
        sa = em.emit("gpsimd", _bld(lambda e, nch=nch: e.dma_start(
            out=D["srow_d"][0:1, nch * 512:(nch + 1) * 512], in_=S["srow_t"][0:1, :])),
            waits=[ev1], dma=True)
        sb = em.emit("gpsimd", _bld(lambda e, nch=nch: e.dma_start(
            out=D["srow_d"][1:2, nch * 512:(nch + 1) * 512], in_=S["srow2_t"][0:1, :])),
            waits=[ev2], dma=True)
        srow_ev[nch] = [sa, sb]

    if _P3_STOP == "op":
        if "x1" in _DBG:
            em.barrier()
            for mt in range(8):
                em.emit("gpsimd", _bld(lambda e, mt=mt: e.dma_start(
                    out=D["dbg_x1"][mt * 128:(mt + 1) * 128, :],
                    in_=S["r1_t"][:, mt * TQ:(mt + 1) * TQ].bitcast(F32))), dma=True)
        return
    ln_rows(ctx, "srow_d", "mrow_d", [srow_ev[n] for n in range(4)])
    mb_ev = em.emit("gpsimd", _bld(lambda e: e.dma_start(
        out=S["mbc_t"][:], in_=D["mrow_d"][0:1, :].broadcast_to([128, TQ]))),
        waits=[ctx["lnrow_ev"]], dma=True)
    rb_ev = em.emit("gpsimd", _bld(lambda e: e.dma_start(
        out=S["rbc2_t"][:], in_=D["mrow_d"][1:2, :].broadcast_to([128, TQ]))),
        waits=[ctx["lnrow_ev"]], dma=True)
    x1_ev = {}
    for mt in range(8):
        em.emit("vector", _bld(lambda e, mt=mt: e.tensor_tensor(
            S["r1_t"][:, mt * TQ:(mt + 1) * TQ], S["r1_t"][:, mt * TQ:(mt + 1) * TQ],
            S["mbc_t"][:], ALU.subtract)),
            waits=[mb_ev, rb_ev, r1_ev[(3, mt)], sq_ev[(3, mt)], st_mm[(3, 7)]])
        em.emit("vector", _bld(lambda e, mt=mt: e.tensor_tensor(
            S["r1_t"][:, mt * TQ:(mt + 1) * TQ], S["r1_t"][:, mt * TQ:(mt + 1) * TQ],
            S["rbc2_t"][:], ALU.mult)))
        x1_ev[mt] = em.emit("vector", _bld(lambda e, mt=mt: e.tensor_copy(
            S["xqb_t"][:, mt * TQ:(mt + 1) * TQ], S["r1_t"][:, mt * TQ:(mt + 1) * TQ])))
    x1_all = em.ev("vector")
    if "x1" in _DBG:
        for mt in range(8):
            em.emit("gpsimd", _bld(lambda e, mt=mt: e.dma_start(
                out=D["dbg_x1"][mt * 128:(mt + 1) * 128, :],
                in_=S["r1_t"][:, mt * TQ:(mt + 1) * TQ].bitcast(F32))),
                waits=[x1_all], dma=True)
        em.barrier()

    if _P3_STOP == "x1":
        return
    GL_B = 4
    lg_ev = {}
    cmb_tr = {}
    for tt in range(NTT):
        for kc in range(8):
            em.emit("tensor", _bld(
                lambda e, tt=tt, kc=kc: e.matmul(
                    PB[GL_B][0:128, 0:8],
                    S["r1_t"][:, kc * TQ + tt * 128:kc * TQ + (tt + 1) * 128],
                    S["gwt_t"][:, kc * E:(kc + 1) * E],
                    start=(kc == 0), stop=(kc == 7))),
                waits=([x1_all, ctx["init_ev"], lg_ev.get(tt - 1)] if kc == 0 else []),
                inc=(kc == 7))
        mmev = em.ev("tensor")
        lg_ev[tt] = em.emit("scalar", _bld(
            lambda e, tt=tt: e.activation(
                S["sc_t"][:, tt * E:(tt + 1) * E], PB[GL_B][0:128, 0:8],
                AF.Sigmoid, bias=0.0, scale=1.0)), waits=[mmev])
    scv = S["sc_t"][:].rearrange("p (g e) -> p g e", e=E)
    cbv = S["cb_t"][:].rearrange("p (g e) -> p g e", e=E)
    m1b = S["m1_t"][:].rearrange("p g -> p g").unsqueeze(2).broadcast_to([128, NTT, E])
    m2b = S["m2_t"][:].rearrange("p g -> p g").unsqueeze(2).broadcast_to([128, NTT, E])
    em.emit("vector", _bld(lambda e: e.tensor_reduce(
        S["m1_t"][:], scv, mybir.AxisListType.X, ALU.max)),
        waits=[lg_ev[t_] for t_ in range(NTT)])
    em.emit("vector", _bld(lambda e: e.tensor_tensor(
        S["lg_t"][:].rearrange("p (g e) -> p g e", e=E), scv, m1b, ALU.is_ge)))
    em.emit("vector", _bld(lambda e: e.scalar_tensor_tensor(
        cbv, S["lg_t"][:].rearrange("p (g e) -> p g e", e=E), -1e30, scv,
        ALU.mult, ALU.add)))
    em.emit("vector", _bld(lambda e: e.tensor_reduce(
        S["m2_t"][:], cbv, mybir.AxisListType.X, ALU.max)))
    em.emit("vector", _bld(lambda e: e.tensor_tensor(
        cbv, cbv, m2b, ALU.is_ge)))
    em.emit("vector", _bld(lambda e: e.tensor_tensor(
        S["lg_t"][:].rearrange("p (g e) -> p g e", e=E),
        S["lg_t"][:].rearrange("p (g e) -> p g e", e=E), m1b, ALU.mult)))
    em.emit("vector", _bld(lambda e: e.tensor_tensor(
        cbv, cbv, m2b, ALU.mult)))
    em.emit("vector", _bld(lambda e: e.tensor_tensor(
        cbv, cbv, S["lg_t"][:].rearrange("p (g e) -> p g e", e=E), ALU.add)))
    em.emit("vector", _bld(lambda e: e.tensor_tensor(
        S["tiny_t"][:, 0:16], S["m1_t"][:], S["m2_t"][:], ALU.add)))
    em.emit("vector", _bld(lambda e: e.tensor_scalar(
        S["tiny_t"][:, 0:16], S["tiny_t"][:, 0:16], 1e-20, None, ALU.add)))
    em.emit("vector", _bld(lambda e: e.reciprocal(
        S["m1_t"][:], S["tiny_t"][:, 0:16])))
    em.emit("vector", _bld(lambda e: e.tensor_scalar(
        S["m1_t"][:], S["m1_t"][:], SCALE_MOE, None, ALU.mult)))
    cmb_all = em.emit("vector", _bld(lambda e: e.tensor_tensor(
        cbv, cbv, m1b, ALU.mult)))
    for tt in range(NTT):
        evt = em.emit("tensor", _bld(lambda e, tt=tt: e.transpose(
            PB[5][0:8, 0:128], S["cb_t"][:, tt * E:(tt + 1) * E], S["idf_t"][:])),
            waits=[cmb_all, cmb_tr.get(tt - 1)])
        cmb_tr[tt] = em.emit("scalar", _bld(lambda e, tt=tt: e.activation(
            S["cbp_t"][0:8, tt * 128:(tt + 1) * 128], PB[5][0:8, 0:128],
            AF.Copy, bias=0.0, scale=1.0)), waits=[evt])
    cb_dma = em.emit("gpsimd", _bld(lambda e: e.dma_start(
        out=D["combt_d"], in_=S["cbp_t"][0:8, :])), waits=[cmb_tr[NTT - 1]], dma=True)
    if "comb" in _DBG:
        em.emit("gpsimd", _bld(lambda e: e.dma_start(
            out=D["dbg_comb"], in_=D["combt_d"])), waits=[cb_dma], dma=True)
    if "sc" in _DBG:
        em.emit("gpsimd", _bld(lambda e: e.dma_start(
            out=D["dbg_sc"], in_=S["sc_t"][:])), waits=[em.ev("vector")], dma=True)

    if _P3_STOP == "gate":
        return
    GB = (4, 6)
    UB = (5, 7)
    DN_B = (0, 1)
    SUM2_B, SQ2_B = 6, 7
    NCH2 = 4
    W2 = 512

    wg_ld, wu_ld, wd_ld, cbb_ev = {}, {}, {}, {}
    sg_ev, hm_ev, dn_ev = {}, {}, {}
    r2_ev, sq2_ev, s2_mm, srow2_ev = {}, {}, {}, {}
    gu_done = {}
    ffn_guard = [em.ev("scalar"), em.ev("vector")]
    nch_list = [0] if _P3_STOP == "ffn0" else list(range(NCH2))
    for nch in nch_list:
        for up in range(5):
            for ui in range(2):
                u = 2 * up + ui
                ub = u % 2
                wldw = [gu_done.get((nch, u - 2)), gu_done.get((nch - 1, u + NU - 2))]
                if nch == 0 and u == 0:
                    wldw += [x1_all, mb_ev, rb_ev, cb_dma]
                wg_ld[(nch, u)] = em.emit("sync", _bld(
                    lambda e, u=u, ub=ub: e.dma_start(
                        out=S["wg_t"][:, ub * 8 * IEX:(ub + 1) * 8 * IEX],
                        in_=_kpm(D["wg_all"][u]))), waits=wldw, dma=True)
                wu_ld[(nch, u)] = em.emit("sync", _bld(
                    lambda e, u=u, ub=ub: e.dma_start(
                        out=S["wu_t"][:, ub * 8 * IEX:(ub + 1) * 8 * IEX],
                        in_=_kpm(D["wu_all"][u]))), waits=wldw, dma=True)
                wd_ld[(nch, u)] = em.emit("sync", _bld(
                    lambda e, u=u, ub=ub: e.dma_start(
                        out=S["wd_t"][:, ub * 4 * DM:(ub + 1) * 4 * DM],
                        in_=_kpm(D["wd_all"][u]))),
                    waits=[dn_ev.get((nch, up - 1)), dn_ev.get((nch - 1, up + 4))]
                    + ([cb_dma, x1_all] if nch == 0 and up == 0 else []),
                    dma=True)
                if u >= 2:
                    cbb_ev[(nch, u)] = em.emit("gpsimd", _bld(
                        lambda e, u=u, nch=nch, ub=ub: e.dma_start(
                            out=S["cbbc_t"][:, ub * W2:(ub + 1) * W2],
                            in_=D["combt_d"][u - 2:u - 1, nch * W2:(nch + 1) * W2]
                            .broadcast_to([128, W2]))),
                        waits=[cb_dma, hm_ev.get((nch, u - 2, 3))], dma=True)
                for it in range(4):
                    gb_ = GB[it % 2]
                    ub_ = UB[it % 2]
                    w0 = [wg_ld[(nch, u)], x1_all, sg_ev.get((nch, u, it - 2)),
                          sg_ev.get((nch, u - 1, 2 + it % 2)),
                          sg_ev.get((nch - 1, NU - 1, 2 + it % 2)),
                          srow2_ev.get(nch - 1) if u == 0 else None]
                    if nch == 0 and u == 0 and it == 0:
                        w0 += ffn_guard
                    for kc in range(8):
                        em.emit("tensor", _bld(
                            lambda e, it=it, kc=kc, ub=ub, nch=nch, gb_=gb_: e.matmul(
                                PB[gb_][:, 0:W2],
                                S["wg_t"][:, ub * 8 * IEX + kc * IEX + it * 128:ub * 8 * IEX + kc * IEX + (it + 1) * 128],
                                S["xqb_t"][:, kc * TQ + nch * W2:kc * TQ + (nch + 1) * W2],
                                start=(kc == 0), stop=(kc == 7))),
                            waits=w0 if kc == 0 else [], inc=(kc == 7))
                    gmm = em.ev("tensor")
                    sg_ev[(nch, u, it)] = em.emit("scalar", _bld(
                        lambda e, it=it, gb_=gb_: e.activation(
                            S["sg_t"][:, it * W2:(it + 1) * W2], PB[gb_][:, 0:W2],
                            AF.Silu, bias=0.0, scale=1.0)), waits=[gmm])
                    w0u = [wu_ld[(nch, u)], hm_ev.get((nch, u - 1, 2 + it % 2)),
                           hm_ev.get((nch, u, it - 2)),
                           srow2_ev.get(nch - 1) if u == 0 else None]
                    if nch == 0 and u == 0 and it == 0:
                        w0u = w0u + ffn_guard
                    if nch > 0 and u <= 1 and it == 0:
                        w0u.append(hm_ev.get((nch - 1, u + 8, 3)))
                    for kc in range(8):
                        em.emit("tensor", _bld(
                            lambda e, it=it, kc=kc, ub=ub, nch=nch, ub_=ub_: e.matmul(
                                PB[ub_][:, 0:W2],
                                S["wu_t"][:, ub * 8 * IEX + kc * IEX + it * 128:ub * 8 * IEX + kc * IEX + (it + 1) * 128],
                                S["xqb_t"][:, kc * TQ + nch * W2:kc * TQ + (nch + 1) * W2],
                                start=(kc == 0), stop=(kc == 7))),
                            waits=w0u if kc == 0 else [], inc=(kc == 7))
                    umm = em.ev("tensor")
                    ev = em.emit("vector", _bld(
                        lambda e, it=it, ub=ub, ub_=ub_: e.tensor_tensor(
                            S["hm_t"][:, (ub * 4 + it) * W2:(ub * 4 + it + 1) * W2],
                            S["sg_t"][:, it * W2:(it + 1) * W2], PB[ub_][:, 0:W2],
                            ALU.mult)),
                        waits=[umm, sg_ev[(nch, u, it)], dn_ev.get((nch, up - 1))])
                    if u >= 2:
                        ev = em.emit("vector", _bld(
                            lambda e, it=it, ub=ub: e.tensor_tensor(
                                S["hm_t"][:, (ub * 4 + it) * W2:(ub * 4 + it + 1) * W2],
                                S["hm_t"][:, (ub * 4 + it) * W2:(ub * 4 + it + 1) * W2],
                                S["cbbc_t"][:, ub * W2:(ub + 1) * W2], ALU.mult)),
                            waits=[cbb_ev[(nch, u)]])
                    hm_ev[(nch, u, it)] = ev
                gu_done[(nch, u)] = em.ev("tensor")
            hm0 = hm_ev[(nch, 2 * up, 3)]
            hm1 = hm_ev[(nch, 2 * up + 1, 3)]
            for mt in range(8):
                bank = DN_B[mt % 2]
                w0 = [wd_ld[(nch, 2 * up)], wd_ld[(nch, 2 * up + 1)], hm0, hm1,
                      r2_ev.get((nch, up, mt - 2)), r2_ev.get((nch, up - 1, mt + 6)),
                      r2_ev.get((nch - 1, 4, mt + 6))]
                if nch == 0 and up == 0 and mt <= 1:
                    w0 += ffn_guard
                first = True
                for ui in range(2):
                    u = 2 * up + ui
                    ub = u % 2
                    for kc in range(4):
                        em.emit("tensor", _bld(
                            lambda e, mt=mt, kc=kc, ub=ub, ui=ui, bank=bank: e.matmul(
                                PB[bank][:, 0:W2],
                                S["wd_t"][:, ub * 4 * DM + kc * DM + mt * 128:ub * 4 * DM + kc * DM + (mt + 1) * 128],
                                S["hm_t"][:, (ub * 4 + kc) * W2:(ub * 4 + kc + 1) * W2],
                                start=(ui == 0 and kc == 0), stop=(ui == 1 and kc == 3))),
                            waits=w0 if first else [], inc=(ui == 1 and kc == 3))
                        first = False
                dmm = em.ev("tensor")
                r2_ev[(nch, up, mt)] = em.emit("vector", _bld(
                    lambda e, mt=mt, nch=nch, bank=bank: e.tensor_tensor(
                        S["r1_t"][:, mt * TQ + nch * W2:mt * TQ + (nch + 1) * W2],
                        S["r1_t"][:, mt * TQ + nch * W2:mt * TQ + (nch + 1) * W2],
                        PB[bank][:, 0:W2], ALU.add)), waits=[dmm])
            dn_ev[(nch, up)] = em.ev("vector")
        for mt in range(8):
            r2_ev[(nch, mt)] = r2_ev[(nch, 4, mt)]
            sq2_ev[(nch, mt)] = em.emit("scalar", _bld(
                lambda e, mt=mt, nch=nch: e.activation(
                    S["sq_t"][:, 0:W2],
                    S["r1_t"][:, mt * TQ + nch * W2:mt * TQ + (nch + 1) * W2],
                    AF.Square, bias=0.0, scale=1.0)),
                waits=[r2_ev[(nch, mt)], s2_mm.get((nch, mt - 1)), s2_mm.get((nch - 1, 7))])
            em.emit("tensor", _bld(
                lambda e, mt=mt, nch=nch: e.matmul(
                    PB[SUM2_B][0:1, 0:W2], S["ones_rt"][:, 0:1],
                    S["r1_t"][:, mt * TQ + nch * W2:mt * TQ + (nch + 1) * W2],
                    start=(mt == 0), stop=(mt == 7))),
                waits=[r2_ev[(nch, mt)], srow2_ev.get(nch - 1)] + (ffn_guard if nch == 0 and mt == 0 else []))
            em.emit("tensor", _bld(
                lambda e, mt=mt: e.matmul(
                    PB[SQ2_B][0:1, 0:W2], S["onesb_t"][:, 0:1], S["sq_t"][:, 0:W2],
                    start=(mt == 0), stop=(mt == 7))),
                waits=[sq2_ev[(nch, mt)]] + (ffn_guard if nch == 0 and mt == 0 else []))
            s2_mm[(nch, mt)] = em.ev("tensor")
        smm = em.ev("tensor")
        ev1 = em.emit("scalar", _bld(lambda e: e.activation(
            S["srow_t"][0:1, 0:W2], PB[SUM2_B][0:1, 0:W2], AF.Copy, bias=0.0, scale=1.0)),
            waits=[smm])
        ev2 = em.emit("scalar", _bld(lambda e: e.activation(
            S["srow2_t"][0:1, 0:W2], PB[SQ2_B][0:1, 0:W2], AF.Copy, bias=0.0, scale=1.0)))
        sa = em.emit("gpsimd", _bld(lambda e, nch=nch: e.dma_start(
            out=D["srow_d"][0:1, nch * W2:(nch + 1) * W2], in_=S["srow_t"][0:1, 0:W2])),
            waits=[ev1], dma=True)
        sb = em.emit("gpsimd", _bld(lambda e, nch=nch: e.dma_start(
            out=D["srow_d"][1:2, nch * W2:(nch + 1) * W2], in_=S["srow2_t"][0:1, 0:W2])),
            waits=[ev2], dma=True)
        srow2_ev[nch] = [sa, sb]

    if _P3_STOP in ("ffn0", "ffnall"):
        return
    ln_rows(ctx, "srow_d", "mrow_d", [srow2_ev[n] for n in range(NCH2)])
    mb2 = em.emit("gpsimd", _bld(lambda e: e.dma_start(
        out=S["mbc_t"][:], in_=D["mrow_d"][0:1, :].broadcast_to([128, TQ]))),
        waits=[ctx["lnrow_ev"]], dma=True)
    rb2 = em.emit("gpsimd", _bld(lambda e: e.dma_start(
        out=S["rbc2_t"][:], in_=D["mrow_d"][1:2, :].broadcast_to([128, TQ]))),
        waits=[ctx["lnrow_ev"]], dma=True)
    if _P3_STOP == "ln2":
        return
    out_st = {}
    xo_ev = {}
    tr_ev = {}
    for tt in range(NTT):
        ob = 0
        for mt in range(8):
            w = [mb2, rb2, r2_ev[(NCH2 - 1, mt)], s2_mm[(NCH2 - 1, 7)], tr_ev.get((tt - 1, mt))]
            if mt == 0:
                w.append(out_st.get(tt - 2))
            e1 = em.emit("vector", _bld(lambda e, tt=tt, mt=mt: e.tensor_tensor(
                S["x2_t"][:, mt * 128:(mt + 1) * 128],
                S["r1_t"][:, mt * TQ + tt * 128:mt * TQ + (tt + 1) * 128],
                S["mbc_t"][:, tt * 128:(tt + 1) * 128], ALU.subtract)), waits=w)
            e2 = em.emit("vector", _bld(lambda e, tt=tt, mt=mt: e.tensor_tensor(
                S["x2_t"][:, mt * 128:(mt + 1) * 128],
                S["x2_t"][:, mt * 128:(mt + 1) * 128],
                S["rbc2_t"][:, tt * 128:(tt + 1) * 128], ALU.mult)))
            evt = em.emit("tensor", _bld(lambda e, mt=mt: e.transpose(
                PB[4 + (mt % 2)][:, (mt // 2) * 128:(mt // 2 + 1) * 128],
                S["x2_t"][:, mt * 128:(mt + 1) * 128], S["idf_t"][:])),
                waits=[e2, xo_ev.get((tt - 1, 6 + (mt % 2))), xo_ev.get((tt, mt - 2))])
            tr_ev[(tt, mt)] = evt
            xo_ev[(tt, mt)] = em.emit("scalar", _bld(lambda e, mt=mt, ob=ob: e.activation(
                S["xo_t"][:, ob * DM + mt * 128:ob * DM + (mt + 1) * 128],
                PB[4 + (mt % 2)][:, (mt // 2) * 128:(mt // 2 + 1) * 128],
                AF.Copy, bias=0.0, scale=1.0)), waits=[evt, out_st.get(tt - 1)])
        oev = em.ev("scalar")
        out_st[tt] = em.emit("gpsimd", _bld(lambda e, tt=tt, ob=ob: e.dma_start(
            out=D["out"][tt * 128:(tt + 1) * 128, :], in_=S["xo_t"][:, ob * DM:(ob + 1) * DM])),
            waits=[oev], dma=True)


def ln_rows(ctx, src, dst, waits):
    D, S, em = ctx["D"], ctx["S"], ctx["em"]
    l1 = em.emit("sync", _bld(lambda e: e.dma_start(
        out=S["m16_t"][:, 0:16], in_=D[src][0:1, :].rearrange("o (p w) -> (o p) w", p=128))),
        waits=waits, dma=True)
    l2 = em.emit("sync", _bld(lambda e: e.dma_start(
        out=S["m16_t"][:, 16:32], in_=D[src][1:2, :].rearrange("o (p w) -> (o p) w", p=128))),
        waits=waits, dma=True)
    e1 = em.emit("vector", _bld(lambda e: e.tensor_scalar(
        S["m16_t"][:, 0:16], S["m16_t"][:, 0:16], 1.0 / DM, None, ALU.mult)),
        waits=[l1])
    e2 = em.emit("vector", _bld(lambda e: e.tensor_tensor(
        S["m16_t"][:, 32:48], S["m16_t"][:, 0:16], S["m16_t"][:, 0:16], ALU.mult)),
        waits=[l2])
    e3 = em.emit("vector", _bld(lambda e: e.scalar_tensor_tensor(
        S["m16_t"][:, 48:64], S["m16_t"][:, 16:32], 1.0 / DM, S["m16_t"][:, 32:48],
        ALU.mult, ALU.subtract)))
    e4 = em.emit("scalar", _bld(lambda e: e.activation(
        S["m16_t"][:, 48:64], S["m16_t"][:, 48:64], AF.Sqrt, bias=EPS, scale=1.0)),
        waits=[e3])
    e5 = em.emit("vector", _bld(lambda e: e.reciprocal(
        S["m16_t"][:, 48:64], S["m16_t"][:, 48:64])), waits=[e4])
    s1 = em.emit("gpsimd", _bld(lambda e: e.dma_start(
        out=D[dst][0:1, :].rearrange("o (p w) -> (o p) w", p=128), in_=S["m16_t"][:, 0:16])),
        waits=[e1], dma=True)
    s2 = em.emit("gpsimd", _bld(lambda e: e.dma_start(
        out=D[dst][1:2, :].rearrange("o (p w) -> (o p) w", p=128), in_=S["m16_t"][:, 48:64])),
        waits=[e5], dma=True)
    ctx["lnrow_ev"] = s2
    em.wait_only("gpsimd", [s1])


def _prep_core_inputs(c, I):
    b, q = c // 4, c % 4
    sl = slice(q * DQ, (q + 1) * DQ)
    f = np.float32
    bf = ml_dtypes.bfloat16
    cont = np.ascontiguousarray

    d = {}
    xb = np.asarray(I["L0_attn_hidden_states"][b], dtype=f)
    d["x"] = cont(xb)
    d["xq"] = cont(xb[q * TQ:(q + 1) * TQ])
    ipw = np.asarray(I["L0_attn_in_proj_w"], dtype=f)
    d["w_inx"] = cont(ipw[sl].T)
    d["w_inz"] = cont(ipw[2048 + q * DQ:2048 + (q + 1) * DQ].T)
    d["convw"] = cont(np.asarray(I["L0_attn_conv1d_w"], dtype=f)[sl].T)
    d["convb"] = cont(np.asarray(I["L0_attn_conv1d_b"], dtype=f)[sl].reshape(NDT, 128).T)
    d["w_xp"] = cont(np.asarray(I["L0_attn_x_proj_w"], dtype=f)[:, sl].T.astype(bf))
    d["w_dt"] = cont(np.asarray(I["L0_attn_dt_proj_w"], dtype=f)[sl].T)
    d["dtb"] = cont(np.asarray(I["L0_attn_dt_proj_b"], dtype=f)[sl].reshape(NDT, 128).T)
    d["acol"] = cont(np.asarray(I["L0_attn_A_log"], dtype=f)[sl]
                     .reshape(NDT, 128, DS).transpose(1, 0, 2).reshape(128, NDT * DS))
    d["dcol"] = cont(np.asarray(I["L0_attn_D"], dtype=f)[sl].reshape(NDT, 128).T)
    d["nrmw"] = cont(np.asarray(I["L0_attn_norm_weight"], dtype=f).reshape(-1, 128).T)
    d["w_out"] = cont(np.asarray(I["L0_attn_out_proj_w"], dtype=f).T.astype(bf))
    d["w_gate"] = cont(np.asarray(I["L0_ffn_gate_weight"], dtype=f).T)
    wg = np.asarray(I["L0_ffn_w_gate"], dtype=f)
    wu = np.asarray(I["L0_ffn_w_up"], dtype=f)
    wd = np.asarray(I["L0_ffn_w_down"], dtype=f)
    wsg = np.asarray(I["L0_ffn_ws_gate"], dtype=f)
    wsu = np.asarray(I["L0_ffn_ws_up"], dtype=f)
    wsd = np.asarray(I["L0_ffn_ws_down"], dtype=f)
    g_all = np.concatenate([wsg.reshape(2, IEX, DM), wg], axis=0)
    u_all = np.concatenate([wsu.reshape(2, IEX, DM), wu], axis=0)
    dn_all = np.concatenate([wsd.T.reshape(2, IEX, DM), wd.transpose(0, 2, 1)], axis=0)
    d["wg_all"] = cont(g_all.transpose(0, 2, 1).astype(bf))
    d["wu_all"] = cont(u_all.transpose(0, 2, 1).astype(bf))
    d["wd_all"] = cont(dn_all.astype(bf))
    d["idbf"] = np.eye(128, dtype=bf)
    d["idf"] = np.eye(128, dtype=f)
    d["ones_r"] = np.ones((128, 8), dtype=f)
    cvals = np.zeros(8, dtype=f)
    cvals[:4] = [EPS, 1.0, -1.0, 1.0 / DM]
    d["consts"] = np.broadcast_to(cvals, (128, 8)).copy()
    yidx = np.empty((128, 64), dtype=np.int32)
    for kc in range(16):
        for nch in range(4):
            yidx[:, kc * 4 + nch] = (kc * 128 + np.arange(128)) * (T // 512) + q * 4 + nch
    d["yidx"] = yidx
    return d


def kernel(**inputs):
    nc, D = build_program()
    in_maps = [_prep_core_inputs(c, inputs) for c in range(NCORE)]
    res = run_bass_kernel_spmd(nc, in_maps, core_ids=list(range(NCORE)))
    out = np.empty((B, T, DM), dtype=np.float32)
    for c in range(NCORE):
        b, q = c // 4, c % 4
        out[b, q * TQ:(q + 1) * TQ] = res.results[c]["out"]
    return out


# version 13 (final)
# speedup vs baseline: 1.0728x; 1.0055x over previous
import numpy as np
import ml_dtypes
import concourse.bass as bass
import concourse.mybir as mybir
from concourse.bass import IndirectOffsetOnAxis
from concourse.bass_utils import run_bass_kernel_spmd

F32 = mybir.dt.float32
F32R = mybir.dt.float32r
BF16 = mybir.dt.bfloat16
I32 = mybir.dt.int32
AF = mybir.ActivationFunctionType
ALU = mybir.AluOpType

B, T, DM = 2, 8192, 1024
DI, DS, DTR, DC = 2048, 16, 64, 4
EPS = 1e-5
NCORE = 8
DQ = DI // 4
NDT = DQ // 128
TQ = T // 4
E = 8
IEX = 512
NU = 10
SCALE_MOE = 2.5
TC = 1024
NTC = T // TC
TG = 512
NTG = T // TG
GROUPS = [[0, 1, 2, 3], [4, 5, 6, 7]]

_DBG = []
_PHASES = ("p1", "dt", "scan", "p3")
_P3_STOP = None

ENGINES = ("sync", "gpsimd", "scalar", "vector", "tensor")


class Em:

    NCH = 4

    def __init__(self, nc):
        self.nc = nc
        self.streams = {e: [] for e in ENGINES}
        self.sems = {}
        self.counts = {}
        for e in ENGINES:
            cm = nc.semaphore(f"S_{e}_c")
            self.sems[(e, "c")] = cm.__enter__()
            self.counts[(e, "c")] = 0
        for e in ("sync", "gpsimd", "scalar"):
            for i in range(self.NCH):
                cm = nc.semaphore(f"S_{e}_q{i}")
                self.sems[(e, f"q{i}")] = cm.__enter__()
                self.counts[(e, f"q{i}")] = 0
        self.rr = {e: 0 for e in ENGINES}
        self.waited = {e: {} for e in ENGINES}

    def ev(self, engine, kind="c"):
        return (engine, kind, self.counts[(engine, kind)])

    def qsnap(self, engine):
        return [(engine, f"q{i}", self.counts[(engine, f"q{i}")])
                for i in range(self.NCH)]

    def _wl(self, engine, kind, waits):
        wl = []
        flat = []
        for w in waits:
            if w is None:
                continue
            if isinstance(w, list):
                flat.extend(x for x in w if x is not None)
            else:
                flat.append(w)
        for w in flat:
            (pe_, pk, val) = w
            if pe_ == engine and pk == "c" and kind == "c" and engine == "tensor":
                continue
            key = (pe_, pk)
            if val <= self.waited[engine].get(key, 0):
                continue
            self.waited[engine][key] = val
            wl.append((self.sems[key], val))
        return wl

    def emit(self, engine, build, waits=(), dma=False, inc=True, inc_amt=None):
        if dma:
            ci = self.rr[engine] % self.NCH
            self.rr[engine] += 1
            kind = f"q{ci}"
            amt = 16
            prev = self.counts[(engine, kind)]
            wl = self._wl(engine, kind, waits)
            key = (engine, kind)
            if prev > self.waited[engine].get(key, 0):
                self.waited[engine][key] = prev
                wl.insert(0, (self.sems[key], prev))
            self.counts[key] = prev + amt
            tgt = prev + amt
            sem = self.sems[key]
        else:
            kind = "c"
            amt = inc_amt if inc_amt is not None else 1
            wl = self._wl(engine, kind, waits)
            if engine in ("vector", "scalar", "gpsimd"):
                key = (engine, "c")
                prev = self.counts[key]
                if prev > self.waited[engine].get(key, 0):
                    self.waited[engine][key] = prev
                    wl.insert(0, (self.sems[key], prev))
            if inc:
                self.counts[(engine, kind)] += amt
            tgt = self.counts[(engine, kind)]
            sem = self.sems[(engine, kind)]

        def run(eng, wl=wl, build=build, sem=sem, amt=amt, inc=inc):
            for (h, v) in wl:
                eng.wait_ge(h, v)
            ins = build(eng)
            if inc and ins is not None:
                ins.then_inc(sem, amt)
            return ins

        self.streams[engine].append(run)
        return (engine, kind, tgt)

    def wait_only(self, engine, waits):
        wl = self._wl(engine, "c", [w for w in waits if w is not None])
        if not wl:
            return

        def run(eng, wl=wl):
            for (h, v) in wl:
                eng.wait_ge(h, v)

        self.streams[engine].append(run)

    def barrier(self):
        evs = [(e, k, self.counts[(e, k)]) for (e, k) in self.counts]
        for e in ENGINES:
            self.wait_only(e, evs)


def _bld(fn, *a, **k):
    return lambda eng: fn(eng, *a, **k)


def _kpm(ap, p=128):
    return ap.rearrange("(kc p) m -> kc p m", p=p).transpose([1, 0, 2])


def build_program():
    nc = bass.Bass("TRN2", target_bir_lowering=False, debug=False)
    D, S = {}, {}

    def din(name, shape, dt=F32):
        D[name] = nc.dram_tensor(name, shape, dt, kind="ExternalInput").ap()

    def dout(name, shape, dt=F32):
        D[name] = nc.dram_tensor(name, shape, dt, kind="ExternalOutput").ap()

    def dint(name, shape, dt=F32):
        D[name] = nc.dram_tensor(name, shape, dt).ap()

    din("x", [T, DM])
    din("xq", [TQ, DM])
    din("w_inx", [DM, DQ])
    din("w_inz", [DM, DQ])
    din("convw", [DC, DQ])
    din("convb", [128, NDT])
    din("w_xp", [DQ, 96], BF16)
    din("w_dt", [DTR, DQ], F32R)
    din("dtb", [128, NDT])
    din("acol", [128, NDT * DS])
    din("dcol", [128, NDT])
    din("nrmw", [128, DM // 128])
    din("w_out", [DI, DM], BF16)
    din("w_gate", [DM, E], F32R)
    din("wg_all", [NU, DM, IEX], BF16)
    din("wu_all", [NU, DM, IEX], BF16)
    din("wd_all", [NU, IEX, DM], BF16)
    din("idbf", [128, 128], BF16)
    din("idf", [128, 128])
    din("ones_r", [128, 8], F32R)
    din("yidx", [128, 64], I32)
    din("consts", [128, 8])
    dout("out", [TQ, DM])

    dint("rstd_row", [1, T])
    dint("ppart", [2, 96, T // 2])
    dint("pred", [2, 96, T // 2])
    dint("dt_d", [DQ, T])
    dint("xc_d", [DQ, T], BF16)
    dint("sz_d", [DQ, T], BF16)
    dint("predb", [2, 32, T // 2], BF16)
    dint("ygat_in", [DQ, T], BF16)
    dint("ygat_all", [DI, T], BF16)
    dint("srow_d", [2, TQ])
    dint("mrow_d", [2, TQ])
    dint("combt_d", [E, TQ])

    dbg_shapes = {
        "xc": ([DQ, T], BF16), "sz": ([DQ, T], BF16), "ppart": ([96, T], F32),
        "pred": ([96, T], F32), "dt": ([DQ, T], F32), "y": ([DQ, T], BF16),
        "ygall": ([DI, T], BF16), "r1": ([DM, TQ], F32), "x1": ([DM, TQ], F32),
        "comb": ([E, TQ], F32), "sc": ([128, 16 * E], F32),
    }
    for nm in _DBG:
        sh, dt_ = dbg_shapes[nm]
        dout("dbg_" + nm, sh, dt_)

    SB_BASE = 16512
    SB_END = 229376
    cur = [SB_BASE]

    def salloc(name, shape, dt=F32, at=None):
        szb = {F32: 4, F32R: 4, BF16: 2, I32: 4}[dt]
        nbytes = int(np.prod(shape[1:])) * szb
        off = ((cur[0] if at is None else at) + 63) & ~63
        if at is None:
            cur[0] = off + nbytes
        assert off + nbytes <= SB_END, (name, off, nbytes)
        S[name] = nc.alloc_sbuf_tensor_at(name, shape, dt, offset=off)
        return off + nbytes

    for nm, sh, dt_ in [
        ("convb_t", [128, NDT], F32), ("dtb_t", [128, NDT], F32),
        ("acol_t", [128, NDT * DS], F32), ("dcol_t", [128, NDT], F32),
        ("nrmw_t", [128, DM // 128], F32), ("idbf_t", [128, 128], BF16),
        ("idf_t", [128, 128], F32), ("ones_rt", [128, 8], F32R),
        ("onesb_t", [128, 8], BF16), ("st_t", [128, NDT * DS], F32),
        ("yidx_t", [128, 64], I32),
        ("spa_t", [128, TG], F32), ("spb_t", [128, 2 * TG], F32),
        ("spc_t", [128, 2 * TG], F32), ("sq_t", [128, TG], BF16),
        ("srow_t", [1, TG], F32), ("srow2_t", [1, TG], F32),
        ("m16_t", [128, 64], F32),
        ("tiny_t", [128, 64], F32),
        ("gwt_t", [128, 8 * E], F32R), ("xstage_t", [128, 2 * DM], F32),
        ("w_dt_t", [64, DQ], F32R), ("dtraw_t", [64, 2 * TG], F32R),
        ("m1_t", [128, 16], F32), ("m2_t", [128, 16], F32),
        ("msk_t", [128, 8], F32), ("msk2_t", [128, 8], F32),
        ("lg_t", [128, 16 * E], F32), ("sc_t", [128, 16 * E], F32),
        ("cb_t", [128, 16 * E], F32), ("cbp_t", [E, TQ], F32),
    ]:
        salloc(nm, sh, dt_)
    salloc("consts_t", [128, 8], F32)
    nc.const_aps.register(F32, EPS, S["consts_t"][:, 0:1])
    nc.const_aps.register(F32, 1.0, S["consts_t"][:, 1:2])
    nc.const_aps.register(F32, -1.0, S["consts_t"][:, 2:3])
    nc.const_aps.register(F32, 1.0 / DM, S["consts_t"][:, 3:4])
    arena = (cur[0] + 63) & ~63

    a = arena
    a = salloc("wtap_t", [128, DC * 8 * DQ], BF16, at=a)
    a = salloc("w_inz_t", [128, 8 * DQ], BF16, at=a)
    a = salloc("cwbc_t", [128, DC * DQ], F32, at=a)
    a = salloc("wtmp_t", [128, DQ], F32, at=a)
    a = salloc("w_xp_t", [128, NDT * 96], BF16, at=a)
    a = salloc("xt_t", [128, 2 * 4 * DM], F32, at=a)
    a = salloc("hn_t", [128, 2 * 8 * (TG + 3)], BF16, at=a)
    a = salloc("rbc_t", [128, 2 * TG], F32, at=a)
    a = salloc("xcg_t", [128, NDT * TG], BF16, at=a)
    a = salloc("szg_t", [128, TG], BF16, at=a)
    a = salloc("rstdc_t", [128, 64], F32, at=a)
    a = salloc("rsev_t", [4, 2 * 128], F32, at=a)
    a = salloc("projs_t", [96, 2 * TG], F32, at=a)
    assert a <= SB_END


    a = arena
    a = salloc("dt_c", [128, 2 * NDT * TC], F32, at=a)
    a = salloc("xcc", [128, 2 * NDT * TC], BF16, at=a)
    a = salloc("szc", [128, 2 * NDT * TC], BF16, at=a)
    a = salloc("dtx_c", [128, NDT * TC], BF16, at=a)
    a = salloc("bbc", [128, 8 * TC], BF16, at=a)
    a = salloc("cbc", [128, 8 * TC], BF16, at=a)
    a = salloc("da_t", [128, 4 * TC], F32, at=a)
    a = salloc("dbx_t", [128, 4 * TC], BF16, at=a)
    a = salloc("h_t", [128, 4 * TC], BF16, at=a)
    a = salloc("hc_t", [128, 4 * TC], BF16, at=a)
    a = salloc("y1_t", [128, TC], F32, at=a)
    a = salloc("yg_t", [128, 2 * TC], BF16, at=a)
    assert a <= SB_END

    a = arena
    a = salloc("xqb_t", [128, 8 * TQ], BF16, at=a)
    XQ_OFF = arena
    a = salloc("r1_t", [128, 8 * TQ], F32R, at=a)
    R1_OFF = a - 8 * TQ * 4
    a3 = a
    a = salloc("w_out_t", [128, 16 * DM], BF16, at=a3)
    YR_OFF = a
    a = salloc("yr_t", [128, 2 * 16 * 512], BF16, at=a)
    assert a <= SB_END
    a = a3
    WG_OFF = a
    a = salloc("wg_t", [128, 2 * 8 * IEX], BF16, at=a)
    a = salloc("wu_t", [128, 2 * 8 * IEX], BF16, at=a)
    a = salloc("wd_t", [128, 2 * 4 * DM], BF16, at=a)
    a = salloc("sg_t", [128, 4 * 512], BF16, at=a)
    a = salloc("hm_t", [128, 2 * 4 * 512], BF16, at=a)
    a = salloc("cbbc_t", [128, 2 * 512], BF16, at=a)
    a = salloc("x2_t", [128, 8 * 128], F32, at=a)
    a = salloc("xo_t", [128, DM], F32, at=a)
    assert a <= SB_END, ("ffn arena", a)
    salloc("mbc_t", [128, TQ], F32, at=WG_OFF)
    salloc("rbc2_t", [128, TQ], F32, at=WG_OFF + TQ * 4)
    salloc("cbp_t", [E, TQ], F32, at=YR_OFF)

    x1b = S["xqb_t"]

    PB = [nc.alloc_psum_tensor(f"pb{i}", [128, 512], F32) for i in range(8)]

    em = Em(nc)
    ctx = dict(nc=nc, D=D, S=S, PB=PB, em=em)

    emit_init(ctx)
    if "p1" in _PHASES:
        emit_phase1(ctx)
    if "dt" in _PHASES:
        ctx["dt_done"] = {}
        emit_dt_init(ctx)
    if "scan" in _PHASES:
        emit_scan(ctx)
    if "p3" in _PHASES:
        emit_phase3(ctx)
    em.barrier()

    with nc.Block() as block:
        @block.sync
        def _(eng):
            for f in em.streams["sync"]:
                f(eng)

        @block.gpsimd
        def _(eng):
            for f in em.streams["gpsimd"]:
                f(eng)

        @block.scalar
        def _(eng):
            for f in em.streams["scalar"]:
                f(eng)

        @block.vector
        def _(eng):
            for f in em.streams["vector"]:
                f(eng)

        @block.tensor
        def _(eng):
            for f in em.streams["tensor"]:
                f(eng)

    return nc, D


def emit_init(ctx):
    D, S, em = ctx["D"], ctx["S"], ctx["em"]
    for nm_s, nm_d in [("convb_t", "convb"), ("dtb_t", "dtb"), ("acol_t", "acol"),
                       ("dcol_t", "dcol"), ("nrmw_t", "nrmw"), ("idbf_t", "idbf"),
                       ("idf_t", "idf"), ("ones_rt", "ones_r"), ("yidx_t", "yidx"),
                       ("consts_t", "consts")]:
        em.emit("sync", _bld(lambda e, o=S[nm_s][:], i=D[nm_d]: e.dma_start(out=o, in_=i)),
                dma=True)
    em.emit("sync", _bld(lambda e: e.dma_start(out=S["gwt_t"][:], in_=_kpm(D["w_gate"]))),
            dma=True)
    ctx["init_ev"] = em.qsnap("sync")
    em.emit("gpsimd", _bld(lambda e: e.dma_start(
        out=S["onesb_t"][:], in_=D["ones_r"].bitcast(F32))), dma=True)
    ctx["init_ev_g"] = em.qsnap("gpsimd")
    e1 = em.emit("scalar", _bld(lambda e: e.activation(
        S["acol_t"][:], S["acol_t"][:], AF.Exp, bias=0.0, scale=1.0)),
        waits=[ctx["init_ev"]])
    em.emit("vector", _bld(lambda e: e.tensor_scalar(
        S["acol_t"][:], S["acol_t"][:], -1.0, None, ALU.mult)), waits=[e1])
    ctx["acol_ev"] = em.ev("vector")


def emit_phase1(ctx):
    D, S, PB, em = ctx["D"], ctx["S"], ctx["PB"], ctx["em"]
    hnW = TG + 3

    ev_z = em.emit("gpsimd", _bld(lambda e: e.dma_start(
        out=S["w_inz_t"][:], in_=_kpm(D["w_inz"]))),
        dma=True)
    for kc in range(8):
        em.emit("vector", _bld(lambda e, kc=kc: e.tensor_scalar(
            S["w_inz_t"][:, kc * DQ:(kc + 1) * DQ], S["w_inz_t"][:, kc * DQ:(kc + 1) * DQ],
            S["nrmw_t"][:, kc:kc + 1], None, ALU.mult)), waits=[ev_z, ctx["init_ev"]])
    ctx["winz_ev"] = em.ev("vector")

    ev_cw = em.emit("gpsimd", _bld(lambda e: e.dma_start(
        out=S["cwbc_t"][:],
        in_=D["convw"].rearrange("k m -> (k m)").unsqueeze(0).broadcast_to([128, DC * DQ]))),
        dma=True)
    for kc in range(8):
        lv = em.emit("sync", _bld(lambda e, kc=kc: e.dma_start(
            out=S["wtmp_t"][:], in_=D["w_inx"][kc * 128:(kc + 1) * 128, :])),
            waits=[em.ev("vector")], dma=True)
        em.emit("vector", _bld(lambda e, kc=kc: e.tensor_scalar(
            S["wtmp_t"][:], S["wtmp_t"][:], S["nrmw_t"][:, kc:kc + 1], None, ALU.mult)),
            waits=[lv, ctx["init_ev"]])
        for k in range(DC):
            em.emit("vector", _bld(lambda e, kc=kc, k=k: e.tensor_tensor(
                S["wtap_t"][:, (k * 8 + kc) * DQ:(k * 8 + kc + 1) * DQ],
                S["wtmp_t"][:], S["cwbc_t"][:, k * DQ:(k + 1) * DQ], ALU.mult)),
                waits=[ev_cw])
    ctx["wtap_ev"] = em.ev("vector")
    ev_xp = em.emit("sync", _bld(lambda e: e.dma_start(
        out=S["w_xp_t"][:], in_=_kpm(D["w_xp"]))),
        dma=True)

    hn = [S["hn_t"][:, 0:8 * hnW].rearrange("p (kc w) -> p kc w", kc=8),
          S["hn_t"][:, 8 * hnW:16 * hnW].rearrange("p (kc w) -> p kc w", kc=8)]
    for b_ in range(2):
        for kc in range(8):
            em.emit("vector", _bld(lambda e, b_=b_, kc=kc: e.tensor_copy(
                S["hn_t"][:, (b_ * 8 + kc) * hnW:(b_ * 8 + kc) * hnW + 3],
                S["consts_t"][:, 4:5].broadcast_to([128, 3]))),
                waits=[ctx["init_ev"]])
    hz_ev = em.ev("vector")

    xld = {}
    rbc_ev = {}
    hn_ev = {}
    tail_ev = {}
    pe_hn = {}
    xc_ev = {}
    sz_st = {}
    xc_st = {}
    pp_st = {}
    CONV_B = (0, 1)
    Z_B = (2, 3)
    XT_B = (4, 5)
    PP_B = 6

    rst_ev = {}
    rrow_st = {}
    ca_ev = {}

    def emit_xld(g):
        gb = g % 2
        for tt in range(4):
            w = [xld.get((g - 2, tt)), hn_ev.get(g - 2)]
            xld[(g, tt)] = em.emit("sync", _bld(
                lambda e, g=g, tt=tt, gb=gb: e.dma_start(
                    out=S["xt_t"][:, (gb * 4 + tt) * DM:(gb * 4 + tt + 1) * DM],
                    in_=D["x"][g * TG + tt * 128:g * TG + (tt + 1) * 128, :])),
                waits=w, dma=True)

    def emit_chainA(g):
        gb = g % 2
        for tt in range(4):
            buf = S["xt_t"][:, (gb * 4 + tt) * DM:(gb * 4 + tt + 1) * DM]
            em.emit("scalar", _bld(
                lambda e, buf=buf, gb=gb, tt=tt: e.activation(
                    S["sq_t"][:], buf[:, 0:TG], AF.Square, bias=0.0, scale=1.0,
                    accum_out=S["rstdc_t"][:, gb * 8 + tt:gb * 8 + tt + 1])),
                waits=[xld[(g, tt)], rst_ev.get(g - 2)])
            em.emit("scalar", _bld(
                lambda e, buf=buf: e.activation(
                    S["sq_t"][:], buf[:, TG:DM], AF.Square, bias=0.0, scale=1.0,
                    accum_out=S["tiny_t"][:, 0:1])),
                waits=[em.ev("vector")])
            em.emit("vector", _bld(
                lambda e, gb=gb, tt=tt: e.tensor_tensor(
                    S["rstdc_t"][:, gb * 8 + tt:gb * 8 + tt + 1],
                    S["rstdc_t"][:, gb * 8 + tt:gb * 8 + tt + 1],
                    S["tiny_t"][:, 0:1], ALU.add)), waits=[em.ev("scalar")])
        e1 = em.emit("scalar", _bld(lambda e, gb=gb: e.activation(
            S["rstdc_t"][:, gb * 8:gb * 8 + 4], S["rstdc_t"][:, gb * 8:gb * 8 + 4],
            AF.Sqrt, bias=EPS, scale=1.0 / DM)), waits=[em.ev("vector")])
        ca_ev[g] = em.emit("vector", _bld(lambda e, gb=gb: e.reciprocal(
            S["rstdc_t"][:, gb * 8:gb * 8 + 4], S["rstdc_t"][:, gb * 8:gb * 8 + 4])),
            waits=[e1])

    def emit_rfin(g):
        gb = g % 2
        e3 = em.emit("tensor", _bld(lambda e, gb=gb: e.transpose(
            PB[7][0:4, 0:128], S["rstdc_t"][:, gb * 8:gb * 8 + 4], S["idf_t"][:])),
            waits=[ca_ev[g], ctx["init_ev"], rst_ev.get(g - 1)])
        rst_ev[g] = em.emit("scalar", _bld(lambda e, gb=gb: e.activation(
            S["rsev_t"][0:4, gb * 128:(gb + 1) * 128], PB[7][0:4, 0:128],
            AF.Copy, bias=0.0, scale=1.0)), waits=[e3])
        rrow_st[g] = em.emit("gpsimd", _bld(lambda e, g=g, gb=gb: e.dma_start(
            out=D["rstd_row"].rearrange("o (g2 w) -> (o g2) w", w=128)[4 * g:4 * (g + 1), :],
            in_=S["rsev_t"][0:4, gb * 128:(gb + 1) * 128])),
            waits=[rst_ev[g]], dma=True)
        rbc_ev[g] = em.emit("gpsimd", _bld(
            lambda e, g=g, gb=gb: e.dma_start(
                out=S["rbc_t"][:, gb * TG:(gb + 1) * TG],
                in_=D["rstd_row"][0:1, g * TG:(g + 1) * TG].broadcast_to([128, TG]))),
            waits=[rrow_st[g], hn_ev.get(g - 2)], dma=True)

    emit_xld(0)
    emit_chainA(0)
    emit_rfin(0)
    for g in range(NTG):
        gb = g % 2
        if g + 1 < NTG:
            emit_xld(g + 1)
            emit_chainA(g + 1)

        for kc in range(8):
            xb = XT_B[kc % 2]
            tevs = []
            for tt in range(4):
                w = [xld[(g, tt)]]
                if kc == 0 and tt == 0:
                    w += [pe_hn.get(g - 2), tail_ev.get(g - 1)]
                w.append(hn_ev.get((g, kc - 2)))
                tevs.append(em.emit("tensor", _bld(
                    lambda e, kc=kc, tt=tt, gb=gb, xb=xb: e.transpose(
                        PB[xb][:, tt * 128:(tt + 1) * 128],
                        S["xt_t"][:, (gb * 4 + tt) * DM + kc * 128:(gb * 4 + tt) * DM + (kc + 1) * 128],
                        S["idf_t"][:])), waits=w))
            hn_ev[(g, kc)] = em.emit("vector", _bld(
                lambda e, kc=kc, gb=gb, xb=xb: e.tensor_tensor(
                    hn[gb][:, kc, 3:3 + TG], PB[xb][:, 0:TG],
                    S["rbc_t"][:, gb * TG:(gb + 1) * TG], ALU.mult)),
                waits=[tevs[-1], rbc_ev[g], hz_ev])
        hn_ev[g] = hn_ev[(g, 7)]
        tail_ev[g] = em.emit("vector", _bld(
            lambda e, gb=gb: e.tensor_copy(hn[1 - gb][:, :, 0:3], hn[gb][:, :, TG:TG + 3])),
            waits=[hn_ev[g], pe_hn.get(g - 1)])

        for dti in range(NDT):
            bank = CONV_B[dti % 2]
            w0 = [hn_ev[g], ctx["wtap_ev"], tail_ev.get(g - 1) if g >= 1 else None,
                  xc_ev.get((g, dti - 2)), xc_ev.get((g - 1, dti + 2))]
            first = True
            for k in range(DC):
                for kc in range(8):
                    last = (k == DC - 1 and kc == 7)
                    em.emit("tensor", _bld(
                        lambda e, k=k, kc=kc, dti=dti, gb=gb, bank=bank, first=first, last=last: e.matmul(
                            PB[bank][:, 0:TG],
                            S["wtap_t"][:, (k * 8 + kc) * DQ + dti * 128:(k * 8 + kc) * DQ + (dti + 1) * 128],
                            hn[gb][:, kc, k:k + TG], start=first, stop=last)),
                        waits=w0 if first else [], inc=last)
                    first = False
            mm = em.ev("tensor")
            xc_ev[(g, dti)] = em.emit("scalar", _bld(
                lambda e, dti=dti, bank=bank: e.activation(
                    S["xcg_t"][:, dti * TG:(dti + 1) * TG], PB[bank][:, 0:TG],
                    AF.Silu, bias=S["convb_t"][:, dti:dti + 1], scale=1.0)),
                waits=[mm, xc_st.get((g - 1, dti))])
            xc_st[(g, dti)] = em.emit("gpsimd", _bld(
                lambda e, g=g, dti=dti: e.dma_start(
                    out=D["xc_d"][dti * 128:(dti + 1) * 128, g * TG:(g + 1) * TG],
                    in_=S["xcg_t"][:, dti * TG:(dti + 1) * TG])),
                waits=[xc_ev[(g, dti)]], dma=True)
            ctx.setdefault("xc_done", {})[(g, dti)] = xc_st[(g, dti)]

        if g + 1 < NTG:
            emit_rfin(g + 1)

        for dti in range(NDT):
            bank = Z_B[dti % 2]
            w0 = [hn_ev[g], ctx["winz_ev"],
                  em.ev("scalar") if dti >= 2 else None]
            for kc in range(8):
                em.emit("tensor", _bld(
                    lambda e, kc=kc, dti=dti, gb=gb, bank=bank: e.matmul(
                        PB[bank][:, 0:TG],
                        S["w_inz_t"][:, kc * DQ + dti * 128:kc * DQ + (dti + 1) * 128],
                        hn[gb][:, kc, 3:3 + TG], start=(kc == 0), stop=(kc == 7))),
                    waits=w0 if kc == 0 else [], inc=(kc == 7))
            mm = em.ev("tensor")
            sev = em.emit("scalar", _bld(
                lambda e, bank=bank: e.activation(
                    S["szg_t"][:], PB[bank][:, 0:TG], AF.Silu, bias=0.0, scale=1.0)),
                waits=[mm, sz_st.get((g, dti - 1)), sz_st.get((g - 1, NDT - 1))])
            sz_st[(g, dti)] = em.emit("gpsimd", _bld(
                lambda e, g=g, dti=dti: e.dma_start(
                    out=D["sz_d"][dti * 128:(dti + 1) * 128, g * TG:(g + 1) * TG],
                    in_=S["szg_t"][:])), waits=[sev], dma=True)
            ctx.setdefault("sz_done", {})[(g, dti)] = sz_st[(g, dti)]
        pe_hn[g] = em.ev("tensor")

        w0 = [xc_ev[(g, d2)] for d2 in range(NDT)] + [ev_xp, pp_st.get(g - 2)]
        for dti in range(NDT):
            em.emit("tensor", _bld(
                lambda e, dti=dti: e.matmul(
                    PB[PP_B][0:96, 0:TG], S["w_xp_t"][:, dti * 96:(dti + 1) * 96],
                    S["xcg_t"][:, dti * TG:(dti + 1) * TG],
                    start=(dti == 0), stop=(dti == NDT - 1))),
                waits=w0 if dti == 0 else [], inc=(dti == NDT - 1))
        mm = em.ev("tensor")
        pev = em.emit("scalar", _bld(
            lambda e, gb=gb: e.activation(
                S["projs_t"][0:96, gb * TG:(gb + 1) * TG], PB[PP_B][0:96, 0:TG],
                AF.Copy, bias=0.0, scale=1.0)), waits=[mm, pp_st.get(g - 2)])
        pp_st[g] = em.emit("gpsimd", _bld(
            lambda e, g=g, gb=gb: e.dma_start(
                out=D["ppart"][g // 8, 0:96, (g % 8) * TG:(g % 8 + 1) * TG],
                in_=S["projs_t"][0:96, gb * TG:(gb + 1) * TG])),
            waits=[pev], dma=True)

    if _DBG:
        em.barrier()
    if "xc" in _DBG:
        em.emit("gpsimd", _bld(lambda e: e.dma_start(out=D["dbg_xc"], in_=D["xc_d"])), dma=True)
    if "sz" in _DBG:
        em.emit("gpsimd", _bld(lambda e: e.dma_start(out=D["dbg_sz"], in_=D["sz_d"])), dma=True)
    if "ppart" in _DBG:
        em.emit("gpsimd", _bld(lambda e: e.dma_start(out=D["dbg_ppart"], in_=D["ppart"])), dma=True)
    arh = []
    pbh = []
    for h in range(2):
        ev = em.emit("gpsimd", _bld(lambda e, h=h: e.collective_compute(
            "AllReduce", ALU.add, replica_groups=GROUPS,
            ins=[D["ppart"][h].opt()], outs=[D["pred"][h].opt()])),
            waits=[pp_st[g] for g in range(h * 8, h * 8 + 8)], inc_amt=1)
        arh.append(ev)
        pbh.append(em.emit("gpsimd", _bld(lambda e, h=h: e.dma_start(
            out=D["predb"][h], in_=D["pred"][h, 64:96, :])), waits=[ev], dma=True))
    ctx["arh"] = arh
    ctx["pbh"] = pbh
    ctx["ar_ev"] = arh[1]
    if "pred" in _DBG:
        em.barrier()
        em.emit("gpsimd", _bld(lambda e: e.dma_start(out=D["dbg_pred"].rearrange("r (h w) -> h r w", h=2), in_=D["pred"])), dma=True)
        em.barrier()


def emit_dt_init(ctx):
    D, S, em = ctx["D"], ctx["S"], ctx["em"]
    ctx["dt_w_ev"] = em.emit("sync", _bld(lambda e: e.dma_start(
        out=S["w_dt_t"][0:64, :], in_=D["w_dt"])), waits=[ctx["init_ev"]], dma=True)
    ctx["dt_st"] = {}
    ctx["dt_e5"] = {}
    ctx["dt_ld"] = {}
    ctx["dt_ti"] = [0]
    ctx["dtb_last"] = {}


def emit_dt_group(ctx, g, extra=()):
    D, S, PB, em = ctx["D"], ctx["S"], ctx["PB"], ctx["em"]
    st_ev, e5s, ld = ctx["dt_st"], ctx["dt_e5"], ctx["dt_ld"]
    gb = g % 2
    ld[g] = em.emit("gpsimd", _bld(lambda e, g=g, gb=gb: e.dma_start(
        out=S["dtraw_t"][0:64, gb * TG:(gb + 1) * TG],
        in_=D["pred"][g // 8, 0:64, (g % 8) * TG:(g % 8 + 1) * TG].bitcast(F32R))),
        waits=[st_ev.get((g - 2, NDT - 1)), ctx["arh"][g // 8]], dma=True)
    for dti in range(NDT):
        ti = ctx["dt_ti"][0]
        bank = 2 + ti % 2
        par = ti % 2
        sp = slice(par * TG, (par + 1) * TG)
        em.emit("tensor", _bld(lambda e, dti=dti, gb=gb, bank=bank: e.matmul(
            PB[bank][:, 0:TG], S["w_dt_t"][0:64, dti * 128:(dti + 1) * 128],
            S["dtraw_t"][0:64, gb * TG:(gb + 1) * TG], start=True, stop=True)),
            waits=[ld[g], ctx["dt_w_ev"], st_ev.get((g, dti - 2)), st_ev.get((g - 1, dti + 2)),
                   e5s.get(ti - 2)] + list(extra))
        mm = em.ev("tensor")
        e1 = em.emit("scalar", _bld(lambda e, bank=bank, dti=dti, sp=sp: e.activation(
            S["spb_t"][:, sp], PB[bank][:, 0:TG], AF.Abs,
            bias=S["dtb_t"][:, dti:dti + 1], scale=1.0)),
            waits=[mm, st_ev.get((g, dti - 2)), st_ev.get((g - 1, dti + 2)),
                   e5s.get(ti - 2)])
        e2 = em.emit("scalar", _bld(lambda e, sp=sp: e.activation(
            S["spc_t"][:, sp], S["spb_t"][:, sp], AF.Exp, bias=0.0, scale=-1.0)))
        e3 = em.emit("scalar", _bld(lambda e, sp=sp: e.activation(
            S["spc_t"][:, sp], S["spc_t"][:, sp], AF.Ln, bias=1.0, scale=1.0)))
        e4 = em.emit("scalar", _bld(lambda e, bank=bank, dti=dti: e.activation(
            S["spa_t"][:], PB[bank][:, 0:TG], AF.Relu,
            bias=S["dtb_t"][:, dti:dti + 1], scale=1.0)),
            waits=[e5s.get(ti - 1)])
        e5 = em.emit("vector", _bld(lambda e, sp=sp: e.tensor_tensor(
            S["spb_t"][:, sp], S["spa_t"][:], S["spc_t"][:, sp], ALU.add)),
            waits=[e4])
        e5s[ti] = e5
        st_ev[(g, dti)] = em.emit("gpsimd", _bld(lambda e, g=g, dti=dti, sp=sp: e.dma_start(
            out=D["dt_d"][dti * 128:(dti + 1) * 128, g * TG:(g + 1) * TG],
            in_=S["spb_t"][:, sp])), waits=[e5], dma=True)
        ctx["dt_done"][(g, dti)] = st_ev[(g, dti)]
        ctx["dt_ti"][0] = ti + 1


def emit_scan(ctx):
    D, S, PB, em = ctx["D"], ctx["S"], ctx["PB"], ctx["em"]
    NBC = 8
    NBUF = 4
    ld_dt, ld_xc, ld_sz = {}, {}, {}
    bb_ev, cb_ev = {}, {}
    dtx_ev, da_ev, dbx_ev, scan_ev, hc_ev, pe_ev, st_cp = {}, {}, {}, {}, {}, {}, {}
    y_done, yg_dma = {}, {}

    if "dt" in _PHASES:
        for g0 in range(4):
            emit_dt_group(ctx, g0,
                          extra=([ctx["sz_done"][(NTG - 1, NDT - 1)]] if g0 == 0 else []))
        ctx["dtb_last"][0] = ctx["dt_e5"][ctx["dt_ti"][0] - 1]
    for tc in range(NTC):
        cbuf = tc % 2
        if "dt" in _PHASES and tc < NTC - 2:
            emit_dt_group(ctx, 2 * tc + 4, extra=[y_done.get((tc - 1, 1))])
            emit_dt_group(ctx, 2 * tc + 5)
            ctx["dtb_last"][tc] = ctx["dt_e5"][ctx["dt_ti"][0] - 1]
        for dti in range(NDT):
            w = [y_done.get((tc - 2, dti))]
            dtw = [ctx["dt_done"][(g2, dti)] for g2 in range(tc * TC // TG, (tc + 1) * TC // TG)]
            ld_dt[(tc, dti)] = em.emit("sync", _bld(
                lambda e, tc=tc, dti=dti, cbuf=cbuf: e.dma_start(
                    out=S["dt_c"][:, (cbuf * NDT + dti) * TC:(cbuf * NDT + dti + 1) * TC],
                    in_=D["dt_d"][dti * 128:(dti + 1) * 128, tc * TC:(tc + 1) * TC])),
                waits=w + dtw + [scan_ev.get((tc - 2, DS - 1, dti))], dma=True)
            xcw = [ctx["xc_done"][(g2, dti)] for g2 in range(tc * TC // TG, (tc + 1) * TC // TG)]
            szw = [ctx["sz_done"][(g2, dti)] for g2 in range(tc * TC // TG, (tc + 1) * TC // TG)]
            ld_xc[(tc, dti)] = em.emit("sync", _bld(
                lambda e, tc=tc, dti=dti, cbuf=cbuf: e.dma_start(
                    out=S["xcc"][:, (cbuf * NDT + dti) * TC:(cbuf * NDT + dti + 1) * TC],
                    in_=D["xc_d"][dti * 128:(dti + 1) * 128, tc * TC:(tc + 1) * TC])),
                waits=w + xcw + [ctx["ar_ev"]], dma=True)
            ld_sz[(tc, dti)] = em.emit("sync", _bld(
                lambda e, tc=tc, dti=dti, cbuf=cbuf: e.dma_start(
                    out=S["szc"][:, (cbuf * NDT + dti) * TC:(cbuf * NDT + dti + 1) * TC],
                    in_=D["sz_d"][dti * 128:(dti + 1) * 128, tc * TC:(tc + 1) * TC])),
                waits=w + szw + [ctx["ar_ev"]], dma=True)
        for dti in range(NDT):
            dtx_ev[(tc, dti)] = em.emit("vector", _bld(
                lambda e, dti=dti, cbuf=cbuf: e.tensor_tensor(
                    S["dtx_c"][:, dti * TC:(dti + 1) * TC],
                    S["dt_c"][:, (cbuf * NDT + dti) * TC:(cbuf * NDT + dti + 1) * TC],
                    S["xcc"][:, (cbuf * NDT + dti) * TC:(cbuf * NDT + dti + 1) * TC],
                    ALU.mult)),
                waits=[ld_dt[(tc, dti)], ld_xc[(tc, dti)],
                       scan_ev.get((tc - 1, DS - 1, dti))])

        for s in range(DS):
            slot = s % NBC
            bb_ev[(tc, s)] = em.emit("scalar", _bld(
                lambda e, tc=tc, s=s, slot=slot: e.dma_start(
                    out=S["bbc"][:, slot * TC:(slot + 1) * TC],
                    in_=D["predb"][tc // 4, s:s + 1, (tc % 4) * TC:(tc % 4 + 1) * TC]
                    .broadcast_to([128, TC]))),
                waits=[dbx_ev.get((tc, s - NBC, NDT - 1)),
                       dbx_ev.get((tc - 1, s - NBC + DS, NDT - 1)), ctx["pbh"][tc // 4]], dma=True)
            cb_ev[(tc, s)] = em.emit("scalar", _bld(
                lambda e, tc=tc, s=s, slot=slot: e.dma_start(
                    out=S["cbc"][:, slot * TC:(slot + 1) * TC],
                    in_=D["predb"][tc // 4, 16 + s:17 + s, (tc % 4) * TC:(tc % 4 + 1) * TC]
                    .broadcast_to([128, TC]))),
                waits=[hc_ev.get((tc, s - NBC, NDT - 1)),
                       hc_ev.get((tc - 1, s - NBC + DS, NDT - 1)), ctx["pbh"][tc // 4]], dma=True)

            for dti in range(NDT):
                i = s * NDT + dti
                ib = i % NBUF
                pi = ((i - NBUF) // NDT, (i - NBUF) % NDT) if i >= NBUF else None
                da_ev[(tc, s, dti)] = em.emit("scalar", _bld(
                    lambda e, s=s, dti=dti, ib=ib, cbuf=cbuf: e.activation(
                        S["da_t"][:, ib * TC:(ib + 1) * TC],
                        S["dt_c"][:, (cbuf * NDT + dti) * TC:(cbuf * NDT + dti + 1) * TC],
                        AF.Exp, bias=0.0,
                        scale=S["acol_t"][:, dti * DS + s:dti * DS + s + 1])),
                    waits=[ld_dt[(tc, dti)], ctx["acol_ev"],
                           scan_ev.get((tc,) + pi) if pi else scan_ev.get((tc - 1, DS - 1, (i - NBUF) % NDT)),
                           ] + ([ctx["ar_ev"]] if tc == 0 and s == 0 else []))
                dbx_ev[(tc, s, dti)] = em.emit("vector", _bld(
                    lambda e, s=s, dti=dti, ib=ib, slot=slot: e.tensor_tensor(
                        S["dbx_t"][:, ib * TC:(ib + 1) * TC],
                        S["dtx_c"][:, dti * TC:(dti + 1) * TC],
                        S["bbc"][:, slot * TC:(slot + 1) * TC], ALU.mult)),
                    waits=[dtx_ev[(tc, dti)], bb_ev[(tc, s)]])

                def mkscan(e, tc=tc, dti=dti, s=s, ib=ib):
                    init = 0.0 if tc == 0 else S["st_t"][:, dti * DS + s:dti * DS + s + 1]
                    return e.tensor_tensor_scan(
                        S["h_t"][:, ib * TC:(ib + 1) * TC],
                        S["da_t"][:, ib * TC:(ib + 1) * TC],
                        S["dbx_t"][:, ib * TC:(ib + 1) * TC], init, ALU.mult, ALU.add)
                prev_h = ((tc,) + pi) if pi else (tc - 1, (64 - NBUF + i) // NDT, (64 - NBUF + i) % NDT)
                scan_ev[(tc, s, dti)] = em.emit("vector", mkscan,
                    waits=[da_ev[(tc, s, dti)], st_cp.get((tc - 1, s, dti)),
                           st_cp.get(prev_h)])
                if tc < NTC - 1:
                    st_cp[(tc, s, dti)] = em.emit("gpsimd", _bld(
                        lambda e, s=s, dti=dti, ib=ib: e.tensor_copy(
                            S["st_t"][:, dti * DS + s:dti * DS + s + 1],
                            S["h_t"][:, ib * TC + TC - 1:ib * TC + TC])),
                        waits=[scan_ev[(tc, s, dti)]])
                hc_ev[(tc, s, dti)] = em.emit("vector", _bld(
                    lambda e, s=s, dti=dti, ib=ib, slot=slot: e.tensor_tensor(
                        S["hc_t"][:, ib * TC:(ib + 1) * TC],
                        S["h_t"][:, ib * TC:(ib + 1) * TC],
                        S["cbc"][:, slot * TC:(slot + 1) * TC], ALU.mult)),
                    waits=[cb_ev[(tc, s)],
                           pe_ev.get((tc,) + pi) if pi else pe_ev.get((tc - 1, DS - 1, (i - NBUF) % NDT))])
                wpe = [hc_ev[(tc, s, dti)], ctx["init_ev_g"], ctx["init_ev"]]
                if s == 0:
                    wpe.append(y_done.get((tc - 1, dti)))
                    if dti == 1:
                        wpe.append(ctx["dtb_last"].get(tc))
                for j in range(2):
                    em.emit("tensor", _bld(
                        lambda e, dti=dti, ib=ib, j=j, s=s: e.matmul(
                            PB[2 * dti + j][:, 0:512], S["idbf_t"][:],
                            S["hc_t"][:, ib * TC + j * 512:ib * TC + (j + 1) * 512],
                            start=(s == 0), stop=(s == DS - 1))),
                        waits=wpe if j == 0 else [], inc=(j == 1))
                pe_ev[(tc, s, dti)] = em.ev("tensor")

        for dti in range(NDT):
            w = [pe_ev[(tc, DS - 1, dti)], ld_xc[(tc, dti)], ctx["init_ev"]]
            for j in range(2):
                em.emit("vector", _bld(
                    lambda e, dti=dti, cbuf=cbuf, j=j: e.scalar_tensor_tensor(
                        S["y1_t"][:, j * 512:(j + 1) * 512],
                        S["xcc"][:, (cbuf * NDT + dti) * TC + j * 512:(cbuf * NDT + dti) * TC + (j + 1) * 512],
                        S["dcol_t"][:, dti:dti + 1], PB[2 * dti + j][:, 0:512],
                        ALU.mult, ALU.add)), waits=w if j == 0 else [])
            gi = tc * NDT + dti
            ygs = gi % 2
            pgi = ((gi - 2) // NDT, (gi - 2) % NDT) if gi >= 2 else None
            em.emit("vector", _bld(
                lambda e, dti=dti, cbuf=cbuf, ygs=ygs: e.tensor_tensor(
                    S["yg_t"][:, ygs * TC:(ygs + 1) * TC], S["y1_t"][:],
                    S["szc"][:, (cbuf * NDT + dti) * TC:(cbuf * NDT + dti + 1) * TC],
                    ALU.mult)),
                waits=[ld_sz[(tc, dti)], yg_dma.get(pgi)])
            y_done[(tc, dti)] = em.ev("vector")
            yg_dma[(tc, dti)] = em.emit("sync", _bld(
                lambda e, tc=tc, dti=dti, ygs=ygs: e.dma_start(
                    out=D["ygat_in"][dti * 128:(dti + 1) * 128, tc * TC:(tc + 1) * TC],
                    in_=S["yg_t"][:, ygs * TC:(ygs + 1) * TC])),
                waits=[y_done[(tc, dti)]], dma=True)

    ctx["scan_done"] = [em.ev("vector"), em.ev("scalar"), em.ev("tensor")] + \
        [yg_dma[(NTC - 1, dti)] for dti in range(NDT)]
    if "y" in _DBG:
        em.barrier()
        em.emit("gpsimd", _bld(lambda e: e.dma_start(out=D["dbg_y"], in_=D["ygat_in"])), dma=True)
        em.barrier()
    ctx["ag_ev"] = em.emit("gpsimd", _bld(lambda e: e.collective_compute(
        "AllGather", ALU.bypass, replica_groups=GROUPS,
        ins=[D["ygat_in"].opt()], outs=[D["ygat_all"].opt()])),
        waits=[yg_dma[(tc, dti)] for tc in range(NTC) for dti in range(NDT)], inc_amt=1)
    if "ygall" in _DBG:
        em.barrier()
        em.emit("gpsimd", _bld(lambda e: e.dma_start(out=D["dbg_ygall"], in_=D["ygat_all"])), dma=True)
        em.barrier()


def emit_phase3(ctx):
    D, S, PB, em = ctx["D"], ctx["S"], ctx["PB"], ctx["em"]
    NTT = TQ // 128

    xq_ld = {}
    xq_ev = {}
    for tt in range(NTT):
        bi = tt % 2
        xq_ld[tt] = em.emit("sync", _bld(
            lambda e, tt=tt, bi=bi: e.dma_start(
                out=S["xstage_t"][:, bi * DM:(bi + 1) * DM],
                in_=D["xq"][tt * 128:(tt + 1) * 128, :])),
            waits=[xq_ev.get(tt - 2)], dma=True)
        for kc in range(8):
            xb = 6 + (kc % 2)
            evt = em.emit("tensor", _bld(
                lambda e, tt=tt, kc=kc, bi=bi, xb=xb: e.transpose(
                    PB[xb][:, 0:128],
                    S["xstage_t"][:, bi * DM + kc * 128:bi * DM + (kc + 1) * 128],
                    S["idf_t"][:])),
                waits=[xq_ld[tt], xq_ev.get((tt, kc - 2)), xq_ev.get((tt - 1, kc + 6)), ctx["init_ev"]]
                + (ctx["scan_done"] if tt == 0 and kc < 2 else []))
            xq_ev[(tt, kc)] = em.emit("scalar", _bld(
                lambda e, tt=tt, kc=kc, xb=xb: e.activation(
                    S["xqb_t"][:, kc * TQ + tt * 128:kc * TQ + (tt + 1) * 128],
                    PB[xb][:, 0:128],
                    AF.Copy, bias=0.0, scale=1.0)),
                waits=[evt] + (ctx["scan_done"] if tt == 0 and kc == 0 else []))
        xq_ev[tt] = xq_ev[(tt, 7)]
    xq_all = em.ev("scalar")
    if _P3_STOP == "xq":
        return
    wo_ld = em.emit("sync", _bld(lambda e: e.dma_start(
        out=S["w_out_t"][:], in_=_kpm(D["w_out"]))),
        waits=ctx["scan_done"], dma=True)

    yld = {}
    r1_ev = {}
    sq_ev = {}
    st_mm = {}
    srow_ev = {}
    OP_B = (0, 1)
    SUM_B, SQ_B = 2, 3
    for nch in range(4):
        nb = nch % 2
        for kc in range(16):
            yld[(nch, kc)] = em.emit("gpsimd", _bld(
                lambda e, nch=nch, kc=kc, nb=nb: e.indirect_dma_start(
                    out=S["yr_t"][:, (nb * 16 + kc) * 512:(nb * 16 + kc + 1) * 512],
                    out_offset=None,
                    in_=D["ygat_all"].rearrange("d (blk w) -> (d blk) w", w=512),
                    in_offset=IndirectOffsetOnAxis(
                        S["yidx_t"][:, kc * 4 + nch:kc * 4 + nch + 1], axis=0))),
                waits=[ctx["init_ev"], ctx["ag_ev"], r1_ev.get((nch - 2, 7))], dma=True)
        for mt in range(8):
            bank = OP_B[mt % 2]
            w0 = [yld[(nch, kc)] for kc in range(16)]
            w0 += [wo_ld, r1_ev.get((nch, mt - 2)), r1_ev.get((nch - 1, mt + 6))]
            for kc in range(16):
                em.emit("tensor", _bld(
                    lambda e, mt=mt, kc=kc, nb=nb, bank=bank: e.matmul(
                        PB[bank][:, 0:512],
                        S["w_out_t"][:, kc * DM + mt * 128:kc * DM + (mt + 1) * 128],
                        S["yr_t"][:, (nb * 16 + kc) * 512:(nb * 16 + kc + 1) * 512],
                        start=(kc == 0), stop=(kc == 15))),
                    waits=w0 if kc == 0 else [], inc=(kc == 15))
            mm = em.ev("tensor")
            r1_ev[(nch, mt)] = em.emit("vector", _bld(
                lambda e, mt=mt, nch=nch, bank=bank: e.scalar_tensor_tensor(
                    S["r1_t"][:, mt * TQ + nch * 512:mt * TQ + (nch + 1) * 512],
                    S["xqb_t"][:, mt * TQ + nch * 512:mt * TQ + (nch + 1) * 512],
                    2.0, PB[bank][:, 0:512], ALU.mult, ALU.add)),
                waits=[mm, xq_all])
            sq_ev[(nch, mt)] = em.emit("scalar", _bld(
                lambda e, mt=mt, nch=nch: e.activation(
                    S["sq_t"][:], S["r1_t"][:, mt * TQ + nch * 512:mt * TQ + (nch + 1) * 512],
                    AF.Square, bias=0.0, scale=1.0)),
                waits=[r1_ev[(nch, mt)], st_mm.get((nch, mt - 1)), st_mm.get((nch - 1, 7))])
            em.emit("tensor", _bld(
                lambda e, mt=mt, nch=nch: e.matmul(
                    PB[SUM_B][0:1, 0:512], S["ones_rt"][:, 0:1],
                    S["r1_t"][:, mt * TQ + nch * 512:mt * TQ + (nch + 1) * 512],
                    start=(mt == 0), stop=(mt == 7))),
                waits=[r1_ev[(nch, mt)], srow_ev.get(nch - 1)])
            em.emit("tensor", _bld(
                lambda e, mt=mt: e.matmul(
                    PB[SQ_B][0:1, 0:512], S["onesb_t"][:, 0:1], S["sq_t"][:, 0:512].bitcast(BF16) if False else S["sq_t"][:],
                    start=(mt == 0), stop=(mt == 7))),
                waits=[sq_ev[(nch, mt)]])
            st_mm[(nch, mt)] = em.ev("tensor")
        smm = em.ev("tensor")
        ev1 = em.emit("scalar", _bld(lambda e: e.activation(
            S["srow_t"][0:1, :], PB[SUM_B][0:1, 0:512], AF.Copy, bias=0.0, scale=1.0)),
            waits=[smm])
        ev2 = em.emit("scalar", _bld(lambda e: e.activation(
            S["srow2_t"][0:1, :], PB[SQ_B][0:1, 0:512], AF.Copy, bias=0.0, scale=1.0)))
        sa = em.emit("gpsimd", _bld(lambda e, nch=nch: e.dma_start(
            out=D["srow_d"][0:1, nch * 512:(nch + 1) * 512], in_=S["srow_t"][0:1, :])),
            waits=[ev1], dma=True)
        sb = em.emit("gpsimd", _bld(lambda e, nch=nch: e.dma_start(
            out=D["srow_d"][1:2, nch * 512:(nch + 1) * 512], in_=S["srow2_t"][0:1, :])),
            waits=[ev2], dma=True)
        srow_ev[nch] = [sa, sb]

    if _P3_STOP == "op":
        if "x1" in _DBG:
            em.barrier()
            for mt in range(8):
                em.emit("gpsimd", _bld(lambda e, mt=mt: e.dma_start(
                    out=D["dbg_x1"][mt * 128:(mt + 1) * 128, :],
                    in_=S["r1_t"][:, mt * TQ:(mt + 1) * TQ].bitcast(F32))), dma=True)
        return
    ln_rows(ctx, "srow_d", "mrow_d", [srow_ev[n] for n in range(4)])
    mb_ev = em.emit("gpsimd", _bld(lambda e: e.dma_start(
        out=S["mbc_t"][:], in_=D["mrow_d"][0:1, :].broadcast_to([128, TQ]))),
        waits=[ctx["lnrow_ev"]], dma=True)
    rb_ev = em.emit("gpsimd", _bld(lambda e: e.dma_start(
        out=S["rbc2_t"][:], in_=D["mrow_d"][1:2, :].broadcast_to([128, TQ]))),
        waits=[ctx["lnrow_ev"]], dma=True)
    x1_ev = {}
    for mt in range(8):
        em.emit("vector", _bld(lambda e, mt=mt: e.tensor_tensor(
            S["r1_t"][:, mt * TQ:(mt + 1) * TQ], S["r1_t"][:, mt * TQ:(mt + 1) * TQ],
            S["mbc_t"][:], ALU.subtract)),
            waits=[mb_ev, rb_ev, r1_ev[(3, mt)], sq_ev[(3, mt)], st_mm[(3, 7)]])
        em.emit("vector", _bld(lambda e, mt=mt: e.tensor_tensor(
            S["r1_t"][:, mt * TQ:(mt + 1) * TQ], S["r1_t"][:, mt * TQ:(mt + 1) * TQ],
            S["rbc2_t"][:], ALU.mult)))
        x1_ev[mt] = em.emit("vector", _bld(lambda e, mt=mt: e.tensor_copy(
            S["xqb_t"][:, mt * TQ:(mt + 1) * TQ], S["r1_t"][:, mt * TQ:(mt + 1) * TQ])))
    x1_all = em.ev("vector")
    if "x1" in _DBG:
        for mt in range(8):
            em.emit("gpsimd", _bld(lambda e, mt=mt: e.dma_start(
                out=D["dbg_x1"][mt * 128:(mt + 1) * 128, :],
                in_=S["r1_t"][:, mt * TQ:(mt + 1) * TQ].bitcast(F32))),
                waits=[x1_all], dma=True)
        em.barrier()

    if _P3_STOP == "x1":
        return
    GL_B = 4
    lg_ev = {}
    cmb_tr = {}
    for tt in range(NTT):
        for kc in range(8):
            em.emit("tensor", _bld(
                lambda e, tt=tt, kc=kc: e.matmul(
                    PB[GL_B][0:128, 0:8],
                    S["r1_t"][:, kc * TQ + tt * 128:kc * TQ + (tt + 1) * 128],
                    S["gwt_t"][:, kc * E:(kc + 1) * E],
                    start=(kc == 0), stop=(kc == 7))),
                waits=([x1_all, ctx["init_ev"], lg_ev.get(tt - 1)] if kc == 0 else []),
                inc=(kc == 7))
        mmev = em.ev("tensor")
        lg_ev[tt] = em.emit("scalar", _bld(
            lambda e, tt=tt: e.activation(
                S["sc_t"][:, tt * E:(tt + 1) * E], PB[GL_B][0:128, 0:8],
                AF.Sigmoid, bias=0.0, scale=1.0)), waits=[mmev])
    scv = S["sc_t"][:].rearrange("p (g e) -> p g e", e=E)
    cbv = S["cb_t"][:].rearrange("p (g e) -> p g e", e=E)
    m1b = S["m1_t"][:].rearrange("p g -> p g").unsqueeze(2).broadcast_to([128, NTT, E])
    m2b = S["m2_t"][:].rearrange("p g -> p g").unsqueeze(2).broadcast_to([128, NTT, E])
    em.emit("vector", _bld(lambda e: e.tensor_reduce(
        S["m1_t"][:], scv, mybir.AxisListType.X, ALU.max)),
        waits=[lg_ev[t_] for t_ in range(NTT)])
    em.emit("vector", _bld(lambda e: e.tensor_tensor(
        S["lg_t"][:].rearrange("p (g e) -> p g e", e=E), scv, m1b, ALU.is_ge)))
    em.emit("vector", _bld(lambda e: e.scalar_tensor_tensor(
        cbv, S["lg_t"][:].rearrange("p (g e) -> p g e", e=E), -1e30, scv,
        ALU.mult, ALU.add)))
    em.emit("vector", _bld(lambda e: e.tensor_reduce(
        S["m2_t"][:], cbv, mybir.AxisListType.X, ALU.max)))
    em.emit("vector", _bld(lambda e: e.tensor_tensor(
        cbv, cbv, m2b, ALU.is_ge)))
    em.emit("vector", _bld(lambda e: e.tensor_tensor(
        S["lg_t"][:].rearrange("p (g e) -> p g e", e=E),
        S["lg_t"][:].rearrange("p (g e) -> p g e", e=E), m1b, ALU.mult)))
    em.emit("vector", _bld(lambda e: e.tensor_tensor(
        cbv, cbv, m2b, ALU.mult)))
    em.emit("vector", _bld(lambda e: e.tensor_tensor(
        cbv, cbv, S["lg_t"][:].rearrange("p (g e) -> p g e", e=E), ALU.add)))
    em.emit("vector", _bld(lambda e: e.tensor_tensor(
        S["tiny_t"][:, 0:16], S["m1_t"][:], S["m2_t"][:], ALU.add)))
    em.emit("vector", _bld(lambda e: e.tensor_scalar(
        S["tiny_t"][:, 0:16], S["tiny_t"][:, 0:16], 1e-20, None, ALU.add)))
    em.emit("vector", _bld(lambda e: e.reciprocal(
        S["m1_t"][:], S["tiny_t"][:, 0:16])))
    em.emit("vector", _bld(lambda e: e.tensor_scalar(
        S["m1_t"][:], S["m1_t"][:], SCALE_MOE, None, ALU.mult)))
    cmb_all = em.emit("vector", _bld(lambda e: e.tensor_tensor(
        cbv, cbv, m1b, ALU.mult)))
    for tt in range(NTT):
        evt = em.emit("tensor", _bld(lambda e, tt=tt: e.transpose(
            PB[5][0:8, 0:128], S["cb_t"][:, tt * E:(tt + 1) * E], S["idf_t"][:])),
            waits=[cmb_all, cmb_tr.get(tt - 1)])
        cmb_tr[tt] = em.emit("scalar", _bld(lambda e, tt=tt: e.activation(
            S["cbp_t"][0:8, tt * 128:(tt + 1) * 128], PB[5][0:8, 0:128],
            AF.Copy, bias=0.0, scale=1.0)), waits=[evt])
    cb_dma = em.emit("gpsimd", _bld(lambda e: e.dma_start(
        out=D["combt_d"], in_=S["cbp_t"][0:8, :])), waits=[cmb_tr[NTT - 1]], dma=True)
    if "comb" in _DBG:
        em.emit("gpsimd", _bld(lambda e: e.dma_start(
            out=D["dbg_comb"], in_=D["combt_d"])), waits=[cb_dma], dma=True)
    if "sc" in _DBG:
        em.emit("gpsimd", _bld(lambda e: e.dma_start(
            out=D["dbg_sc"], in_=S["sc_t"][:])), waits=[em.ev("vector")], dma=True)

    if _P3_STOP == "gate":
        return
    GB = (4, 6)
    UB = (5, 7)
    DN_B = (0, 1)
    SUM2_B, SQ2_B = 6, 7
    NCH2 = 4
    W2 = 512

    wg_ld, wu_ld, wd_ld, cbb_ev = {}, {}, {}, {}
    sg_ev, hm_ev, dn_ev = {}, {}, {}
    r2_ev, sq2_ev, s2_mm, srow2_ev = {}, {}, {}, {}
    gu_done = {}
    ffn_guard = [em.ev("scalar"), em.ev("vector")]
    nch_list = [0] if _P3_STOP == "ffn0" else list(range(NCH2))
    for nch in nch_list:
        for up in range(5):
            for ui in range(2):
                u = 2 * up + ui
                ub = u % 2
                wldw = [gu_done.get((nch, u - 2)), gu_done.get((nch - 1, u + NU - 2))]
                if nch == 0 and u == 0:
                    wldw += [x1_all, mb_ev, rb_ev, cb_dma]
                wg_ld[(nch, u)] = em.emit("sync", _bld(
                    lambda e, u=u, ub=ub: e.dma_start(
                        out=S["wg_t"][:, ub * 8 * IEX:(ub + 1) * 8 * IEX],
                        in_=_kpm(D["wg_all"][u]))), waits=wldw, dma=True)
                wu_ld[(nch, u)] = em.emit("sync", _bld(
                    lambda e, u=u, ub=ub: e.dma_start(
                        out=S["wu_t"][:, ub * 8 * IEX:(ub + 1) * 8 * IEX],
                        in_=_kpm(D["wu_all"][u]))), waits=wldw, dma=True)
                wd_ld[(nch, u)] = em.emit("sync", _bld(
                    lambda e, u=u, ub=ub: e.dma_start(
                        out=S["wd_t"][:, ub * 4 * DM:(ub + 1) * 4 * DM],
                        in_=_kpm(D["wd_all"][u]))),
                    waits=[dn_ev.get((nch, up - 1)), dn_ev.get((nch - 1, up + 4))]
                    + ([cb_dma, x1_all] if nch == 0 and up == 0 else []),
                    dma=True)
                if u >= 2:
                    cbb_ev[(nch, u)] = em.emit("gpsimd", _bld(
                        lambda e, u=u, nch=nch, ub=ub: e.dma_start(
                            out=S["cbbc_t"][:, ub * W2:(ub + 1) * W2],
                            in_=D["combt_d"][u - 2:u - 1, nch * W2:(nch + 1) * W2]
                            .broadcast_to([128, W2]))),
                        waits=[cb_dma, hm_ev.get((nch, u - 2, 3))], dma=True)
                for it in range(4):
                    gb_ = GB[it % 2]
                    ub_ = UB[it % 2]
                    w0 = [wg_ld[(nch, u)], x1_all, sg_ev.get((nch, u, it - 2)),
                          sg_ev.get((nch, u - 1, 2 + it % 2)),
                          sg_ev.get((nch - 1, NU - 1, 2 + it % 2)),
                          srow2_ev.get(nch - 1) if u == 0 else None]
                    if nch == 0 and u == 0 and it == 0:
                        w0 += ffn_guard
                    for kc in range(8):
                        em.emit("tensor", _bld(
                            lambda e, it=it, kc=kc, ub=ub, nch=nch, gb_=gb_: e.matmul(
                                PB[gb_][:, 0:W2],
                                S["wg_t"][:, ub * 8 * IEX + kc * IEX + it * 128:ub * 8 * IEX + kc * IEX + (it + 1) * 128],
                                S["xqb_t"][:, kc * TQ + nch * W2:kc * TQ + (nch + 1) * W2],
                                start=(kc == 0), stop=(kc == 7))),
                            waits=w0 if kc == 0 else [], inc=(kc == 7))
                    gmm = em.ev("tensor")
                    sg_ev[(nch, u, it)] = em.emit("scalar", _bld(
                        lambda e, it=it, gb_=gb_: e.activation(
                            S["sg_t"][:, it * W2:(it + 1) * W2], PB[gb_][:, 0:W2],
                            AF.Silu, bias=0.0, scale=1.0)), waits=[gmm])
                    w0u = [wu_ld[(nch, u)], hm_ev.get((nch, u - 1, 2 + it % 2)),
                           hm_ev.get((nch, u, it - 2)),
                           srow2_ev.get(nch - 1) if u == 0 else None]
                    if nch == 0 and u == 0 and it == 0:
                        w0u = w0u + ffn_guard
                    if nch > 0 and u <= 1 and it == 0:
                        w0u.append(hm_ev.get((nch - 1, u + 8, 3)))
                    for kc in range(8):
                        em.emit("tensor", _bld(
                            lambda e, it=it, kc=kc, ub=ub, nch=nch, ub_=ub_: e.matmul(
                                PB[ub_][:, 0:W2],
                                S["wu_t"][:, ub * 8 * IEX + kc * IEX + it * 128:ub * 8 * IEX + kc * IEX + (it + 1) * 128],
                                S["xqb_t"][:, kc * TQ + nch * W2:kc * TQ + (nch + 1) * W2],
                                start=(kc == 0), stop=(kc == 7))),
                            waits=w0u if kc == 0 else [], inc=(kc == 7))
                    umm = em.ev("tensor")
                    ev = em.emit("vector", _bld(
                        lambda e, it=it, ub=ub, ub_=ub_: e.tensor_tensor(
                            S["hm_t"][:, (ub * 4 + it) * W2:(ub * 4 + it + 1) * W2],
                            S["sg_t"][:, it * W2:(it + 1) * W2], PB[ub_][:, 0:W2],
                            ALU.mult)),
                        waits=[umm, sg_ev[(nch, u, it)], dn_ev.get((nch, up - 1))])
                    if u >= 2:
                        ev = em.emit("vector", _bld(
                            lambda e, it=it, ub=ub: e.tensor_tensor(
                                S["hm_t"][:, (ub * 4 + it) * W2:(ub * 4 + it + 1) * W2],
                                S["hm_t"][:, (ub * 4 + it) * W2:(ub * 4 + it + 1) * W2],
                                S["cbbc_t"][:, ub * W2:(ub + 1) * W2], ALU.mult)),
                            waits=[cbb_ev[(nch, u)]])
                    hm_ev[(nch, u, it)] = ev
                gu_done[(nch, u)] = em.ev("tensor")
            hm0 = hm_ev[(nch, 2 * up, 3)]
            hm1 = hm_ev[(nch, 2 * up + 1, 3)]
            for mt in range(8):
                bank = DN_B[mt % 2]
                w0 = [wd_ld[(nch, 2 * up)], wd_ld[(nch, 2 * up + 1)], hm0, hm1,
                      r2_ev.get((nch, up, mt - 2)), r2_ev.get((nch, up - 1, mt + 6)),
                      r2_ev.get((nch - 1, 4, mt + 6))]
                if nch == 0 and up == 0 and mt <= 1:
                    w0 += ffn_guard
                first = True
                for ui in range(2):
                    u = 2 * up + ui
                    ub = u % 2
                    for kc in range(4):
                        em.emit("tensor", _bld(
                            lambda e, mt=mt, kc=kc, ub=ub, ui=ui, bank=bank: e.matmul(
                                PB[bank][:, 0:W2],
                                S["wd_t"][:, ub * 4 * DM + kc * DM + mt * 128:ub * 4 * DM + kc * DM + (mt + 1) * 128],
                                S["hm_t"][:, (ub * 4 + kc) * W2:(ub * 4 + kc + 1) * W2],
                                start=(ui == 0 and kc == 0), stop=(ui == 1 and kc == 3))),
                            waits=w0 if first else [], inc=(ui == 1 and kc == 3))
                        first = False
                dmm = em.ev("tensor")
                r2_ev[(nch, up, mt)] = em.emit("vector", _bld(
                    lambda e, mt=mt, nch=nch, bank=bank: e.tensor_tensor(
                        S["r1_t"][:, mt * TQ + nch * W2:mt * TQ + (nch + 1) * W2],
                        S["r1_t"][:, mt * TQ + nch * W2:mt * TQ + (nch + 1) * W2],
                        PB[bank][:, 0:W2], ALU.add)), waits=[dmm])
            dn_ev[(nch, up)] = em.ev("vector")
        for mt in range(8):
            r2_ev[(nch, mt)] = r2_ev[(nch, 4, mt)]
            sq2_ev[(nch, mt)] = em.emit("scalar", _bld(
                lambda e, mt=mt, nch=nch: e.activation(
                    S["sq_t"][:, 0:W2],
                    S["r1_t"][:, mt * TQ + nch * W2:mt * TQ + (nch + 1) * W2],
                    AF.Square, bias=0.0, scale=1.0)),
                waits=[r2_ev[(nch, mt)], s2_mm.get((nch, mt - 1)), s2_mm.get((nch - 1, 7))])
            em.emit("tensor", _bld(
                lambda e, mt=mt, nch=nch: e.matmul(
                    PB[SUM2_B][0:1, 0:W2], S["ones_rt"][:, 0:1],
                    S["r1_t"][:, mt * TQ + nch * W2:mt * TQ + (nch + 1) * W2],
                    start=(mt == 0), stop=(mt == 7))),
                waits=[r2_ev[(nch, mt)], srow2_ev.get(nch - 1)] + (ffn_guard if nch == 0 and mt == 0 else []))
            em.emit("tensor", _bld(
                lambda e, mt=mt: e.matmul(
                    PB[SQ2_B][0:1, 0:W2], S["onesb_t"][:, 0:1], S["sq_t"][:, 0:W2],
                    start=(mt == 0), stop=(mt == 7))),
                waits=[sq2_ev[(nch, mt)]] + (ffn_guard if nch == 0 and mt == 0 else []))
            s2_mm[(nch, mt)] = em.ev("tensor")
        smm = em.ev("tensor")
        ev1 = em.emit("scalar", _bld(lambda e: e.activation(
            S["srow_t"][0:1, 0:W2], PB[SUM2_B][0:1, 0:W2], AF.Copy, bias=0.0, scale=1.0)),
            waits=[smm])
        ev2 = em.emit("scalar", _bld(lambda e: e.activation(
            S["srow2_t"][0:1, 0:W2], PB[SQ2_B][0:1, 0:W2], AF.Copy, bias=0.0, scale=1.0)))
        sa = em.emit("gpsimd", _bld(lambda e, nch=nch: e.dma_start(
            out=D["srow_d"][0:1, nch * W2:(nch + 1) * W2], in_=S["srow_t"][0:1, 0:W2])),
            waits=[ev1], dma=True)
        sb = em.emit("gpsimd", _bld(lambda e, nch=nch: e.dma_start(
            out=D["srow_d"][1:2, nch * W2:(nch + 1) * W2], in_=S["srow2_t"][0:1, 0:W2])),
            waits=[ev2], dma=True)
        srow2_ev[nch] = [sa, sb]

    if _P3_STOP in ("ffn0", "ffnall"):
        return
    ln_rows(ctx, "srow_d", "mrow_d", [srow2_ev[n] for n in range(NCH2)])
    mb2 = em.emit("gpsimd", _bld(lambda e: e.dma_start(
        out=S["mbc_t"][:], in_=D["mrow_d"][0:1, :].broadcast_to([128, TQ]))),
        waits=[ctx["lnrow_ev"]], dma=True)
    rb2 = em.emit("gpsimd", _bld(lambda e: e.dma_start(
        out=S["rbc2_t"][:], in_=D["mrow_d"][1:2, :].broadcast_to([128, TQ]))),
        waits=[ctx["lnrow_ev"]], dma=True)
    if _P3_STOP == "ln2":
        return
    out_st = {}
    xo_ev = {}
    tr_ev = {}
    for tt in range(NTT):
        ob = 0
        for mt in range(8):
            w = [mb2, rb2, r2_ev[(NCH2 - 1, mt)], s2_mm[(NCH2 - 1, 7)], tr_ev.get((tt - 1, mt))]
            if mt == 0:
                w.append(out_st.get(tt - 2))
            e1 = em.emit("vector", _bld(lambda e, tt=tt, mt=mt: e.tensor_tensor(
                S["x2_t"][:, mt * 128:(mt + 1) * 128],
                S["r1_t"][:, mt * TQ + tt * 128:mt * TQ + (tt + 1) * 128],
                S["mbc_t"][:, tt * 128:(tt + 1) * 128], ALU.subtract)), waits=w)
            e2 = em.emit("vector", _bld(lambda e, tt=tt, mt=mt: e.tensor_tensor(
                S["x2_t"][:, mt * 128:(mt + 1) * 128],
                S["x2_t"][:, mt * 128:(mt + 1) * 128],
                S["rbc2_t"][:, tt * 128:(tt + 1) * 128], ALU.mult)))
            evt = em.emit("tensor", _bld(lambda e, mt=mt: e.transpose(
                PB[4 + (mt % 2)][:, (mt // 2) * 128:(mt // 2 + 1) * 128],
                S["x2_t"][:, mt * 128:(mt + 1) * 128], S["idf_t"][:])),
                waits=[e2, xo_ev.get((tt - 1, 6 + (mt % 2))), xo_ev.get((tt, mt - 2))])
            tr_ev[(tt, mt)] = evt
            xo_ev[(tt, mt)] = em.emit("scalar", _bld(lambda e, mt=mt, ob=ob: e.activation(
                S["xo_t"][:, ob * DM + mt * 128:ob * DM + (mt + 1) * 128],
                PB[4 + (mt % 2)][:, (mt // 2) * 128:(mt // 2 + 1) * 128],
                AF.Copy, bias=0.0, scale=1.0)), waits=[evt, out_st.get(tt - 1)])
        oev = em.ev("scalar")
        out_st[tt] = em.emit("gpsimd", _bld(lambda e, tt=tt, ob=ob: e.dma_start(
            out=D["out"][tt * 128:(tt + 1) * 128, :], in_=S["xo_t"][:, ob * DM:(ob + 1) * DM])),
            waits=[oev], dma=True)


def ln_rows(ctx, src, dst, waits):
    D, S, em = ctx["D"], ctx["S"], ctx["em"]
    l1 = em.emit("sync", _bld(lambda e: e.dma_start(
        out=S["m16_t"][:, 0:16], in_=D[src][0:1, :].rearrange("o (p w) -> (o p) w", p=128))),
        waits=waits, dma=True)
    l2 = em.emit("sync", _bld(lambda e: e.dma_start(
        out=S["m16_t"][:, 16:32], in_=D[src][1:2, :].rearrange("o (p w) -> (o p) w", p=128))),
        waits=waits, dma=True)
    e1 = em.emit("vector", _bld(lambda e: e.tensor_scalar(
        S["m16_t"][:, 0:16], S["m16_t"][:, 0:16], 1.0 / DM, None, ALU.mult)),
        waits=[l1])
    e2 = em.emit("vector", _bld(lambda e: e.tensor_tensor(
        S["m16_t"][:, 32:48], S["m16_t"][:, 0:16], S["m16_t"][:, 0:16], ALU.mult)),
        waits=[l2])
    e3 = em.emit("vector", _bld(lambda e: e.scalar_tensor_tensor(
        S["m16_t"][:, 48:64], S["m16_t"][:, 16:32], 1.0 / DM, S["m16_t"][:, 32:48],
        ALU.mult, ALU.subtract)))
    e4 = em.emit("scalar", _bld(lambda e: e.activation(
        S["m16_t"][:, 48:64], S["m16_t"][:, 48:64], AF.Sqrt, bias=EPS, scale=1.0)),
        waits=[e3])
    e5 = em.emit("vector", _bld(lambda e: e.reciprocal(
        S["m16_t"][:, 48:64], S["m16_t"][:, 48:64])), waits=[e4])
    s1 = em.emit("gpsimd", _bld(lambda e: e.dma_start(
        out=D[dst][0:1, :].rearrange("o (p w) -> (o p) w", p=128), in_=S["m16_t"][:, 0:16])),
        waits=[e1], dma=True)
    s2 = em.emit("gpsimd", _bld(lambda e: e.dma_start(
        out=D[dst][1:2, :].rearrange("o (p w) -> (o p) w", p=128), in_=S["m16_t"][:, 48:64])),
        waits=[e5], dma=True)
    ctx["lnrow_ev"] = s2
    em.wait_only("gpsimd", [s1])


def _prep_core_inputs(c, I):
    b, q = c // 4, c % 4
    sl = slice(q * DQ, (q + 1) * DQ)
    f = np.float32
    bf = ml_dtypes.bfloat16
    cont = np.ascontiguousarray

    d = {}
    xb = np.asarray(I["L0_attn_hidden_states"][b], dtype=f)
    d["x"] = cont(xb)
    d["xq"] = cont(xb[q * TQ:(q + 1) * TQ])
    ipw = np.asarray(I["L0_attn_in_proj_w"], dtype=f)
    d["w_inx"] = cont(ipw[sl].T)
    d["w_inz"] = cont(ipw[2048 + q * DQ:2048 + (q + 1) * DQ].T)
    d["convw"] = cont(np.asarray(I["L0_attn_conv1d_w"], dtype=f)[sl].T)
    d["convb"] = cont(np.asarray(I["L0_attn_conv1d_b"], dtype=f)[sl].reshape(NDT, 128).T)
    d["w_xp"] = cont(np.asarray(I["L0_attn_x_proj_w"], dtype=f)[:, sl].T.astype(bf))
    d["w_dt"] = cont(np.asarray(I["L0_attn_dt_proj_w"], dtype=f)[sl].T)
    d["dtb"] = cont(np.asarray(I["L0_attn_dt_proj_b"], dtype=f)[sl].reshape(NDT, 128).T)
    d["acol"] = cont(np.asarray(I["L0_attn_A_log"], dtype=f)[sl]
                     .reshape(NDT, 128, DS).transpose(1, 0, 2).reshape(128, NDT * DS))
    d["dcol"] = cont(np.asarray(I["L0_attn_D"], dtype=f)[sl].reshape(NDT, 128).T)
    d["nrmw"] = cont(np.asarray(I["L0_attn_norm_weight"], dtype=f).reshape(-1, 128).T)
    d["w_out"] = cont(np.asarray(I["L0_attn_out_proj_w"], dtype=f).T.astype(bf))
    d["w_gate"] = cont(np.asarray(I["L0_ffn_gate_weight"], dtype=f).T)
    wg = np.asarray(I["L0_ffn_w_gate"], dtype=f)
    wu = np.asarray(I["L0_ffn_w_up"], dtype=f)
    wd = np.asarray(I["L0_ffn_w_down"], dtype=f)
    wsg = np.asarray(I["L0_ffn_ws_gate"], dtype=f)
    wsu = np.asarray(I["L0_ffn_ws_up"], dtype=f)
    wsd = np.asarray(I["L0_ffn_ws_down"], dtype=f)
    g_all = np.concatenate([wsg.reshape(2, IEX, DM), wg], axis=0)
    u_all = np.concatenate([wsu.reshape(2, IEX, DM), wu], axis=0)
    dn_all = np.concatenate([wsd.T.reshape(2, IEX, DM), wd.transpose(0, 2, 1)], axis=0)
    d["wg_all"] = cont(g_all.transpose(0, 2, 1).astype(bf))
    d["wu_all"] = cont(u_all.transpose(0, 2, 1).astype(bf))
    d["wd_all"] = cont(dn_all.astype(bf))
    d["idbf"] = np.eye(128, dtype=bf)
    d["idf"] = np.eye(128, dtype=f)
    d["ones_r"] = np.ones((128, 8), dtype=f)
    cvals = np.zeros(8, dtype=f)
    cvals[:4] = [EPS, 1.0, -1.0, 1.0 / DM]
    d["consts"] = np.broadcast_to(cvals, (128, 8)).copy()
    yidx = np.empty((128, 64), dtype=np.int32)
    for kc in range(16):
        for nch in range(4):
            yidx[:, kc * 4 + nch] = (kc * 128 + np.arange(128)) * (T // 512) + q * 4 + nch
    d["yidx"] = yidx
    return d


def kernel(**inputs):
    nc, D = build_program()
    in_maps = [_prep_core_inputs(c, inputs) for c in range(NCORE)]
    res = run_bass_kernel_spmd(nc, in_maps, core_ids=list(range(NCORE)))
    out = np.empty((B, T, DM), dtype=np.float32)
    for c in range(NCORE):
        b, q = c // 4, c % 4
        out[b, q * TQ:(q + 1) * TQ] = res.results[c]["out"]
    return out


# version 14
# speedup vs baseline: 1.0797x; 1.0065x over previous
import numpy as np
import ml_dtypes
import concourse.bass as bass
import concourse.mybir as mybir
from concourse.bass import IndirectOffsetOnAxis
from concourse.bass_utils import run_bass_kernel_spmd

F32 = mybir.dt.float32
F32R = mybir.dt.float32r
BF16 = mybir.dt.bfloat16
I32 = mybir.dt.int32
AF = mybir.ActivationFunctionType
ALU = mybir.AluOpType

B, T, DM = 2, 8192, 1024
DI, DS, DTR, DC = 2048, 16, 64, 4
EPS = 1e-5
NCORE = 8
DQ = DI // 4
NDT = DQ // 128
TQ = T // 4
E = 8
IEX = 512
NU = 10
SCALE_MOE = 2.5
TC = 1024
NTC = T // TC
TG = 512
NTG = T // TG
GROUPS = [[0, 1, 2, 3], [4, 5, 6, 7]]

_DBG = []
_PHASES = ("p1", "dt", "scan", "p3")
_P3_STOP = None

ENGINES = ("sync", "gpsimd", "scalar", "vector", "tensor")


class Em:

    NCH = 4

    def __init__(self, nc):
        self.nc = nc
        self.streams = {e: [] for e in ENGINES}
        self.sems = {}
        self.counts = {}
        for e in ENGINES:
            cm = nc.semaphore(f"S_{e}_c")
            self.sems[(e, "c")] = cm.__enter__()
            self.counts[(e, "c")] = 0
        for e in ("sync", "gpsimd", "scalar"):
            for i in range(self.NCH):
                cm = nc.semaphore(f"S_{e}_q{i}")
                self.sems[(e, f"q{i}")] = cm.__enter__()
                self.counts[(e, f"q{i}")] = 0
        self.rr = {e: 0 for e in ENGINES}
        self.waited = {e: {} for e in ENGINES}

    def ev(self, engine, kind="c"):
        return (engine, kind, self.counts[(engine, kind)])

    def qsnap(self, engine):
        return [(engine, f"q{i}", self.counts[(engine, f"q{i}")])
                for i in range(self.NCH)]

    def _wl(self, engine, kind, waits):
        wl = []
        flat = []
        for w in waits:
            if w is None:
                continue
            if isinstance(w, list):
                flat.extend(x for x in w if x is not None)
            else:
                flat.append(w)
        for w in flat:
            (pe_, pk, val) = w
            if pe_ == engine and pk == "c" and kind == "c" and engine == "tensor":
                continue
            key = (pe_, pk)
            if val <= self.waited[engine].get(key, 0):
                continue
            self.waited[engine][key] = val
            wl.append((self.sems[key], val))
        return wl

    def emit(self, engine, build, waits=(), dma=False, inc=True, inc_amt=None):
        if dma:
            ci = self.rr[engine] % self.NCH
            self.rr[engine] += 1
            kind = f"q{ci}"
            amt = 16
            prev = self.counts[(engine, kind)]
            wl = self._wl(engine, kind, waits)
            key = (engine, kind)
            if prev > self.waited[engine].get(key, 0):
                self.waited[engine][key] = prev
                wl.insert(0, (self.sems[key], prev))
            self.counts[key] = prev + amt
            tgt = prev + amt
            sem = self.sems[key]
        else:
            kind = "c"
            amt = inc_amt if inc_amt is not None else 1
            wl = self._wl(engine, kind, waits)
            if engine in ("vector", "scalar", "gpsimd"):
                key = (engine, "c")
                prev = self.counts[key]
                if prev > self.waited[engine].get(key, 0):
                    self.waited[engine][key] = prev
                    wl.insert(0, (self.sems[key], prev))
            if inc:
                self.counts[(engine, kind)] += amt
            tgt = self.counts[(engine, kind)]
            sem = self.sems[(engine, kind)]

        def run(eng, wl=wl, build=build, sem=sem, amt=amt, inc=inc):
            for (h, v) in wl:
                eng.wait_ge(h, v)
            ins = build(eng)
            if inc and ins is not None:
                ins.then_inc(sem, amt)
            return ins

        self.streams[engine].append(run)
        return (engine, kind, tgt)

    def wait_only(self, engine, waits):
        wl = self._wl(engine, "c", [w for w in waits if w is not None])
        if not wl:
            return

        def run(eng, wl=wl):
            for (h, v) in wl:
                eng.wait_ge(h, v)

        self.streams[engine].append(run)

    def barrier(self):
        evs = [(e, k, self.counts[(e, k)]) for (e, k) in self.counts]
        for e in ENGINES:
            self.wait_only(e, evs)


def _bld(fn, *a, **k):
    return lambda eng: fn(eng, *a, **k)


def _kpm(ap, p=128):
    return ap.rearrange("(kc p) m -> kc p m", p=p).transpose([1, 0, 2])


def build_program():
    nc = bass.Bass("TRN2", target_bir_lowering=False, debug=False)
    D, S = {}, {}

    def din(name, shape, dt=F32):
        D[name] = nc.dram_tensor(name, shape, dt, kind="ExternalInput").ap()

    def dout(name, shape, dt=F32):
        D[name] = nc.dram_tensor(name, shape, dt, kind="ExternalOutput").ap()

    def dint(name, shape, dt=F32):
        D[name] = nc.dram_tensor(name, shape, dt).ap()

    din("x", [T, DM])
    din("xq", [TQ, DM])
    din("w_inx", [DM, DQ])
    din("w_inz", [DM, DQ])
    din("convw", [DC, DQ])
    din("convb", [128, NDT])
    din("w_xp", [DQ, 96], BF16)
    din("w_dt", [DTR, DQ], F32R)
    din("dtb", [128, NDT])
    din("acol", [128, NDT * DS])
    din("dcol", [128, NDT])
    din("nrmw", [128, DM // 128])
    din("w_out", [DI, DM], BF16)
    din("w_gate", [DM, E], F32R)
    din("wg_all", [NU, DM, IEX], BF16)
    din("wu_all", [NU, DM, IEX], BF16)
    din("wd_all", [NU, IEX, DM], BF16)
    din("idbf", [128, 128], BF16)
    din("idf", [128, 128])
    din("ones_r", [128, 8], F32R)
    din("yidx", [128, 64], I32)
    din("consts", [128, 8])
    dout("out", [TQ, DM])

    dint("rstd_row", [1, T])
    dint("ppart", [2, 96, T // 2])
    dint("pred", [2, 96, T // 2])
    dint("dt_d", [DQ, T])
    dint("xc_d", [DQ, T], BF16)
    dint("sz_d", [DQ, T], BF16)
    dint("predb", [2, 32, T // 2], BF16)
    dint("ygat_in", [DQ, T], BF16)
    dint("ygat_all", [DI, T], BF16)
    dint("srow_d", [2, TQ])
    dint("mrow_d", [2, TQ])
    dint("combt_d", [E, TQ])

    dbg_shapes = {
        "xc": ([DQ, T], BF16), "sz": ([DQ, T], BF16), "ppart": ([96, T], F32),
        "pred": ([96, T], F32), "dt": ([DQ, T], F32), "y": ([DQ, T], BF16),
        "ygall": ([DI, T], BF16), "r1": ([DM, TQ], F32), "x1": ([DM, TQ], F32),
        "comb": ([E, TQ], F32), "sc": ([128, 16 * E], F32),
    }
    for nm in _DBG:
        sh, dt_ = dbg_shapes[nm]
        dout("dbg_" + nm, sh, dt_)

    SB_BASE = 16512
    SB_END = 229376
    cur = [SB_BASE]

    def salloc(name, shape, dt=F32, at=None):
        szb = {F32: 4, F32R: 4, BF16: 2, I32: 4}[dt]
        nbytes = int(np.prod(shape[1:])) * szb
        off = ((cur[0] if at is None else at) + 63) & ~63
        if at is None:
            cur[0] = off + nbytes
        assert off + nbytes <= SB_END, (name, off, nbytes)
        S[name] = nc.alloc_sbuf_tensor_at(name, shape, dt, offset=off)
        return off + nbytes

    for nm, sh, dt_ in [
        ("convb_t", [128, NDT], F32), ("dtb_t", [128, NDT], F32),
        ("acol_t", [128, NDT * DS], F32), ("dcol_t", [128, NDT], F32),
        ("nrmw_t", [128, DM // 128], F32), ("idbf_t", [128, 128], BF16),
        ("idf_t", [128, 128], F32), ("ones_rt", [128, 8], F32R),
        ("onesb_t", [128, 8], BF16), ("st_t", [128, NDT * DS], F32),
        ("yidx_t", [128, 64], I32),
        ("spa_t", [128, TG], F32), ("spb_t", [128, 2 * TG], F32),
        ("spc_t", [128, 2 * TG], F32), ("sq_t", [128, TG], BF16),
        ("srow_t", [1, TG], F32), ("srow2_t", [1, TG], F32),
        ("m16_t", [128, 64], F32),
        ("tiny_t", [128, 64], F32),
        ("gwt_t", [128, 8 * E], F32R), ("xstage_t", [128, 2 * DM], F32),
        ("w_dt_t", [64, DQ], F32R), ("dtraw_t", [64, 2 * TG], F32R),
        ("m1_t", [128, 16], F32), ("m2_t", [128, 16], F32),
        ("msk_t", [128, 8], F32), ("msk2_t", [128, 8], F32),
        ("lg_t", [128, 16 * E], F32), ("sc_t", [128, 16 * E], F32),
        ("cb_t", [128, 16 * E], F32), ("cbp_t", [E, TQ], F32),
    ]:
        salloc(nm, sh, dt_)
    salloc("consts_t", [128, 8], F32)
    nc.const_aps.register(F32, EPS, S["consts_t"][:, 0:1])
    nc.const_aps.register(F32, 1.0, S["consts_t"][:, 1:2])
    nc.const_aps.register(F32, -1.0, S["consts_t"][:, 2:3])
    nc.const_aps.register(F32, 1.0 / DM, S["consts_t"][:, 3:4])
    arena = (cur[0] + 63) & ~63

    a = arena
    a = salloc("wtap_t", [128, DC * 8 * DQ], BF16, at=a)
    a = salloc("w_inz_t", [128, 8 * DQ], BF16, at=a)
    a = salloc("cwbc_t", [128, DC * DQ], F32, at=a)
    a = salloc("wtmp_t", [128, DQ], F32, at=a)
    a = salloc("w_xp_t", [128, NDT * 96], BF16, at=a)
    a = salloc("xt_t", [128, 2 * 4 * DM], F32, at=a)
    a = salloc("hn_t", [128, 2 * 8 * (TG + 3)], BF16, at=a)
    a = salloc("rbc_t", [128, 2 * TG], F32, at=a)
    a = salloc("xcg_t", [128, NDT * TG], BF16, at=a)
    a = salloc("szg_t", [128, TG], BF16, at=a)
    a = salloc("rstdc_t", [128, 64], F32, at=a)
    a = salloc("rsev_t", [4, 2 * 128], F32, at=a)
    a = salloc("projs_t", [96, 2 * TG], F32, at=a)
    assert a <= SB_END


    a = arena
    a = salloc("dt_c", [128, 2 * NDT * TC], F32, at=a)
    a = salloc("xcc", [128, 2 * NDT * TC], BF16, at=a)
    a = salloc("szc", [128, 2 * NDT * TC], BF16, at=a)
    a = salloc("dtx_c", [128, NDT * TC], BF16, at=a)
    a = salloc("bbc", [128, 8 * TC], BF16, at=a)
    a = salloc("cbc", [128, 8 * TC], BF16, at=a)
    a = salloc("da_t", [128, 4 * TC], F32, at=a)
    a = salloc("dbx_t", [128, 4 * TC], BF16, at=a)
    a = salloc("h_t", [128, 4 * TC], BF16, at=a)
    a = salloc("hc_t", [128, 4 * TC], BF16, at=a)
    a = salloc("y1_t", [128, TC], F32, at=a)
    a = salloc("yg_t", [128, 2 * TC], BF16, at=a)
    assert a <= SB_END

    a = arena
    a = salloc("xqb_t", [128, 8 * TQ], BF16, at=a)
    XQ_OFF = arena
    a = salloc("r1_t", [128, 8 * TQ], F32R, at=a)
    R1_OFF = a - 8 * TQ * 4
    a3 = a
    a = salloc("w_out_t", [128, 16 * DM], BF16, at=a3)
    YR_OFF = a
    a = salloc("yr_t", [128, 2 * 16 * 512], BF16, at=a)
    assert a <= SB_END
    a = a3
    WG_OFF = a
    a = salloc("wg_t", [128, 2 * 8 * IEX], BF16, at=a)
    a = salloc("wu_t", [128, 2 * 8 * IEX], BF16, at=a)
    a = salloc("wd_t", [128, 2 * 4 * DM], BF16, at=a)
    a = salloc("sg_t", [128, 4 * 512], BF16, at=a)
    a = salloc("hm_t", [128, 2 * 4 * 512], BF16, at=a)
    a = salloc("cbbc_t", [128, 2 * 512], BF16, at=a)
    a = salloc("x2_t", [128, 8 * 128], F32, at=a)
    a = salloc("xo_t", [128, DM], F32, at=a)
    assert a <= SB_END, ("ffn arena", a)
    salloc("mbc_t", [128, TQ], F32, at=WG_OFF)
    salloc("rbc2_t", [128, TQ], F32, at=WG_OFF + TQ * 4)
    salloc("cbp_t", [E, TQ], F32, at=YR_OFF)

    x1b = S["xqb_t"]

    PB = [nc.alloc_psum_tensor(f"pb{i}", [128, 512], F32) for i in range(8)]

    em = Em(nc)
    ctx = dict(nc=nc, D=D, S=S, PB=PB, em=em)

    emit_init(ctx)
    if "p1" in _PHASES:
        emit_phase1(ctx)
    if "dt" in _PHASES:
        ctx["dt_done"] = {}
        emit_dt_init(ctx)
    if "scan" in _PHASES:
        emit_scan(ctx)
    if "p3" in _PHASES:
        emit_phase3(ctx)
    em.barrier()

    with nc.Block() as block:
        @block.sync
        def _(eng):
            for f in em.streams["sync"]:
                f(eng)

        @block.gpsimd
        def _(eng):
            for f in em.streams["gpsimd"]:
                f(eng)

        @block.scalar
        def _(eng):
            for f in em.streams["scalar"]:
                f(eng)

        @block.vector
        def _(eng):
            for f in em.streams["vector"]:
                f(eng)

        @block.tensor
        def _(eng):
            for f in em.streams["tensor"]:
                f(eng)

    return nc, D


def emit_init(ctx):
    D, S, em = ctx["D"], ctx["S"], ctx["em"]
    for nm_s, nm_d in [("convb_t", "convb"), ("dtb_t", "dtb"), ("acol_t", "acol"),
                       ("dcol_t", "dcol"), ("nrmw_t", "nrmw"), ("idbf_t", "idbf"),
                       ("idf_t", "idf"), ("ones_rt", "ones_r"), ("yidx_t", "yidx"),
                       ("consts_t", "consts")]:
        em.emit("sync", _bld(lambda e, o=S[nm_s][:], i=D[nm_d]: e.dma_start(out=o, in_=i)),
                dma=True)
    em.emit("sync", _bld(lambda e: e.dma_start(out=S["gwt_t"][:], in_=_kpm(D["w_gate"]))),
            dma=True)
    ctx["init_ev"] = em.qsnap("sync")
    em.emit("gpsimd", _bld(lambda e: e.dma_start(
        out=S["onesb_t"][:], in_=D["ones_r"].bitcast(F32))), dma=True)
    ctx["init_ev_g"] = em.qsnap("gpsimd")
    e1 = em.emit("scalar", _bld(lambda e: e.activation(
        S["acol_t"][:], S["acol_t"][:], AF.Exp, bias=0.0, scale=1.0)),
        waits=[ctx["init_ev"]])
    em.emit("vector", _bld(lambda e: e.tensor_scalar(
        S["acol_t"][:], S["acol_t"][:], -1.0, None, ALU.mult)), waits=[e1])
    ctx["acol_ev"] = em.ev("vector")


def emit_phase1(ctx):
    D, S, PB, em = ctx["D"], ctx["S"], ctx["PB"], ctx["em"]
    hnW = TG + 3

    ev_z = em.emit("gpsimd", _bld(lambda e: e.dma_start(
        out=S["w_inz_t"][:], in_=_kpm(D["w_inz"]))),
        dma=True)
    for kc in range(8):
        em.emit("vector", _bld(lambda e, kc=kc: e.tensor_scalar(
            S["w_inz_t"][:, kc * DQ:(kc + 1) * DQ], S["w_inz_t"][:, kc * DQ:(kc + 1) * DQ],
            S["nrmw_t"][:, kc:kc + 1], None, ALU.mult)), waits=[ev_z, ctx["init_ev"]])
    ctx["winz_ev"] = em.ev("vector")

    ev_cw = em.emit("gpsimd", _bld(lambda e: e.dma_start(
        out=S["cwbc_t"][:],
        in_=D["convw"].rearrange("k m -> (k m)").unsqueeze(0).broadcast_to([128, DC * DQ]))),
        dma=True)
    for kc in range(8):
        lv = em.emit("sync", _bld(lambda e, kc=kc: e.dma_start(
            out=S["wtmp_t"][:], in_=D["w_inx"][kc * 128:(kc + 1) * 128, :])),
            waits=[em.ev("vector")], dma=True)
        em.emit("vector", _bld(lambda e, kc=kc: e.tensor_scalar(
            S["wtmp_t"][:], S["wtmp_t"][:], S["nrmw_t"][:, kc:kc + 1], None, ALU.mult)),
            waits=[lv, ctx["init_ev"]])
        for k in range(DC):
            em.emit("vector", _bld(lambda e, kc=kc, k=k: e.tensor_tensor(
                S["wtap_t"][:, (k * 8 + kc) * DQ:(k * 8 + kc + 1) * DQ],
                S["wtmp_t"][:], S["cwbc_t"][:, k * DQ:(k + 1) * DQ], ALU.mult)),
                waits=[ev_cw])
    ctx["wtap_ev"] = em.ev("vector")
    ev_xp = em.emit("sync", _bld(lambda e: e.dma_start(
        out=S["w_xp_t"][:], in_=_kpm(D["w_xp"]))),
        dma=True)

    hn = [S["hn_t"][:, 0:8 * hnW].rearrange("p (kc w) -> p kc w", kc=8),
          S["hn_t"][:, 8 * hnW:16 * hnW].rearrange("p (kc w) -> p kc w", kc=8)]
    for b_ in range(2):
        for kc in range(8):
            em.emit("vector", _bld(lambda e, b_=b_, kc=kc: e.tensor_copy(
                S["hn_t"][:, (b_ * 8 + kc) * hnW:(b_ * 8 + kc) * hnW + 3],
                S["consts_t"][:, 4:5].broadcast_to([128, 3]))),
                waits=[ctx["init_ev"]])
    hz_ev = em.ev("vector")

    xld = {}
    rbc_ev = {}
    hn_ev = {}
    tail_ev = {}
    pe_hn = {}
    xc_ev = {}
    sz_st = {}
    xc_st = {}
    pp_st = {}
    CONV_B = (0, 1)
    Z_B = (2, 3)
    XT_B = (4, 5)
    PP_B = 6

    rst_ev = {}
    rrow_st = {}
    ca_ev = {}

    def emit_xld(g):
        gb = g % 2
        for tt in range(4):
            w = [xld.get((g - 2, tt)), hn_ev.get(g - 2)]
            xld[(g, tt)] = em.emit("sync", _bld(
                lambda e, g=g, tt=tt, gb=gb: e.dma_start(
                    out=S["xt_t"][:, (gb * 4 + tt) * DM:(gb * 4 + tt + 1) * DM],
                    in_=D["x"][g * TG + tt * 128:g * TG + (tt + 1) * 128, :])),
                waits=w, dma=True)

    def emit_chainA(g):
        gb = g % 2
        for tt in range(4):
            buf = S["xt_t"][:, (gb * 4 + tt) * DM:(gb * 4 + tt + 1) * DM]
            em.emit("scalar", _bld(
                lambda e, buf=buf, gb=gb, tt=tt: e.activation(
                    S["sq_t"][:], buf[:, 0:TG], AF.Square, bias=0.0, scale=1.0,
                    accum_out=S["rstdc_t"][:, gb * 8 + tt:gb * 8 + tt + 1])),
                waits=[xld[(g, tt)], rst_ev.get(g - 2)])
            em.emit("scalar", _bld(
                lambda e, buf=buf: e.activation(
                    S["sq_t"][:], buf[:, TG:DM], AF.Square, bias=0.0, scale=1.0,
                    accum_out=S["tiny_t"][:, 0:1])),
                waits=[em.ev("vector")])
            em.emit("vector", _bld(
                lambda e, gb=gb, tt=tt: e.tensor_tensor(
                    S["rstdc_t"][:, gb * 8 + tt:gb * 8 + tt + 1],
                    S["rstdc_t"][:, gb * 8 + tt:gb * 8 + tt + 1],
                    S["tiny_t"][:, 0:1], ALU.add)), waits=[em.ev("scalar")])
        e1 = em.emit("scalar", _bld(lambda e, gb=gb: e.activation(
            S["rstdc_t"][:, gb * 8:gb * 8 + 4], S["rstdc_t"][:, gb * 8:gb * 8 + 4],
            AF.Sqrt, bias=EPS, scale=1.0 / DM)), waits=[em.ev("vector")])
        ca_ev[g] = em.emit("vector", _bld(lambda e, gb=gb: e.reciprocal(
            S["rstdc_t"][:, gb * 8:gb * 8 + 4], S["rstdc_t"][:, gb * 8:gb * 8 + 4])),
            waits=[e1])

    def emit_rfin(g):
        gb = g % 2
        e3 = em.emit("tensor", _bld(lambda e, gb=gb: e.transpose(
            PB[7][0:4, 0:128], S["rstdc_t"][:, gb * 8:gb * 8 + 4], S["idf_t"][:])),
            waits=[ca_ev[g], ctx["init_ev"], rst_ev.get(g - 1)])
        rst_ev[g] = em.emit("scalar", _bld(lambda e, gb=gb: e.activation(
            S["rsev_t"][0:4, gb * 128:(gb + 1) * 128], PB[7][0:4, 0:128],
            AF.Copy, bias=0.0, scale=1.0)), waits=[e3])
        rrow_st[g] = em.emit("gpsimd", _bld(lambda e, g=g, gb=gb: e.dma_start(
            out=D["rstd_row"].rearrange("o (g2 w) -> (o g2) w", w=128)[4 * g:4 * (g + 1), :],
            in_=S["rsev_t"][0:4, gb * 128:(gb + 1) * 128])),
            waits=[rst_ev[g]], dma=True)
        rbc_ev[g] = em.emit("gpsimd", _bld(
            lambda e, g=g, gb=gb: e.dma_start(
                out=S["rbc_t"][:, gb * TG:(gb + 1) * TG],
                in_=D["rstd_row"][0:1, g * TG:(g + 1) * TG].broadcast_to([128, TG]))),
            waits=[rrow_st[g], hn_ev.get(g - 2)], dma=True)

    emit_xld(0)
    emit_chainA(0)
    emit_rfin(0)
    for g in range(NTG):
        gb = g % 2
        if g + 1 < NTG:
            emit_xld(g + 1)
            emit_chainA(g + 1)

        for kc in range(8):
            xb = XT_B[kc % 2]
            tevs = []
            for tt in range(4):
                w = [xld[(g, tt)]]
                if kc == 0 and tt == 0:
                    w += [pe_hn.get(g - 2), tail_ev.get(g - 1)]
                w.append(hn_ev.get((g, kc - 2)))
                tevs.append(em.emit("tensor", _bld(
                    lambda e, kc=kc, tt=tt, gb=gb, xb=xb: e.transpose(
                        PB[xb][:, tt * 128:(tt + 1) * 128],
                        S["xt_t"][:, (gb * 4 + tt) * DM + kc * 128:(gb * 4 + tt) * DM + (kc + 1) * 128],
                        S["idf_t"][:])), waits=w))
            hn_ev[(g, kc)] = em.emit("vector", _bld(
                lambda e, kc=kc, gb=gb, xb=xb: e.tensor_tensor(
                    hn[gb][:, kc, 3:3 + TG], PB[xb][:, 0:TG],
                    S["rbc_t"][:, gb * TG:(gb + 1) * TG], ALU.mult)),
                waits=[tevs[-1], rbc_ev[g], hz_ev])
        hn_ev[g] = hn_ev[(g, 7)]
        tail_ev[g] = em.emit("vector", _bld(
            lambda e, gb=gb: e.tensor_copy(hn[1 - gb][:, :, 0:3], hn[gb][:, :, TG:TG + 3])),
            waits=[hn_ev[g], pe_hn.get(g - 1)])

        for dti in range(NDT):
            bank = CONV_B[dti % 2]
            w0 = [hn_ev[g], ctx["wtap_ev"], tail_ev.get(g - 1) if g >= 1 else None,
                  xc_ev.get((g, dti - 2)), xc_ev.get((g - 1, dti + 2))]
            first = True
            for k in range(DC):
                for kc in range(8):
                    last = (k == DC - 1 and kc == 7)
                    em.emit("tensor", _bld(
                        lambda e, k=k, kc=kc, dti=dti, gb=gb, bank=bank, first=first, last=last: e.matmul(
                            PB[bank][:, 0:TG],
                            S["wtap_t"][:, (k * 8 + kc) * DQ + dti * 128:(k * 8 + kc) * DQ + (dti + 1) * 128],
                            hn[gb][:, kc, k:k + TG], start=first, stop=last)),
                        waits=w0 if first else [], inc=last)
                    first = False
            mm = em.ev("tensor")
            xc_ev[(g, dti)] = em.emit("scalar", _bld(
                lambda e, dti=dti, bank=bank: e.activation(
                    S["xcg_t"][:, dti * TG:(dti + 1) * TG], PB[bank][:, 0:TG],
                    AF.Silu, bias=S["convb_t"][:, dti:dti + 1], scale=1.0)),
                waits=[mm, xc_st.get((g - 1, dti))])
            xc_st[(g, dti)] = em.emit("gpsimd", _bld(
                lambda e, g=g, dti=dti: e.dma_start(
                    out=D["xc_d"][dti * 128:(dti + 1) * 128, g * TG:(g + 1) * TG],
                    in_=S["xcg_t"][:, dti * TG:(dti + 1) * TG])),
                waits=[xc_ev[(g, dti)]], dma=True)
            ctx.setdefault("xc_done", {})[(g, dti)] = xc_st[(g, dti)]

        if g + 1 < NTG:
            emit_rfin(g + 1)

        for dti in range(NDT):
            bank = Z_B[dti % 2]
            w0 = [hn_ev[g], ctx["winz_ev"],
                  em.ev("scalar") if dti >= 2 else None]
            for kc in range(8):
                em.emit("tensor", _bld(
                    lambda e, kc=kc, dti=dti, gb=gb, bank=bank: e.matmul(
                        PB[bank][:, 0:TG],
                        S["w_inz_t"][:, kc * DQ + dti * 128:kc * DQ + (dti + 1) * 128],
                        hn[gb][:, kc, 3:3 + TG], start=(kc == 0), stop=(kc == 7))),
                    waits=w0 if kc == 0 else [], inc=(kc == 7))
            mm = em.ev("tensor")
            sev = em.emit("scalar", _bld(
                lambda e, bank=bank: e.activation(
                    S["szg_t"][:], PB[bank][:, 0:TG], AF.Silu, bias=0.0, scale=1.0)),
                waits=[mm, sz_st.get((g, dti - 1)), sz_st.get((g - 1, NDT - 1))])
            sz_st[(g, dti)] = em.emit("gpsimd", _bld(
                lambda e, g=g, dti=dti: e.dma_start(
                    out=D["sz_d"][dti * 128:(dti + 1) * 128, g * TG:(g + 1) * TG],
                    in_=S["szg_t"][:])), waits=[sev], dma=True)
            ctx.setdefault("sz_done", {})[(g, dti)] = sz_st[(g, dti)]
        pe_hn[g] = em.ev("tensor")

        w0 = [xc_ev[(g, d2)] for d2 in range(NDT)] + [ev_xp, pp_st.get(g - 2)]
        for dti in range(NDT):
            em.emit("tensor", _bld(
                lambda e, dti=dti: e.matmul(
                    PB[PP_B][0:96, 0:TG], S["w_xp_t"][:, dti * 96:(dti + 1) * 96],
                    S["xcg_t"][:, dti * TG:(dti + 1) * TG],
                    start=(dti == 0), stop=(dti == NDT - 1))),
                waits=w0 if dti == 0 else [], inc=(dti == NDT - 1))
        mm = em.ev("tensor")
        pev = em.emit("scalar", _bld(
            lambda e, gb=gb: e.activation(
                S["projs_t"][0:96, gb * TG:(gb + 1) * TG], PB[PP_B][0:96, 0:TG],
                AF.Copy, bias=0.0, scale=1.0)), waits=[mm, pp_st.get(g - 2)])
        pp_st[g] = em.emit("gpsimd", _bld(
            lambda e, g=g, gb=gb: e.dma_start(
                out=D["ppart"][g // 8, 0:96, (g % 8) * TG:(g % 8 + 1) * TG],
                in_=S["projs_t"][0:96, gb * TG:(gb + 1) * TG])),
            waits=[pev], dma=True)

    if _DBG:
        em.barrier()
    if "xc" in _DBG:
        em.emit("gpsimd", _bld(lambda e: e.dma_start(out=D["dbg_xc"], in_=D["xc_d"])), dma=True)
    if "sz" in _DBG:
        em.emit("gpsimd", _bld(lambda e: e.dma_start(out=D["dbg_sz"], in_=D["sz_d"])), dma=True)
    if "ppart" in _DBG:
        em.emit("gpsimd", _bld(lambda e: e.dma_start(out=D["dbg_ppart"], in_=D["ppart"])), dma=True)
    arh = []
    pbh = []
    for h in range(2):
        ev = em.emit("gpsimd", _bld(lambda e, h=h: e.collective_compute(
            "AllReduce", ALU.add, replica_groups=GROUPS,
            ins=[D["ppart"][h].opt()], outs=[D["pred"][h].opt()])),
            waits=[pp_st[g] for g in range(h * 8, h * 8 + 8)], inc_amt=1)
        arh.append(ev)
        pbh.append(em.emit("gpsimd", _bld(lambda e, h=h: e.dma_start(
            out=D["predb"][h], in_=D["pred"][h, 64:96, :])), waits=[ev], dma=True))
    ctx["arh"] = arh
    ctx["pbh"] = pbh
    ctx["ar_ev"] = arh[1]
    if "pred" in _DBG:
        em.barrier()
        em.emit("gpsimd", _bld(lambda e: e.dma_start(out=D["dbg_pred"].rearrange("r (h w) -> h r w", h=2), in_=D["pred"])), dma=True)
        em.barrier()


def emit_dt_init(ctx):
    D, S, em = ctx["D"], ctx["S"], ctx["em"]
    ctx["dt_w_ev"] = em.emit("sync", _bld(lambda e: e.dma_start(
        out=S["w_dt_t"][0:64, :], in_=D["w_dt"])), waits=[ctx["init_ev"]], dma=True)
    ctx["dt_st"] = {}
    ctx["dt_e5"] = {}
    ctx["dt_ld"] = {}
    ctx["dt_ti"] = [0]
    ctx["dtb_last"] = {}


def emit_dt_group(ctx, g, extra=()):
    D, S, PB, em = ctx["D"], ctx["S"], ctx["PB"], ctx["em"]
    st_ev, e5s, ld = ctx["dt_st"], ctx["dt_e5"], ctx["dt_ld"]
    gb = g % 2
    ld[g] = em.emit("gpsimd", _bld(lambda e, g=g, gb=gb: e.dma_start(
        out=S["dtraw_t"][0:64, gb * TG:(gb + 1) * TG],
        in_=D["pred"][g // 8, 0:64, (g % 8) * TG:(g % 8 + 1) * TG].bitcast(F32R))),
        waits=[st_ev.get((g - 2, NDT - 1)), ctx["arh"][g // 8]], dma=True)
    for dti in range(NDT):
        ti = ctx["dt_ti"][0]
        bank = 2 + ti % 2
        par = ti % 2
        sp = slice(par * TG, (par + 1) * TG)
        em.emit("tensor", _bld(lambda e, dti=dti, gb=gb, bank=bank: e.matmul(
            PB[bank][:, 0:TG], S["w_dt_t"][0:64, dti * 128:(dti + 1) * 128],
            S["dtraw_t"][0:64, gb * TG:(gb + 1) * TG], start=True, stop=True)),
            waits=[ld[g], ctx["dt_w_ev"], st_ev.get((g, dti - 2)), st_ev.get((g - 1, dti + 2)),
                   e5s.get(ti - 2)] + list(extra))
        mm = em.ev("tensor")
        e1 = em.emit("scalar", _bld(lambda e, bank=bank, dti=dti, sp=sp: e.activation(
            S["spb_t"][:, sp], PB[bank][:, 0:TG], AF.Abs,
            bias=S["dtb_t"][:, dti:dti + 1], scale=1.0)),
            waits=[mm, st_ev.get((g, dti - 2)), st_ev.get((g - 1, dti + 2)),
                   e5s.get(ti - 2)])
        e2 = em.emit("scalar", _bld(lambda e, sp=sp: e.activation(
            S["spc_t"][:, sp], S["spb_t"][:, sp], AF.Exp, bias=0.0, scale=-1.0)))
        e3 = em.emit("scalar", _bld(lambda e, sp=sp: e.activation(
            S["spc_t"][:, sp], S["spc_t"][:, sp], AF.Ln, bias=1.0, scale=1.0)))
        e4 = em.emit("scalar", _bld(lambda e, bank=bank, dti=dti: e.activation(
            S["spa_t"][:], PB[bank][:, 0:TG], AF.Relu,
            bias=S["dtb_t"][:, dti:dti + 1], scale=1.0)),
            waits=[e5s.get(ti - 1)])
        e5 = em.emit("vector", _bld(lambda e, sp=sp: e.tensor_tensor(
            S["spb_t"][:, sp], S["spa_t"][:], S["spc_t"][:, sp], ALU.add)),
            waits=[e4])
        e5s[ti] = e5
        st_ev[(g, dti)] = em.emit("gpsimd", _bld(lambda e, g=g, dti=dti, sp=sp: e.dma_start(
            out=D["dt_d"][dti * 128:(dti + 1) * 128, g * TG:(g + 1) * TG],
            in_=S["spb_t"][:, sp])), waits=[e5], dma=True)
        ctx["dt_done"][(g, dti)] = st_ev[(g, dti)]
        ctx["dt_ti"][0] = ti + 1


def emit_scan(ctx):
    D, S, PB, em = ctx["D"], ctx["S"], ctx["PB"], ctx["em"]
    NBC = 8
    NBUF = 4
    ld_dt, ld_xc, ld_sz = {}, {}, {}
    bb_ev, cb_ev = {}, {}
    dtx_ev, da_ev, dbx_ev, scan_ev, hc_ev, pe_ev, st_cp = {}, {}, {}, {}, {}, {}, {}
    y_done, yg_dma = {}, {}

    if "dt" in _PHASES:
        for g0 in range(2):
            emit_dt_group(ctx, g0,
                          extra=([ctx["sz_done"][(NTG - 1, NDT - 1)]] if g0 == 0 else []))
        ctx["dtb_last"][0] = ctx["dt_e5"][ctx["dt_ti"][0] - 1]
    for tc in range(NTC):
        cbuf = tc % 2
        if "dt" in _PHASES and tc < NTC - 1:
            emit_dt_group(ctx, 2 * tc + 2, extra=[y_done.get((tc - 1, 1))])
            emit_dt_group(ctx, 2 * tc + 3)
            ctx["dtb_last"][tc] = ctx["dt_e5"][ctx["dt_ti"][0] - 1]
        for dti in range(NDT):
            w = [y_done.get((tc - 2, dti))]
            dtw = [ctx["dt_done"][(g2, dti)] for g2 in range(tc * TC // TG, (tc + 1) * TC // TG)]
            ld_dt[(tc, dti)] = em.emit("sync", _bld(
                lambda e, tc=tc, dti=dti, cbuf=cbuf: e.dma_start(
                    out=S["dt_c"][:, (cbuf * NDT + dti) * TC:(cbuf * NDT + dti + 1) * TC],
                    in_=D["dt_d"][dti * 128:(dti + 1) * 128, tc * TC:(tc + 1) * TC])),
                waits=w + dtw + [scan_ev.get((tc - 2, DS - 1, dti))], dma=True)
            xcw = [ctx["xc_done"][(g2, dti)] for g2 in range(tc * TC // TG, (tc + 1) * TC // TG)]
            szw = [ctx["sz_done"][(g2, dti)] for g2 in range(tc * TC // TG, (tc + 1) * TC // TG)]
            ld_xc[(tc, dti)] = em.emit("sync", _bld(
                lambda e, tc=tc, dti=dti, cbuf=cbuf: e.dma_start(
                    out=S["xcc"][:, (cbuf * NDT + dti) * TC:(cbuf * NDT + dti + 1) * TC],
                    in_=D["xc_d"][dti * 128:(dti + 1) * 128, tc * TC:(tc + 1) * TC])),
                waits=w + xcw + [ctx["ar_ev"]], dma=True)
            ld_sz[(tc, dti)] = em.emit("sync", _bld(
                lambda e, tc=tc, dti=dti, cbuf=cbuf: e.dma_start(
                    out=S["szc"][:, (cbuf * NDT + dti) * TC:(cbuf * NDT + dti + 1) * TC],
                    in_=D["sz_d"][dti * 128:(dti + 1) * 128, tc * TC:(tc + 1) * TC])),
                waits=w + szw + [ctx["ar_ev"]], dma=True)
        for dti in range(NDT):
            dtx_ev[(tc, dti)] = em.emit("vector", _bld(
                lambda e, dti=dti, cbuf=cbuf: e.tensor_tensor(
                    S["dtx_c"][:, dti * TC:(dti + 1) * TC],
                    S["dt_c"][:, (cbuf * NDT + dti) * TC:(cbuf * NDT + dti + 1) * TC],
                    S["xcc"][:, (cbuf * NDT + dti) * TC:(cbuf * NDT + dti + 1) * TC],
                    ALU.mult)),
                waits=[ld_dt[(tc, dti)], ld_xc[(tc, dti)],
                       scan_ev.get((tc - 1, DS - 1, dti))])

        for s in range(DS):
            slot = s % NBC
            bb_ev[(tc, s)] = em.emit("scalar", _bld(
                lambda e, tc=tc, s=s, slot=slot: e.dma_start(
                    out=S["bbc"][:, slot * TC:(slot + 1) * TC],
                    in_=D["predb"][tc // 4, s:s + 1, (tc % 4) * TC:(tc % 4 + 1) * TC]
                    .broadcast_to([128, TC]))),
                waits=[dbx_ev.get((tc, s - NBC, NDT - 1)),
                       dbx_ev.get((tc - 1, s - NBC + DS, NDT - 1)), ctx["pbh"][tc // 4]], dma=True)
            cb_ev[(tc, s)] = em.emit("scalar", _bld(
                lambda e, tc=tc, s=s, slot=slot: e.dma_start(
                    out=S["cbc"][:, slot * TC:(slot + 1) * TC],
                    in_=D["predb"][tc // 4, 16 + s:17 + s, (tc % 4) * TC:(tc % 4 + 1) * TC]
                    .broadcast_to([128, TC]))),
                waits=[hc_ev.get((tc, s - NBC, NDT - 1)),
                       hc_ev.get((tc - 1, s - NBC + DS, NDT - 1)), ctx["pbh"][tc // 4]], dma=True)

            for dti in range(NDT):
                i = s * NDT + dti
                ib = i % NBUF
                pi = ((i - NBUF) // NDT, (i - NBUF) % NDT) if i >= NBUF else None
                da_ev[(tc, s, dti)] = em.emit("scalar", _bld(
                    lambda e, s=s, dti=dti, ib=ib, cbuf=cbuf: e.activation(
                        S["da_t"][:, ib * TC:(ib + 1) * TC],
                        S["dt_c"][:, (cbuf * NDT + dti) * TC:(cbuf * NDT + dti + 1) * TC],
                        AF.Exp, bias=0.0,
                        scale=S["acol_t"][:, dti * DS + s:dti * DS + s + 1])),
                    waits=[ld_dt[(tc, dti)], ctx["acol_ev"],
                           scan_ev.get((tc,) + pi) if pi else scan_ev.get((tc - 1, DS - 1, (i - NBUF) % NDT)),
                           ] + ([ctx["ar_ev"]] if tc == 0 and s == 0 else []))
                dbx_ev[(tc, s, dti)] = em.emit("vector", _bld(
                    lambda e, s=s, dti=dti, ib=ib, slot=slot: e.tensor_tensor(
                        S["dbx_t"][:, ib * TC:(ib + 1) * TC],
                        S["dtx_c"][:, dti * TC:(dti + 1) * TC],
                        S["bbc"][:, slot * TC:(slot + 1) * TC], ALU.mult)),
                    waits=[dtx_ev[(tc, dti)], bb_ev[(tc, s)]])

                def mkscan(e, tc=tc, dti=dti, s=s, ib=ib):
                    init = 0.0 if tc == 0 else S["st_t"][:, dti * DS + s:dti * DS + s + 1]
                    return e.tensor_tensor_scan(
                        S["h_t"][:, ib * TC:(ib + 1) * TC],
                        S["da_t"][:, ib * TC:(ib + 1) * TC],
                        S["dbx_t"][:, ib * TC:(ib + 1) * TC], init, ALU.mult, ALU.add)
                prev_h = ((tc,) + pi) if pi else (tc - 1, (64 - NBUF + i) // NDT, (64 - NBUF + i) % NDT)
                scan_ev[(tc, s, dti)] = em.emit("vector", mkscan,
                    waits=[da_ev[(tc, s, dti)], st_cp.get((tc - 1, s, dti)),
                           st_cp.get(prev_h)])
                if tc < NTC - 1:
                    st_cp[(tc, s, dti)] = em.emit("gpsimd", _bld(
                        lambda e, s=s, dti=dti, ib=ib: e.tensor_copy(
                            S["st_t"][:, dti * DS + s:dti * DS + s + 1],
                            S["h_t"][:, ib * TC + TC - 1:ib * TC + TC])),
                        waits=[scan_ev[(tc, s, dti)]])
                hc_ev[(tc, s, dti)] = em.emit("vector", _bld(
                    lambda e, s=s, dti=dti, ib=ib, slot=slot: e.tensor_tensor(
                        S["hc_t"][:, ib * TC:(ib + 1) * TC],
                        S["h_t"][:, ib * TC:(ib + 1) * TC],
                        S["cbc"][:, slot * TC:(slot + 1) * TC], ALU.mult)),
                    waits=[cb_ev[(tc, s)],
                           pe_ev.get((tc,) + pi) if pi else pe_ev.get((tc - 1, DS - 1, (i - NBUF) % NDT))])
                wpe = [hc_ev[(tc, s, dti)], ctx["init_ev_g"], ctx["init_ev"]]
                if s == 0:
                    wpe.append(y_done.get((tc - 1, dti)))
                    if dti == 1:
                        wpe.append(ctx["dtb_last"].get(tc))
                for j in range(2):
                    em.emit("tensor", _bld(
                        lambda e, dti=dti, ib=ib, j=j, s=s: e.matmul(
                            PB[2 * dti + j][:, 0:512], S["idbf_t"][:],
                            S["hc_t"][:, ib * TC + j * 512:ib * TC + (j + 1) * 512],
                            start=(s == 0), stop=(s == DS - 1))),
                        waits=wpe if j == 0 else [], inc=(j == 1))
                pe_ev[(tc, s, dti)] = em.ev("tensor")

        for dti in range(NDT):
            w = [pe_ev[(tc, DS - 1, dti)], ld_xc[(tc, dti)], ctx["init_ev"]]
            for j in range(2):
                em.emit("vector", _bld(
                    lambda e, dti=dti, cbuf=cbuf, j=j: e.scalar_tensor_tensor(
                        S["y1_t"][:, j * 512:(j + 1) * 512],
                        S["xcc"][:, (cbuf * NDT + dti) * TC + j * 512:(cbuf * NDT + dti) * TC + (j + 1) * 512],
                        S["dcol_t"][:, dti:dti + 1], PB[2 * dti + j][:, 0:512],
                        ALU.mult, ALU.add)), waits=w if j == 0 else [])
            gi = tc * NDT + dti
            ygs = gi % 2
            pgi = ((gi - 2) // NDT, (gi - 2) % NDT) if gi >= 2 else None
            em.emit("vector", _bld(
                lambda e, dti=dti, cbuf=cbuf, ygs=ygs: e.tensor_tensor(
                    S["yg_t"][:, ygs * TC:(ygs + 1) * TC], S["y1_t"][:],
                    S["szc"][:, (cbuf * NDT + dti) * TC:(cbuf * NDT + dti + 1) * TC],
                    ALU.mult)),
                waits=[ld_sz[(tc, dti)], yg_dma.get(pgi)])
            y_done[(tc, dti)] = em.ev("vector")
            yg_dma[(tc, dti)] = em.emit("sync", _bld(
                lambda e, tc=tc, dti=dti, ygs=ygs: e.dma_start(
                    out=D["ygat_in"][dti * 128:(dti + 1) * 128, tc * TC:(tc + 1) * TC],
                    in_=S["yg_t"][:, ygs * TC:(ygs + 1) * TC])),
                waits=[y_done[(tc, dti)]], dma=True)

    ctx["scan_done"] = [em.ev("vector"), em.ev("scalar"), em.ev("tensor")] + \
        [yg_dma[(NTC - 1, dti)] for dti in range(NDT)]
    if "y" in _DBG:
        em.barrier()
        em.emit("gpsimd", _bld(lambda e: e.dma_start(out=D["dbg_y"], in_=D["ygat_in"])), dma=True)
        em.barrier()
    ctx["ag_ev"] = em.emit("gpsimd", _bld(lambda e: e.collective_compute(
        "AllGather", ALU.bypass, replica_groups=GROUPS,
        ins=[D["ygat_in"].opt()], outs=[D["ygat_all"].opt()])),
        waits=[yg_dma[(tc, dti)] for tc in range(NTC) for dti in range(NDT)], inc_amt=1)
    if "ygall" in _DBG:
        em.barrier()
        em.emit("gpsimd", _bld(lambda e: e.dma_start(out=D["dbg_ygall"], in_=D["ygat_all"])), dma=True)
        em.barrier()


def emit_phase3(ctx):
    D, S, PB, em = ctx["D"], ctx["S"], ctx["PB"], ctx["em"]
    NTT = TQ // 128

    xq_ld = {}
    xq_ev = {}
    for tt in range(NTT):
        bi = tt % 2
        xq_ld[tt] = em.emit("sync", _bld(
            lambda e, tt=tt, bi=bi: e.dma_start(
                out=S["xstage_t"][:, bi * DM:(bi + 1) * DM],
                in_=D["xq"][tt * 128:(tt + 1) * 128, :])),
            waits=[xq_ev.get(tt - 2)], dma=True)
        for kc in range(8):
            xb = 6 + (kc % 2)
            evt = em.emit("tensor", _bld(
                lambda e, tt=tt, kc=kc, bi=bi, xb=xb: e.transpose(
                    PB[xb][:, 0:128],
                    S["xstage_t"][:, bi * DM + kc * 128:bi * DM + (kc + 1) * 128],
                    S["idf_t"][:])),
                waits=[xq_ld[tt], xq_ev.get((tt, kc - 2)), xq_ev.get((tt - 1, kc + 6)), ctx["init_ev"]]
                + (ctx["scan_done"] if tt == 0 and kc < 2 else []))
            xq_ev[(tt, kc)] = em.emit("scalar", _bld(
                lambda e, tt=tt, kc=kc, xb=xb: e.activation(
                    S["xqb_t"][:, kc * TQ + tt * 128:kc * TQ + (tt + 1) * 128],
                    PB[xb][:, 0:128],
                    AF.Copy, bias=0.0, scale=1.0)),
                waits=[evt] + (ctx["scan_done"] if tt == 0 and kc == 0 else []))
        xq_ev[tt] = xq_ev[(tt, 7)]
    xq_all = em.ev("scalar")
    if _P3_STOP == "xq":
        return
    wo_ld = em.emit("sync", _bld(lambda e: e.dma_start(
        out=S["w_out_t"][:], in_=_kpm(D["w_out"]))),
        waits=ctx["scan_done"], dma=True)

    yld = {}
    r1_ev = {}
    sq_ev = {}
    st_mm = {}
    srow_ev = {}
    OP_B = (0, 1)
    SUM_B, SQ_B = 2, 3
    for nch in range(4):
        nb = nch % 2
        for kc in range(16):
            yld[(nch, kc)] = em.emit("gpsimd", _bld(
                lambda e, nch=nch, kc=kc, nb=nb: e.indirect_dma_start(
                    out=S["yr_t"][:, (nb * 16 + kc) * 512:(nb * 16 + kc + 1) * 512],
                    out_offset=None,
                    in_=D["ygat_all"].rearrange("d (blk w) -> (d blk) w", w=512),
                    in_offset=IndirectOffsetOnAxis(
                        S["yidx_t"][:, kc * 4 + nch:kc * 4 + nch + 1], axis=0))),
                waits=[ctx["init_ev"], ctx["ag_ev"], r1_ev.get((nch - 2, 7))], dma=True)
        for mt in range(8):
            bank = OP_B[mt % 2]
            w0 = [yld[(nch, kc)] for kc in range(16)]
            w0 += [wo_ld, r1_ev.get((nch, mt - 2)), r1_ev.get((nch - 1, mt + 6))]
            for kc in range(16):
                em.emit("tensor", _bld(
                    lambda e, mt=mt, kc=kc, nb=nb, bank=bank: e.matmul(
                        PB[bank][:, 0:512],
                        S["w_out_t"][:, kc * DM + mt * 128:kc * DM + (mt + 1) * 128],
                        S["yr_t"][:, (nb * 16 + kc) * 512:(nb * 16 + kc + 1) * 512],
                        start=(kc == 0), stop=(kc == 15))),
                    waits=w0 if kc == 0 else [], inc=(kc == 15))
            mm = em.ev("tensor")
            r1_ev[(nch, mt)] = em.emit("vector", _bld(
                lambda e, mt=mt, nch=nch, bank=bank: e.scalar_tensor_tensor(
                    S["r1_t"][:, mt * TQ + nch * 512:mt * TQ + (nch + 1) * 512],
                    S["xqb_t"][:, mt * TQ + nch * 512:mt * TQ + (nch + 1) * 512],
                    2.0, PB[bank][:, 0:512], ALU.mult, ALU.add)),
                waits=[mm, xq_all])
            sq_ev[(nch, mt)] = em.emit("scalar", _bld(
                lambda e, mt=mt, nch=nch: e.activation(
                    S["sq_t"][:], S["r1_t"][:, mt * TQ + nch * 512:mt * TQ + (nch + 1) * 512],
                    AF.Square, bias=0.0, scale=1.0)),
                waits=[r1_ev[(nch, mt)], st_mm.get((nch, mt - 1)), st_mm.get((nch - 1, 7))])
            em.emit("tensor", _bld(
                lambda e, mt=mt, nch=nch: e.matmul(
                    PB[SUM_B][0:1, 0:512], S["ones_rt"][:, 0:1],
                    S["r1_t"][:, mt * TQ + nch * 512:mt * TQ + (nch + 1) * 512],
                    start=(mt == 0), stop=(mt == 7))),
                waits=[r1_ev[(nch, mt)], srow_ev.get(nch - 1)])
            em.emit("tensor", _bld(
                lambda e, mt=mt: e.matmul(
                    PB[SQ_B][0:1, 0:512], S["onesb_t"][:, 0:1], S["sq_t"][:, 0:512].bitcast(BF16) if False else S["sq_t"][:],
                    start=(mt == 0), stop=(mt == 7))),
                waits=[sq_ev[(nch, mt)]])
            st_mm[(nch, mt)] = em.ev("tensor")
        smm = em.ev("tensor")
        ev1 = em.emit("scalar", _bld(lambda e: e.activation(
            S["srow_t"][0:1, :], PB[SUM_B][0:1, 0:512], AF.Copy, bias=0.0, scale=1.0)),
            waits=[smm])
        ev2 = em.emit("scalar", _bld(lambda e: e.activation(
            S["srow2_t"][0:1, :], PB[SQ_B][0:1, 0:512], AF.Copy, bias=0.0, scale=1.0)))
        sa = em.emit("gpsimd", _bld(lambda e, nch=nch: e.dma_start(
            out=D["srow_d"][0:1, nch * 512:(nch + 1) * 512], in_=S["srow_t"][0:1, :])),
            waits=[ev1], dma=True)
        sb = em.emit("gpsimd", _bld(lambda e, nch=nch: e.dma_start(
            out=D["srow_d"][1:2, nch * 512:(nch + 1) * 512], in_=S["srow2_t"][0:1, :])),
            waits=[ev2], dma=True)
        srow_ev[nch] = [sa, sb]

    if _P3_STOP == "op":
        if "x1" in _DBG:
            em.barrier()
            for mt in range(8):
                em.emit("gpsimd", _bld(lambda e, mt=mt: e.dma_start(
                    out=D["dbg_x1"][mt * 128:(mt + 1) * 128, :],
                    in_=S["r1_t"][:, mt * TQ:(mt + 1) * TQ].bitcast(F32))), dma=True)
        return
    ln_rows(ctx, "srow_d", "mrow_d", [srow_ev[n] for n in range(4)])
    mb_ev = em.emit("gpsimd", _bld(lambda e: e.dma_start(
        out=S["mbc_t"][:], in_=D["mrow_d"][0:1, :].broadcast_to([128, TQ]))),
        waits=[ctx["lnrow_ev"]], dma=True)
    rb_ev = em.emit("gpsimd", _bld(lambda e: e.dma_start(
        out=S["rbc2_t"][:], in_=D["mrow_d"][1:2, :].broadcast_to([128, TQ]))),
        waits=[ctx["lnrow_ev"]], dma=True)
    x1_ev = {}
    for mt in range(8):
        em.emit("vector", _bld(lambda e, mt=mt: e.tensor_tensor(
            S["r1_t"][:, mt * TQ:(mt + 1) * TQ], S["r1_t"][:, mt * TQ:(mt + 1) * TQ],
            S["mbc_t"][:], ALU.subtract)),
            waits=[mb_ev, rb_ev, r1_ev[(3, mt)], sq_ev[(3, mt)], st_mm[(3, 7)]])
        em.emit("vector", _bld(lambda e, mt=mt: e.tensor_tensor(
            S["r1_t"][:, mt * TQ:(mt + 1) * TQ], S["r1_t"][:, mt * TQ:(mt + 1) * TQ],
            S["rbc2_t"][:], ALU.mult)))
        x1_ev[mt] = em.emit("vector", _bld(lambda e, mt=mt: e.tensor_copy(
            S["xqb_t"][:, mt * TQ:(mt + 1) * TQ], S["r1_t"][:, mt * TQ:(mt + 1) * TQ])))
    x1_all = em.ev("vector")
    if "x1" in _DBG:
        for mt in range(8):
            em.emit("gpsimd", _bld(lambda e, mt=mt: e.dma_start(
                out=D["dbg_x1"][mt * 128:(mt + 1) * 128, :],
                in_=S["r1_t"][:, mt * TQ:(mt + 1) * TQ].bitcast(F32))),
                waits=[x1_all], dma=True)
        em.barrier()

    if _P3_STOP == "x1":
        return
    GL_B = 4
    lg_ev = {}
    cmb_tr = {}
    for tt in range(NTT):
        for kc in range(8):
            em.emit("tensor", _bld(
                lambda e, tt=tt, kc=kc: e.matmul(
                    PB[GL_B][0:128, 0:8],
                    S["r1_t"][:, kc * TQ + tt * 128:kc * TQ + (tt + 1) * 128],
                    S["gwt_t"][:, kc * E:(kc + 1) * E],
                    start=(kc == 0), stop=(kc == 7))),
                waits=([x1_all, ctx["init_ev"], lg_ev.get(tt - 1)] if kc == 0 else []),
                inc=(kc == 7))
        mmev = em.ev("tensor")
        lg_ev[tt] = em.emit("scalar", _bld(
            lambda e, tt=tt: e.activation(
                S["sc_t"][:, tt * E:(tt + 1) * E], PB[GL_B][0:128, 0:8],
                AF.Sigmoid, bias=0.0, scale=1.0)), waits=[mmev])
    scv = S["sc_t"][:].rearrange("p (g e) -> p g e", e=E)
    cbv = S["cb_t"][:].rearrange("p (g e) -> p g e", e=E)
    m1b = S["m1_t"][:].rearrange("p g -> p g").unsqueeze(2).broadcast_to([128, NTT, E])
    m2b = S["m2_t"][:].rearrange("p g -> p g").unsqueeze(2).broadcast_to([128, NTT, E])
    em.emit("vector", _bld(lambda e: e.tensor_reduce(
        S["m1_t"][:], scv, mybir.AxisListType.X, ALU.max)),
        waits=[lg_ev[t_] for t_ in range(NTT)])
    em.emit("vector", _bld(lambda e: e.tensor_tensor(
        S["lg_t"][:].rearrange("p (g e) -> p g e", e=E), scv, m1b, ALU.is_ge)))
    em.emit("vector", _bld(lambda e: e.scalar_tensor_tensor(
        cbv, S["lg_t"][:].rearrange("p (g e) -> p g e", e=E), -1e30, scv,
        ALU.mult, ALU.add)))
    em.emit("vector", _bld(lambda e: e.tensor_reduce(
        S["m2_t"][:], cbv, mybir.AxisListType.X, ALU.max)))
    em.emit("vector", _bld(lambda e: e.tensor_tensor(
        cbv, cbv, m2b, ALU.is_ge)))
    em.emit("vector", _bld(lambda e: e.tensor_tensor(
        S["lg_t"][:].rearrange("p (g e) -> p g e", e=E),
        S["lg_t"][:].rearrange("p (g e) -> p g e", e=E), m1b, ALU.mult)))
    em.emit("vector", _bld(lambda e: e.tensor_tensor(
        cbv, cbv, m2b, ALU.mult)))
    em.emit("vector", _bld(lambda e: e.tensor_tensor(
        cbv, cbv, S["lg_t"][:].rearrange("p (g e) -> p g e", e=E), ALU.add)))
    em.emit("vector", _bld(lambda e: e.tensor_tensor(
        S["tiny_t"][:, 0:16], S["m1_t"][:], S["m2_t"][:], ALU.add)))
    em.emit("vector", _bld(lambda e: e.tensor_scalar(
        S["tiny_t"][:, 0:16], S["tiny_t"][:, 0:16], 1e-20, None, ALU.add)))
    em.emit("vector", _bld(lambda e: e.reciprocal(
        S["m1_t"][:], S["tiny_t"][:, 0:16])))
    em.emit("vector", _bld(lambda e: e.tensor_scalar(
        S["m1_t"][:], S["m1_t"][:], SCALE_MOE, None, ALU.mult)))
    cmb_all = em.emit("vector", _bld(lambda e: e.tensor_tensor(
        cbv, cbv, m1b, ALU.mult)))
    for tt in range(NTT):
        evt = em.emit("tensor", _bld(lambda e, tt=tt: e.transpose(
            PB[5][0:8, 0:128], S["cb_t"][:, tt * E:(tt + 1) * E], S["idf_t"][:])),
            waits=[cmb_all, cmb_tr.get(tt - 1)])
        cmb_tr[tt] = em.emit("scalar", _bld(lambda e, tt=tt: e.activation(
            S["cbp_t"][0:8, tt * 128:(tt + 1) * 128], PB[5][0:8, 0:128],
            AF.Copy, bias=0.0, scale=1.0)), waits=[evt])
    cb_dma = em.emit("gpsimd", _bld(lambda e: e.dma_start(
        out=D["combt_d"], in_=S["cbp_t"][0:8, :])), waits=[cmb_tr[NTT - 1]], dma=True)
    if "comb" in _DBG:
        em.emit("gpsimd", _bld(lambda e: e.dma_start(
            out=D["dbg_comb"], in_=D["combt_d"])), waits=[cb_dma], dma=True)
    if "sc" in _DBG:
        em.emit("gpsimd", _bld(lambda e: e.dma_start(
            out=D["dbg_sc"], in_=S["sc_t"][:])), waits=[em.ev("vector")], dma=True)

    if _P3_STOP == "gate":
        return
    GB = (4, 6)
    UB = (5, 7)
    DN_B = (0, 1)
    SUM2_B, SQ2_B = 6, 7
    NCH2 = 4
    W2 = 512

    wg_ld, wu_ld, wd_ld, cbb_ev = {}, {}, {}, {}
    sg_ev, hm_ev, dn_ev = {}, {}, {}
    r2_ev, sq2_ev, s2_mm, srow2_ev = {}, {}, {}, {}
    gu_done = {}
    ffn_guard = [em.ev("scalar"), em.ev("vector")]
    nch_list = [0] if _P3_STOP == "ffn0" else list(range(NCH2))
    for nch in nch_list:
        for up in range(5):
            for ui in range(2):
                u = 2 * up + ui
                ub = u % 2
                wldw = [gu_done.get((nch, u - 2)), gu_done.get((nch - 1, u + NU - 2))]
                if nch == 0 and u == 0:
                    wldw += [x1_all, mb_ev, rb_ev, cb_dma]
                wg_ld[(nch, u)] = em.emit("sync", _bld(
                    lambda e, u=u, ub=ub: e.dma_start(
                        out=S["wg_t"][:, ub * 8 * IEX:(ub + 1) * 8 * IEX],
                        in_=_kpm(D["wg_all"][u]))), waits=wldw, dma=True)
                wu_ld[(nch, u)] = em.emit("sync", _bld(
                    lambda e, u=u, ub=ub: e.dma_start(
                        out=S["wu_t"][:, ub * 8 * IEX:(ub + 1) * 8 * IEX],
                        in_=_kpm(D["wu_all"][u]))), waits=wldw, dma=True)
                wd_ld[(nch, u)] = em.emit("sync", _bld(
                    lambda e, u=u, ub=ub: e.dma_start(
                        out=S["wd_t"][:, ub * 4 * DM:(ub + 1) * 4 * DM],
                        in_=_kpm(D["wd_all"][u]))),
                    waits=[dn_ev.get((nch, up - 1)), dn_ev.get((nch - 1, up + 4))]
                    + ([cb_dma, x1_all] if nch == 0 and up == 0 else []),
                    dma=True)
                if u >= 2:
                    cbb_ev[(nch, u)] = em.emit("gpsimd", _bld(
                        lambda e, u=u, nch=nch, ub=ub: e.dma_start(
                            out=S["cbbc_t"][:, ub * W2:(ub + 1) * W2],
                            in_=D["combt_d"][u - 2:u - 1, nch * W2:(nch + 1) * W2]
                            .broadcast_to([128, W2]))),
                        waits=[cb_dma, hm_ev.get((nch, u - 2, 3))], dma=True)
                for it in range(4):
                    gb_ = GB[it % 2]
                    ub_ = UB[it % 2]
                    w0 = [wg_ld[(nch, u)], x1_all, sg_ev.get((nch, u, it - 2)),
                          sg_ev.get((nch, u - 1, 2 + it % 2)),
                          sg_ev.get((nch - 1, NU - 1, 2 + it % 2)),
                          srow2_ev.get(nch - 1) if u == 0 else None]
                    if nch == 0 and u == 0 and it == 0:
                        w0 += ffn_guard
                    for kc in range(8):
                        em.emit("tensor", _bld(
                            lambda e, it=it, kc=kc, ub=ub, nch=nch, gb_=gb_: e.matmul(
                                PB[gb_][:, 0:W2],
                                S["wg_t"][:, ub * 8 * IEX + kc * IEX + it * 128:ub * 8 * IEX + kc * IEX + (it + 1) * 128],
                                S["xqb_t"][:, kc * TQ + nch * W2:kc * TQ + (nch + 1) * W2],
                                start=(kc == 0), stop=(kc == 7))),
                            waits=w0 if kc == 0 else [], inc=(kc == 7))
                    gmm = em.ev("tensor")
                    sg_ev[(nch, u, it)] = em.emit("scalar", _bld(
                        lambda e, it=it, gb_=gb_: e.activation(
                            S["sg_t"][:, it * W2:(it + 1) * W2], PB[gb_][:, 0:W2],
                            AF.Silu, bias=0.0, scale=1.0)), waits=[gmm])
                    w0u = [wu_ld[(nch, u)], hm_ev.get((nch, u - 1, 2 + it % 2)),
                           hm_ev.get((nch, u, it - 2)),
                           srow2_ev.get(nch - 1) if u == 0 else None]
                    if nch == 0 and u == 0 and it == 0:
                        w0u = w0u + ffn_guard
                    if nch > 0 and u <= 1 and it == 0:
                        w0u.append(hm_ev.get((nch - 1, u + 8, 3)))
                    for kc in range(8):
                        em.emit("tensor", _bld(
                            lambda e, it=it, kc=kc, ub=ub, nch=nch, ub_=ub_: e.matmul(
                                PB[ub_][:, 0:W2],
                                S["wu_t"][:, ub * 8 * IEX + kc * IEX + it * 128:ub * 8 * IEX + kc * IEX + (it + 1) * 128],
                                S["xqb_t"][:, kc * TQ + nch * W2:kc * TQ + (nch + 1) * W2],
                                start=(kc == 0), stop=(kc == 7))),
                            waits=w0u if kc == 0 else [], inc=(kc == 7))
                    umm = em.ev("tensor")
                    ev = em.emit("vector", _bld(
                        lambda e, it=it, ub=ub, ub_=ub_: e.tensor_tensor(
                            S["hm_t"][:, (ub * 4 + it) * W2:(ub * 4 + it + 1) * W2],
                            S["sg_t"][:, it * W2:(it + 1) * W2], PB[ub_][:, 0:W2],
                            ALU.mult)),
                        waits=[umm, sg_ev[(nch, u, it)], dn_ev.get((nch, up - 1))])
                    if u >= 2:
                        ev = em.emit("vector", _bld(
                            lambda e, it=it, ub=ub: e.tensor_tensor(
                                S["hm_t"][:, (ub * 4 + it) * W2:(ub * 4 + it + 1) * W2],
                                S["hm_t"][:, (ub * 4 + it) * W2:(ub * 4 + it + 1) * W2],
                                S["cbbc_t"][:, ub * W2:(ub + 1) * W2], ALU.mult)),
                            waits=[cbb_ev[(nch, u)]])
                    hm_ev[(nch, u, it)] = ev
                gu_done[(nch, u)] = em.ev("tensor")
            hm0 = hm_ev[(nch, 2 * up, 3)]
            hm1 = hm_ev[(nch, 2 * up + 1, 3)]
            for mt in range(8):
                bank = DN_B[mt % 2]
                w0 = [wd_ld[(nch, 2 * up)], wd_ld[(nch, 2 * up + 1)], hm0, hm1,
                      r2_ev.get((nch, up, mt - 2)), r2_ev.get((nch, up - 1, mt + 6)),
                      r2_ev.get((nch - 1, 4, mt + 6))]
                if nch == 0 and up == 0 and mt <= 1:
                    w0 += ffn_guard
                first = True
                for ui in range(2):
                    u = 2 * up + ui
                    ub = u % 2
                    for kc in range(4):
                        em.emit("tensor", _bld(
                            lambda e, mt=mt, kc=kc, ub=ub, ui=ui, bank=bank: e.matmul(
                                PB[bank][:, 0:W2],
                                S["wd_t"][:, ub * 4 * DM + kc * DM + mt * 128:ub * 4 * DM + kc * DM + (mt + 1) * 128],
                                S["hm_t"][:, (ub * 4 + kc) * W2:(ub * 4 + kc + 1) * W2],
                                start=(ui == 0 and kc == 0), stop=(ui == 1 and kc == 3))),
                            waits=w0 if first else [], inc=(ui == 1 and kc == 3))
                        first = False
                dmm = em.ev("tensor")
                r2_ev[(nch, up, mt)] = em.emit("vector", _bld(
                    lambda e, mt=mt, nch=nch, bank=bank: e.tensor_tensor(
                        S["r1_t"][:, mt * TQ + nch * W2:mt * TQ + (nch + 1) * W2],
                        S["r1_t"][:, mt * TQ + nch * W2:mt * TQ + (nch + 1) * W2],
                        PB[bank][:, 0:W2], ALU.add)), waits=[dmm])
            dn_ev[(nch, up)] = em.ev("vector")
        for mt in range(8):
            r2_ev[(nch, mt)] = r2_ev[(nch, 4, mt)]
            sq2_ev[(nch, mt)] = em.emit("scalar", _bld(
                lambda e, mt=mt, nch=nch: e.activation(
                    S["sq_t"][:, 0:W2],
                    S["r1_t"][:, mt * TQ + nch * W2:mt * TQ + (nch + 1) * W2],
                    AF.Square, bias=0.0, scale=1.0)),
                waits=[r2_ev[(nch, mt)], s2_mm.get((nch, mt - 1)), s2_mm.get((nch - 1, 7))])
            em.emit("tensor", _bld(
                lambda e, mt=mt, nch=nch: e.matmul(
                    PB[SUM2_B][0:1, 0:W2], S["ones_rt"][:, 0:1],
                    S["r1_t"][:, mt * TQ + nch * W2:mt * TQ + (nch + 1) * W2],
                    start=(mt == 0), stop=(mt == 7))),
                waits=[r2_ev[(nch, mt)], srow2_ev.get(nch - 1)] + (ffn_guard if nch == 0 and mt == 0 else []))
            em.emit("tensor", _bld(
                lambda e, mt=mt: e.matmul(
                    PB[SQ2_B][0:1, 0:W2], S["onesb_t"][:, 0:1], S["sq_t"][:, 0:W2],
                    start=(mt == 0), stop=(mt == 7))),
                waits=[sq2_ev[(nch, mt)]] + (ffn_guard if nch == 0 and mt == 0 else []))
            s2_mm[(nch, mt)] = em.ev("tensor")
        smm = em.ev("tensor")
        ev1 = em.emit("scalar", _bld(lambda e: e.activation(
            S["srow_t"][0:1, 0:W2], PB[SUM2_B][0:1, 0:W2], AF.Copy, bias=0.0, scale=1.0)),
            waits=[smm])
        ev2 = em.emit("scalar", _bld(lambda e: e.activation(
            S["srow2_t"][0:1, 0:W2], PB[SQ2_B][0:1, 0:W2], AF.Copy, bias=0.0, scale=1.0)))
        sa = em.emit("gpsimd", _bld(lambda e, nch=nch: e.dma_start(
            out=D["srow_d"][0:1, nch * W2:(nch + 1) * W2], in_=S["srow_t"][0:1, 0:W2])),
            waits=[ev1], dma=True)
        sb = em.emit("gpsimd", _bld(lambda e, nch=nch: e.dma_start(
            out=D["srow_d"][1:2, nch * W2:(nch + 1) * W2], in_=S["srow2_t"][0:1, 0:W2])),
            waits=[ev2], dma=True)
        srow2_ev[nch] = [sa, sb]

    if _P3_STOP in ("ffn0", "ffnall"):
        return
    ln_rows(ctx, "srow_d", "mrow_d", [srow2_ev[n] for n in range(NCH2)])
    mb2 = em.emit("gpsimd", _bld(lambda e: e.dma_start(
        out=S["mbc_t"][:], in_=D["mrow_d"][0:1, :].broadcast_to([128, TQ]))),
        waits=[ctx["lnrow_ev"]], dma=True)
    rb2 = em.emit("gpsimd", _bld(lambda e: e.dma_start(
        out=S["rbc2_t"][:], in_=D["mrow_d"][1:2, :].broadcast_to([128, TQ]))),
        waits=[ctx["lnrow_ev"]], dma=True)
    if _P3_STOP == "ln2":
        return
    out_st = {}
    xo_ev = {}
    tr_ev = {}
    for tt in range(NTT):
        ob = 0
        for mt in range(8):
            w = [mb2, rb2, r2_ev[(NCH2 - 1, mt)], s2_mm[(NCH2 - 1, 7)], tr_ev.get((tt - 1, mt))]
            if mt == 0:
                w.append(out_st.get(tt - 2))
            e1 = em.emit("vector", _bld(lambda e, tt=tt, mt=mt: e.tensor_tensor(
                S["x2_t"][:, mt * 128:(mt + 1) * 128],
                S["r1_t"][:, mt * TQ + tt * 128:mt * TQ + (tt + 1) * 128],
                S["mbc_t"][:, tt * 128:(tt + 1) * 128], ALU.subtract)), waits=w)
            e2 = em.emit("vector", _bld(lambda e, tt=tt, mt=mt: e.tensor_tensor(
                S["x2_t"][:, mt * 128:(mt + 1) * 128],
                S["x2_t"][:, mt * 128:(mt + 1) * 128],
                S["rbc2_t"][:, tt * 128:(tt + 1) * 128], ALU.mult)))
            evt = em.emit("tensor", _bld(lambda e, mt=mt: e.transpose(
                PB[4 + (mt % 2)][:, (mt // 2) * 128:(mt // 2 + 1) * 128],
                S["x2_t"][:, mt * 128:(mt + 1) * 128], S["idf_t"][:])),
                waits=[e2, xo_ev.get((tt - 1, 6 + (mt % 2))), xo_ev.get((tt, mt - 2))])
            tr_ev[(tt, mt)] = evt
            xo_ev[(tt, mt)] = em.emit("scalar", _bld(lambda e, mt=mt, ob=ob: e.activation(
                S["xo_t"][:, ob * DM + mt * 128:ob * DM + (mt + 1) * 128],
                PB[4 + (mt % 2)][:, (mt // 2) * 128:(mt // 2 + 1) * 128],
                AF.Copy, bias=0.0, scale=1.0)), waits=[evt, out_st.get(tt - 1)])
        oev = em.ev("scalar")
        out_st[tt] = em.emit("gpsimd", _bld(lambda e, tt=tt, ob=ob: e.dma_start(
            out=D["out"][tt * 128:(tt + 1) * 128, :], in_=S["xo_t"][:, ob * DM:(ob + 1) * DM])),
            waits=[oev], dma=True)


def ln_rows(ctx, src, dst, waits):
    D, S, em = ctx["D"], ctx["S"], ctx["em"]
    l1 = em.emit("sync", _bld(lambda e: e.dma_start(
        out=S["m16_t"][:, 0:16], in_=D[src][0:1, :].rearrange("o (p w) -> (o p) w", p=128))),
        waits=waits, dma=True)
    l2 = em.emit("sync", _bld(lambda e: e.dma_start(
        out=S["m16_t"][:, 16:32], in_=D[src][1:2, :].rearrange("o (p w) -> (o p) w", p=128))),
        waits=waits, dma=True)
    e1 = em.emit("vector", _bld(lambda e: e.tensor_scalar(
        S["m16_t"][:, 0:16], S["m16_t"][:, 0:16], 1.0 / DM, None, ALU.mult)),
        waits=[l1])
    e2 = em.emit("vector", _bld(lambda e: e.tensor_tensor(
        S["m16_t"][:, 32:48], S["m16_t"][:, 0:16], S["m16_t"][:, 0:16], ALU.mult)),
        waits=[l2])
    e3 = em.emit("vector", _bld(lambda e: e.scalar_tensor_tensor(
        S["m16_t"][:, 48:64], S["m16_t"][:, 16:32], 1.0 / DM, S["m16_t"][:, 32:48],
        ALU.mult, ALU.subtract)))
    e4 = em.emit("scalar", _bld(lambda e: e.activation(
        S["m16_t"][:, 48:64], S["m16_t"][:, 48:64], AF.Sqrt, bias=EPS, scale=1.0)),
        waits=[e3])
    e5 = em.emit("vector", _bld(lambda e: e.reciprocal(
        S["m16_t"][:, 48:64], S["m16_t"][:, 48:64])), waits=[e4])
    s1 = em.emit("gpsimd", _bld(lambda e: e.dma_start(
        out=D[dst][0:1, :].rearrange("o (p w) -> (o p) w", p=128), in_=S["m16_t"][:, 0:16])),
        waits=[e1], dma=True)
    s2 = em.emit("gpsimd", _bld(lambda e: e.dma_start(
        out=D[dst][1:2, :].rearrange("o (p w) -> (o p) w", p=128), in_=S["m16_t"][:, 48:64])),
        waits=[e5], dma=True)
    ctx["lnrow_ev"] = s2
    em.wait_only("gpsimd", [s1])


def _prep_core_inputs(c, I):
    b, q = c // 4, c % 4
    sl = slice(q * DQ, (q + 1) * DQ)
    f = np.float32
    bf = ml_dtypes.bfloat16
    cont = np.ascontiguousarray

    d = {}
    xb = np.asarray(I["L0_attn_hidden_states"][b], dtype=f)
    d["x"] = cont(xb)
    d["xq"] = cont(xb[q * TQ:(q + 1) * TQ])
    ipw = np.asarray(I["L0_attn_in_proj_w"], dtype=f)
    d["w_inx"] = cont(ipw[sl].T)
    d["w_inz"] = cont(ipw[2048 + q * DQ:2048 + (q + 1) * DQ].T)
    d["convw"] = cont(np.asarray(I["L0_attn_conv1d_w"], dtype=f)[sl].T)
    d["convb"] = cont(np.asarray(I["L0_attn_conv1d_b"], dtype=f)[sl].reshape(NDT, 128).T)
    d["w_xp"] = cont(np.asarray(I["L0_attn_x_proj_w"], dtype=f)[:, sl].T.astype(bf))
    d["w_dt"] = cont(np.asarray(I["L0_attn_dt_proj_w"], dtype=f)[sl].T)
    d["dtb"] = cont(np.asarray(I["L0_attn_dt_proj_b"], dtype=f)[sl].reshape(NDT, 128).T)
    d["acol"] = cont(np.asarray(I["L0_attn_A_log"], dtype=f)[sl]
                     .reshape(NDT, 128, DS).transpose(1, 0, 2).reshape(128, NDT * DS))
    d["dcol"] = cont(np.asarray(I["L0_attn_D"], dtype=f)[sl].reshape(NDT, 128).T)
    d["nrmw"] = cont(np.asarray(I["L0_attn_norm_weight"], dtype=f).reshape(-1, 128).T)
    d["w_out"] = cont(np.asarray(I["L0_attn_out_proj_w"], dtype=f).T.astype(bf))
    d["w_gate"] = cont(np.asarray(I["L0_ffn_gate_weight"], dtype=f).T)
    wg = np.asarray(I["L0_ffn_w_gate"], dtype=f)
    wu = np.asarray(I["L0_ffn_w_up"], dtype=f)
    wd = np.asarray(I["L0_ffn_w_down"], dtype=f)
    wsg = np.asarray(I["L0_ffn_ws_gate"], dtype=f)
    wsu = np.asarray(I["L0_ffn_ws_up"], dtype=f)
    wsd = np.asarray(I["L0_ffn_ws_down"], dtype=f)
    g_all = np.concatenate([wsg.reshape(2, IEX, DM), wg], axis=0)
    u_all = np.concatenate([wsu.reshape(2, IEX, DM), wu], axis=0)
    dn_all = np.concatenate([wsd.T.reshape(2, IEX, DM), wd.transpose(0, 2, 1)], axis=0)
    d["wg_all"] = cont(g_all.transpose(0, 2, 1).astype(bf))
    d["wu_all"] = cont(u_all.transpose(0, 2, 1).astype(bf))
    d["wd_all"] = cont(dn_all.astype(bf))
    d["idbf"] = np.eye(128, dtype=bf)
    d["idf"] = np.eye(128, dtype=f)
    d["ones_r"] = np.ones((128, 8), dtype=f)
    cvals = np.zeros(8, dtype=f)
    cvals[:4] = [EPS, 1.0, -1.0, 1.0 / DM]
    d["consts"] = np.broadcast_to(cvals, (128, 8)).copy()
    yidx = np.empty((128, 64), dtype=np.int32)
    for kc in range(16):
        for nch in range(4):
            yidx[:, kc * 4 + nch] = (kc * 128 + np.arange(128)) * (T // 512) + q * 4 + nch
    d["yidx"] = yidx
    return d


def kernel(**inputs):
    nc, D = build_program()
    in_maps = [_prep_core_inputs(c, inputs) for c in range(NCORE)]
    res = run_bass_kernel_spmd(nc, in_maps, core_ids=list(range(NCORE)))
    out = np.empty((B, T, DM), dtype=np.float32)
    for c in range(NCORE):
        b, q = c // 4, c % 4
        out[b, q * TQ:(q + 1) * TQ] = res.results[c]["out"]
    return out
